# version 10
# baseline (speedup 1.0000x reference)
import numpy as np
import ml_dtypes
from contextlib import ExitStack

import concourse.bass as bass
import concourse.mybir as mybir
from concourse.bass_utils import run_bass_kernel_spmd
from bass_rust import IndirectOffsetOnAxis

F32 = mybir.dt.float32
BF16 = mybir.dt.bfloat16
U32 = mybir.dt.uint32
AF = mybir.ActivationFunctionType
ALU = mybir.AluOpType
AX = mybir.AxisListType

B = 4
N = 2048
DM = 1024
H = 16
PAIRS = 8
HD = 64
MLM = 64
SEG = 32
ROWS = B * N
NCH = ROWS // 128
E = 8
FF = 4096
CAP = 4096
NSC = CAP // 256
NFC = FF // 128
EPS = 1e-5

ENGS = ("sync", "tensor", "vector", "scalar", "gpsimd")


class Sem:
    def __init__(self, stack, nc, name):
        self.h = stack.enter_context(nc.semaphore(name))
        self.v = 0

    def bump(self, k=1):
        self.v += k
        return self.v


class MK:
    def __init__(self, nc, stack, debug_stage=None):
        self.nc = nc
        self.stack = stack
        self.debug_stage = debug_stage
        self.q = None

    def new_block(self):
        self.q = {k: [] for k in ENGS}

    def emit(self, eng, fn):
        self.q[eng].append(fn)

    def run_block(self):
        q = self.q
        with self.nc.Block() as blk:
            for eng in ENGS:
                fns = q[eng]
                if not fns:
                    continue
                def body(e, fns=fns):
                    for fn in fns:
                        fn(e)
                getattr(blk, eng)(body)
        self.q = None

    _uid = 0

    def uname(self, name):
        MK._uid += 1
        return f"{name}_u{MK._uid}"

    def sb(self, name, shape, dt):
        return self.stack.enter_context(self.nc.sbuf_tensor(name, shape, dt))

    def ps(self, name, shape, dt):
        return self.stack.enter_context(self.nc.psum_tensor(name, shape, dt))

    def sem(self, name):
        return Sem(self.stack, self.nc, name)


def build_program(debug_stage=None):
    nc = bass.Bass("TRN2", target_bir_lowering=False, detect_race_conditions=False)

    x_d = nc.dram_tensor("x", [ROWS, DM], F32, kind="ExternalInput").ap()
    wqkv_d = nc.dram_tensor("wqkv", [DM, 3 * DM], F32, kind="ExternalInput").ap()
    wproj_d = nc.dram_tensor("wproj", [DM, DM], F32, kind="ExternalInput").ap()
    cent_d = nc.dram_tensor("cent", [E, DM], F32, kind="ExternalInput").ap()
    lng_d = nc.dram_tensor("lng", [E, DM], F32, kind="ExternalInput").ap()
    lnb_d = nc.dram_tensor("lnb", [E, DM], F32, kind="ExternalInput").ap()
    w1_d = nc.dram_tensor("w1", [E, FF, DM], F32, kind="ExternalInput").ap()
    b1_d = nc.dram_tensor("b1", [E, FF], F32, kind="ExternalInput").ap()
    w2_d = nc.dram_tensor("w2", [E, DM, FF], F32, kind="ExternalInput").ap()
    b2_d = nc.dram_tensor("b2", [E, DM], F32, kind="ExternalInput").ap()
    out_d = nc.dram_tensor("out", [ROWS, DM], F32, kind="ExternalOutput").ap()

    dbg_d = None
    if debug_stage in ("p2abd",):
        dbg_d = nc.dram_tensor("dbg", [128, 4, 2048], BF16, kind="ExternalOutput").ap()
    x1_kind = "ExternalOutput" if debug_stage == "attn" else "Internal"
    x1s = nc.dram_tensor("x1s", [ROWS, DM], F32, kind=x1_kind).ap()
    zs = nc.dram_tensor("zs", [ROWS, DM], BF16, kind="Internal").ap()
    znbkt = nc.dram_tensor("znbkt", [E * CAP, DM], BF16, kind="Internal").ap()
    tbkt = nc.dram_tensor("tbkt", [E * CAP + 128, DM], BF16, kind="Internal").ap()
    if debug_stage == "dump":
        d_zn = nc.dram_tensor("d_zn", [E * CAP, DM], BF16, kind="ExternalOutput").ap()
        d_t = nc.dram_tensor("d_t", [E * CAP + 128, DM], BF16, kind="ExternalOutput").ap()
        d_z = nc.dram_tensor("d_z", [ROWS, DM], BF16, kind="ExternalOutput").ap()
        d_o = nc.dram_tensor("d_o", [128, 2, NCH], U32, kind="ExternalOutput").ap()
        d_rk = nc.dram_tensor("d_rk", [128, 4, NCH], F32, kind="ExternalOutput").ap()
        d_al = nc.dram_tensor("d_al", [128, NCH], F32, kind="ExternalOutput").ap()
    w1b = nc.dram_tensor("w1b", [E, FF, DM], BF16, kind="Internal").ap()
    w2b = nc.dram_tensor("w2b", [E, DM, FF], BF16, kind="Internal").ap()

    with ExitStack() as stack:
        k = MK(nc, stack, debug_stage)
        sb, ps, sem = k.sb, k.ps, k.sem

        id128 = sb("id128", [128, 128], BF16)
        id64s = sb("id64s", [128, 64], BF16)
        id128f = sb("id128f", [128, 128], F32)
        ltu = sb("ltu", [128, 128], F32)
        ones128f = sb("ones128f", [128, 1], F32)
        iota8 = sb("iota8", [128, 8], F32)
        ones128 = sb("ones128", [128, 1], BF16)
        ones1r = sb("ones1r", [1, 128], BF16)
        ones1f = sb("ones1f", [1, 128], F32)
        sel0 = sb("sel0", [1, 128], F32)
        sel1 = sb("sel1", [1, 128], F32)
        cIs = sb("cIs", [128, 3, 64], F32)
        CT = sb("CT", [128, 8, E], F32)
        GT = sb("GT", [128, 8, E], F32)
        BT = sb("BT", [128, 8, E], F32)
        B1T = sb("B1T", [128, NFC, E], F32)
        alpha_all = sb("alpha_all", [128, NCH], F32)
        mmax_all = sb("mmax_all", [128, NCH], F32)
        dbg_rk = sb("dbg_rk", [128, 4, NCH], F32)
        o_all = sb("o_all", [128, NCH], U32)
        og_all = sb("og_all", [128, NCH], U32)
        Rrow = sb("Rrow", [1, E], F32)
        nse_f = sb("nse_f", [1, E], F32)
        nse_i = sb("nse_i", [1, E], mybir.dt.int32)
        epsc = sb("epsc", [128, 1], F32)
        zot = sb("zot", [128, DM], BF16)

        pA = ps("pA", [128, 512], F32)
        pB = ps("pB", [128, 512], F32)
        pC = ps("pC", [128, 512], F32)
        pD = ps("pD", [128, 512], F32)
        pE_ = ps("pE", [128, 512], F32)
        pF_ = ps("pF", [128, 512], F32)
        pTrb = ps("pTrb", [128, 8, 128], BF16)
        pTr2 = ps("pTr2", [128, 8, 128], BF16)

        s_x = sem("s_x")
        s_st = sem("s_st")
        s_gq = sem("s_gq")
        s_wp = sem("s_wp")
        s_ln = sem("s_ln")
        s_sq = sem("s_sq")
        s_tp = sem("s_tp")
        s_ev = sem("s_ev")
        s_ev2 = sem("s_ev2")
        s_mm = sem("s_mm")
        s_nt = sem("s_nt")
        s_nv = sem("s_nv")
        s_gp = sem("s_gp")
        s_rt = sem("s_rt")
        s_w = sem("s_w")
        s_zn = sem("s_zn")
        s_trg = sem("s_trg")
        s_znT = sem("s_znT")
        s_w1 = sem("s_w1")
        s_he = sem("s_he")
        s_w2 = sem("s_w2")
        s_te = sem("s_te")
        s_sc = sem("s_sc")

        wprep_units = [("w1", e) for e in range(E)] + [("w2", e) for e in range(E)]

        def emit_wprep(units):
            def body(g, units=units):
                for kind, e in units:
                    if kind == "w1":
                        g.dma_start(w1b[e, :, :], w1_d[e, :, :]).then_inc(s_wp.h, 16)
                    else:
                        g.dma_start(w2b[e, :, :], w2_d[e, :, :]).then_inc(s_wp.h, 16)
            s_wp.bump(16 * len(units))
            return body

        p0stk = ExitStack()
        rowbuf = p0stk.enter_context(nc.sbuf_tensor("rowbuf", [1, 4 * DM], F32))
        k.new_block()

        def p0_g(g):
            def ident_into(t, val=1.0, halfsize=None):
                g.memset(t, 0.0)
                if halfsize is None:
                    sq = t.shape[-1]
                    g.affine_select(out=t, in_=t, compare_op=ALU.not_equal, fill=val,
                                    base=0, pattern=[[-1, sq]], channel_multiplier=1)
            ident_into(id128[:, :])
            ident_into(id128f[:, :])
            g.memset(id64s[:, :], 0.0)
            for hh in range(2):
                g.affine_select(out=id64s[hh * 64:(hh + 1) * 64, :],
                                in_=id64s[hh * 64:(hh + 1) * 64, :],
                                compare_op=ALU.not_equal, fill=1.0, base=0,
                                pattern=[[-1, 64]], channel_multiplier=1)
            g.memset(ltu[:, :], 1.0)
            g.affine_select(out=ltu[:, :], in_=ltu[:, :], compare_op=ALU.is_ge,
                            fill=0.0, base=-1, pattern=[[1, 128]], channel_multiplier=-1)
            g.iota(iota8[:, :], [[1, 8]], base=0, channel_multiplier=0,
                   allow_small_or_imprecise_dtypes=True)
            g.memset(ones128[:, :], 1.0)
            g.memset(ones128f[:, :], 1.0)
            g.memset(ones1r[:, :], 1.0)
            g.memset(ones1f[:, :], 1.0)
            g.memset(sel0[:, :], 1.0)
            g.affine_select(out=sel0[:, :], in_=sel0[:, :], compare_op=ALU.is_ge,
                            fill=0.0, base=63, pattern=[[-1, 128]], channel_multiplier=0)
            g.memset(sel1[:, :], 1.0)
            g.affine_select(out=sel1[:, :], in_=sel1[:, :], compare_op=ALU.is_ge,
                            fill=0.0, base=-64, pattern=[[1, 128]], channel_multiplier=0)
            g.memset(cIs[:, :, :], 0.0)
            for j, c in enumerate([7.0, 15.0, 13.0]):
                for hh in range(2):
                    g.affine_select(out=cIs[hh * 64:(hh + 1) * 64, j, :],
                                    in_=cIs[hh * 64:(hh + 1) * 64, j, :],
                                    compare_op=ALU.not_equal, fill=c, base=0,
                                    pattern=[[-1, 64]], channel_multiplier=1)
            g.memset(Rrow[:, :], 0.0)
            g.memset(epsc[:, :], EPS)
            g.memset(zot[:, :], 0.0)
            g.nop().then_inc(s_gp.h, 1)
        P0G = s_gp.bump()
        k.emit("gpsimd", p0_g)

        p0_plan = []
        ev_ms = []
        for e in range(E):
            p0_plan.append(("4rows", e))
        for seg in range(1, 4):
            for e in range(E):
                p0_plan.append(("b1seg", e, seg))

        def p0_sync(sync):
            for it, item in enumerate(p0_plan):
                if it >= 1:
                    sync.wait_ge(s_ev.h, it)
                if item[0] == "4rows":
                    e = item[1]
                    sync.dma_start(rowbuf[0:1, 0:DM], cent_d[e:e + 1, :]).then_inc(s_x.h, 16)
                    sync.dma_start(rowbuf[0:1, DM:2 * DM], lng_d[e:e + 1, :]).then_inc(s_x.h, 16)
                    sync.dma_start(rowbuf[0:1, 2 * DM:3 * DM], lnb_d[e:e + 1, :]).then_inc(s_x.h, 16)
                    sync.dma_start(rowbuf[0:1, 3 * DM:4 * DM], b1_d[e:e + 1, 0:1024]).then_inc(s_x.h, 16)
                else:
                    e, seg = item[1], item[2]
                    sync.dma_start(rowbuf[0:1, 0:DM],
                                   b1_d[e:e + 1, seg * 1024:(seg + 1) * 1024]).then_inc(s_x.h, 16)
        nload = 0
        p0_loadcnt = []
        for item in p0_plan:
            nload += 4 if item[0] == "4rows" else 1
            p0_loadcnt.append(nload)
        s_x.bump(16 * nload)

        def p0_t(t):
            for it, item in enumerate(p0_plan):
                t.wait_ge(s_x.h, 16 * p0_loadcnt[it])
                if it >= 2:
                    t.wait_ge(s_ev.h, it - 1)
                pX = pA if it % 2 == 0 else pB
                nr = 4 if item[0] == "4rows" else 1
                for c in range(8):
                    for r in range(nr):
                        mm = t.matmul(pX[:, 4 * c + r:4 * c + r + 1],
                                      rowbuf[0:1, r * DM + c * 128:r * DM + (c + 1) * 128],
                                      ones1f[0:1, 0:1], start=True, stop=True)
                mm.then_inc(s_mm.h, 1)
        p0_mm0 = s_mm.v
        s_mm.bump(len(p0_plan))

        def p0_v(v):
            for it, item in enumerate(p0_plan):
                v.wait_ge(s_mm.h, p0_mm0 + it + 1)
                pX = pA if it % 2 == 0 else pB
                if item[0] == "4rows":
                    e = item[1]
                    for c in range(8):
                        v.tensor_copy(CT[:, c, e:e + 1], pX[:, 4 * c:4 * c + 1])
                        v.tensor_copy(GT[:, c, e:e + 1], pX[:, 4 * c + 1:4 * c + 2])
                        v.tensor_copy(BT[:, c, e:e + 1], pX[:, 4 * c + 2:4 * c + 3])
                        op = v.tensor_copy(B1T[:, c, e:e + 1], pX[:, 4 * c + 3:4 * c + 4])
                    op.then_inc(s_ev.h, 1)
                else:
                    e, seg = item[1], item[2]
                    for c in range(8):
                        op = v.tensor_copy(B1T[:, seg * 8 + c, e:e + 1],
                                           pX[:, 4 * c:4 * c + 1])
                    op.then_inc(s_ev.h, 1)
        s_ev.bump(len(p0_plan))

        k.emit("sync", p0_sync)
        k.emit("tensor", p0_t)
        k.emit("vector", p0_v)
        k.run_block()
        p0stk.close()

        if debug_stage == "p0":
            return nc

        wprep_pos = 0
        CTbf = None

        nbatch = 1 if debug_stage in ("p2ab", "p2abd", "p2c", "attn1") else B
        for b in range(nbatch):
            with ExitStack() as bstk:
                bsb = lambda n, s, d: bstk.enter_context(nc.sbuf_tensor(k.uname(n), s, d))
                QKT = bsb("QKT", [128, 16, N], BF16)
                Vrm = bsb("Vrm", [128, 16, DM], BF16)
                xnlT = bsb("xnlT", [128, 8, MLM], BF16)
                QlKlT = bsb("QlKlT", [128, 16, MLM], BF16)

                with ExitStack() as astk:
                    asb = lambda n, s, d: astk.enter_context(nc.sbuf_tensor(k.uname(n), s, d))
                    xnT = asb("xnT", [128, 8, N], BF16)
                    wqkvb = asb("wqkvb", [128, 8, 3 * DM], BF16)
                    xc = asb("xc", [128, 2, DM], F32)
                    xnrm = asb("xnrm", [128, 2, DM], BF16)
                    stat = asb("stat", [128, 16], F32)
                    mv2 = asb("mv2", [128, 2], F32)
                    rstd = asb("rstd", [128, 1], F32)
                    sjunk = asb("sjunk", [128, 4], F32)
                    lred = asb("lred", [128, MLM], F32)

                    k.new_block()
                    xrow0 = b * N

                    gq0 = s_gq.v
                    def a_gq(g, gq0=gq0):
                        for d in range(8):
                            g.dma_start(wqkvb[:, d, :],
                                        wqkv_d[d * 128:(d + 1) * 128, :]).then_inc(s_gq.h, 16)
                    s_gq.bump(16 * 8)
                    k.emit("gpsimd", a_gq)

                    ln_stats = {}
                    ln_apply = {}
                    tp_ms = {}
                    evV = {}
                    evS = {}
                    for i in range(16):
                        x_at = s_x.v + 16

                        def c_sync(sync, i=i, xrow0=xrow0):
                            if i >= 2:
                                sync.wait_ge(s_ln.h, ln_apply[i - 2])
                            sync.dma_start(
                                xc[:, i % 2, :],
                                x_d[xrow0 + 128 * i:xrow0 + 128 * (i + 1), :]).then_inc(s_x.h, 16)
                        s_x.bump(16)
                        k.emit("sync", c_sync)

                        e1_, e2_ = s_ln.bump(), s_ln.bump()
                        def c_stats(v, i=i, x_at=x_at, e1_=e1_, e2_=e2_):
                            v.wait_ge(s_x.h, x_at)
                            v.bn_stats(stat[:, 0:6], xc[:, i % 2, 0:512]).then_inc(s_ln.h, 1)
                            v.wait_ge(s_ln.h, e1_)
                            v.bn_stats(stat[:, 6:12], xc[:, i % 2, 512:1024]).then_inc(s_ln.h, 1)
                            v.wait_ge(s_ln.h, e2_)
                            v.bn_aggr(mv2[:, :], stat[:, 0:12]).then_inc(s_ln.h, 1)
                        ln_stats[i] = s_ln.bump()
                        k.emit("vector", c_stats)

                        def c_sqrt(sc, i=i):
                            sc.wait_ge(s_ln.h, ln_stats[i])
                            sc.activation(sjunk[:, 0:1], mv2[:, 1:2], AF.Sqrt,
                                          bias=epsc[:, 0:1], scale=1.0).then_inc(s_sq.h, 1)
                        sq_i = s_sq.bump()
                        k.emit("scalar", c_sqrt)

                        er_ = s_ln.bump()
                        def c_apply(v, i=i, sq_i=sq_i, er_=er_):
                            v.wait_ge(s_sq.h, sq_i)
                            if i >= 2:
                                v.wait_ge(s_tp.h, tp_ms[i - 2])
                            v.reciprocal(rstd[:, :], sjunk[:, 0:1]).then_inc(s_ln.h, 1)
                            v.wait_ge(s_ln.h, er_)
                            v.tensor_scalar(xnrm[:, i % 2, :], xc[:, i % 2, :],
                                            mv2[:, 0:1], rstd[:, :],
                                            op0=ALU.subtract, op1=ALU.mult
                                            ).then_inc(s_ln.h, 1)
                        ln_apply[i] = s_ln.bump()
                        k.emit("vector", c_apply)

                        def c_tr(t, i=i):
                            t.wait_ge(s_ln.h, ln_apply[i])
                            if i >= 2:
                                t.wait_ge(s_ev.h, evV[i - 2])
                            pT = pTrb if i % 2 == 0 else pTr2
                            for d in range(8):
                                mm = t.transpose(pT[:, d, :],
                                                 xnrm[:, i % 2, d * 128:(d + 1) * 128],
                                                 id128[:, :])
                            mm.then_inc(s_tp.h, 1)
                        tp_ms[i] = s_tp.bump()
                        k.emit("tensor", c_tr)

                        def c_evV(v, i=i):
                            v.wait_ge(s_tp.h, tp_ms[i])
                            pT = pTrb if i % 2 == 0 else pTr2
                            for d in range(8):
                                mm = v.tensor_copy(xnT[:, d, i * 128:(i + 1) * 128],
                                                   pT[:, d, :])
                            mm.then_inc(s_ev.h, 1)
                        evV[i] = s_ev.bump()
                        k.emit("vector", c_evV)
                        evS[i] = s_ev2.v

                    if b == 0:
                        def zfill(sync):
                            for cz in range(E * CAP // 128):
                                sync.dma_start(
                                    znbkt[128 * cz:128 * (cz + 1), :],
                                    zot[:, :]).then_inc(s_st.h, 16)

                        globals()
                        k.emit("sync", zfill)
                        ZFILL_AT = s_st.bump(16 * (E * CAP // 128))

                    lm_pieces = []
                    for d in range(8):
                        el_ = None
                        def c_lm(v, d=d, el_=None):
                            if d == 0:
                                v.wait_ge(s_ev.h, evV[15])
                            v.tensor_reduce(lred[:, :],
                                            xnT[:, d, :].rearrange("p (m s) -> p m s", m=MLM),
                                            axis=AX.X, op=ALU.add).then_inc(s_ln.h, 1)
                            v.wait_ge(s_ln.h, el_)
                            v.tensor_scalar(xnlT[:, d, :], lred[:, :], 1.0 / SEG, None,
                                            op0=ALU.mult).then_inc(s_ln.h, 1)
                        lm_pieces.append(c_lm)
                    lm_ms = []

                    qk_mm = {}
                    qk_ev = {}
                    wq_at = s_gq.v

                    for nh in range(4):
                        for cc in range(16):
                            gidx = nh * 16 + cc

                            def qk_t(t, nh=nh, cc=cc, gidx=gidx, wq_at=wq_at):
                                if gidx == 0:
                                    t.wait_ge(s_gq.h, wq_at)
                                if cc == 0:
                                    for j in range(4):
                                        t.wait_ge(s_ev.h, evV[4 * nh + j])
                                if gidx >= 2:
                                    t.wait_ge(s_ev.h, qk_ev[gidx - 2])
                                pQ = pA if gidx % 2 == 0 else pB
                                for d in range(8):
                                    mm = t.matmul(pQ[:, :],
                                                  wqkvb[:, d, cc * 128:(cc + 1) * 128],
                                                  xnT[:, d, nh * 512:(nh + 1) * 512],
                                                  start=(d == 0), stop=(d == 7))
                                mm.then_inc(s_mm.h, 1)
                            qk_mm[gidx] = s_mm.bump()
                            k.emit("tensor", qk_t)

                            def qk_e(v, nh=nh, cc=cc, gidx=gidx):
                                v.wait_ge(s_mm.h, qk_mm[gidx])
                                pQ = pA if gidx % 2 == 0 else pB
                                scale = 0.125 if cc < 8 else 1.0
                                v.tensor_scalar(QKT[:, cc, nh * 512:(nh + 1) * 512],
                                                pQ[:, :], scale, None,
                                                op0=ALU.mult).then_inc(s_ev.h, 1)
                            qk_ev[gidx] = s_ev.bump()
                            k.emit("vector", qk_e)
                            if 8 <= gidx < 16:
                                el_ = s_ln.bump()
                                fn = lm_pieces[gidx - 8]
                                k.emit("vector",
                                       lambda v, fn=fn, el_=el_: fn(v, el_=el_))
                                lm_ms.append(s_ln.bump())

                    v_mm = {}
                    v_ev = {}
                    for i in range(16):
                        for dh in range(2):
                            vc = i * 2 + dh

                            def v_t(t, i=i, dh=dh, vc=vc):
                                if vc >= 2:
                                    t.wait_ge(s_ev2.h, v_ev[vc - 2])
                                pV = pC if vc % 2 == 0 else pD
                                for d in range(8):
                                    mm = t.matmul(pV[:, :], xnT[:, d, i * 128:(i + 1) * 128],
                                                  wqkvb[:, d, 2048 + dh * 512:2048 + (dh + 1) * 512],
                                                  start=(d == 0), stop=(d == 7))
                                mm.then_inc(s_mm.h, 1)
                            v_mm[vc] = s_mm.bump()
                            k.emit("tensor", v_t)

                            def v_e(sc, i=i, dh=dh, vc=vc):
                                sc.wait_ge(s_mm.h, v_mm[vc])
                                pV = pC if vc % 2 == 0 else pD
                                sc.copy(Vrm[:, i, dh * 512:(dh + 1) * 512],
                                        pV[:, :]).then_inc(s_ev2.h, 1)
                            v_ev[vc] = s_ev2.bump()
                            k.emit("scalar", v_e)

                    ql_mm = {}
                    ql_ev = {}
                    for cc in range(16):
                        def ql_t(t, cc=cc):
                            if cc == 0:
                                t.wait_ge(s_ln.h, lm_ms[-1])
                            if cc >= 2:
                                t.wait_ge(s_ev.h, ql_ev[cc - 2])
                            pQ = pE_ if cc % 2 == 0 else pF_
                            for d in range(8):
                                mm = t.matmul(pQ[:, 0:MLM],
                                              wqkvb[:, d, cc * 128:(cc + 1) * 128],
                                              xnlT[:, d, :], start=(d == 0), stop=(d == 7))
                            mm.then_inc(s_mm.h, 1)
                        ql_mm[cc] = s_mm.bump()
                        k.emit("tensor", ql_t)

                        def ql_e(v, cc=cc):
                            v.wait_ge(s_mm.h, ql_mm[cc])
                            pQ = pE_ if cc % 2 == 0 else pF_
                            scale = 0.125 if cc < 8 else 1.0
                            v.tensor_scalar(QlKlT[:, cc, :], pQ[:, 0:MLM], scale, None,
                                            op0=ALU.mult).then_inc(s_ev.h, 1)
                        ql_ev[cc] = s_ev.bump()
                        k.emit("vector", ql_e)

                    nun = 4
                    k.emit("gpsimd", emit_wprep(wprep_units[wprep_pos:wprep_pos + nun]))
                    wprep_pos += nun

                    k.run_block()

                if debug_stage == "p2ab":
                    return nc
                if debug_stage == "p2abd":
                    k.new_block()
                    def dump(sync):
                        sync.dma_start(dbg_d[:, 0, :], xnT[:, 0, :]).then_inc(s_st.h, 16)
                        sync.dma_start(dbg_d[:, 1, :], QKT[:, 0, :]).then_inc(s_st.h, 16)
                        sync.dma_start(dbg_d[:, 2, :], QKT[:, 8, :]).then_inc(s_st.h, 16)
                        sync.dma_start(dbg_d[:, 3, 0:1024], Vrm[:, 0, :]).then_inc(s_st.h, 16)
                        sync.dma_start(dbg_d[:, 3, 1024:1088], QlKlT[:, 0, :]).then_inc(s_st.h, 16)
                        sync.dma_start(dbg_d[:, 3, 1088:1152], xnlT[:, 0, :]).then_inc(s_st.h, 16)
                        sync.wait_ge(s_st.h, s_st_target)
                    s_st_target = s_st.bump(16 * 6)
                    k.emit("sync", dump)
                    k.run_block()
                    return nc

                with ExitStack() as pstk:
                    psb = lambda n, s, d: pstk.enter_context(nc.sbuf_tensor(k.uname(n), s, d))
                    SVT = psb("SVT", [128, 8, N], BF16)
                    E1T = psb("E1T", [128, 2, N], BF16)
                    E3T = psb("E3T", [128, 2, 16, 128], BF16)
                    e1f = psb("e1f", [128, 2, 512], F32)
                    e1n = psb("e1n", [128, 2, 512], BF16)
                    k2e = psb("k2e", [128, 64], F32)
                    k2s = psb("k2s", [128, 1], F32)
                    k2n = psb("k2n", [128, 64], BF16)
                    k2T_st = psb("k2T_st", [128, 64], BF16)
                    k2T_bd = psb("k2T_bd", [128, 128], BF16)
                    Kl_bd = psb("Kl_bd", [128, 128], BF16)
                    Ql_bd = psb("Ql_bd", [128, 128], BF16)
                    V_st = psb("V_st", [128, 64], BF16)
                    V_bd = psb("V_bd", [128, 128], BF16)
                    VT_bd = psb("VT_bd", [128, 128], BF16)
                    KVT_bd = psb("KVT_bd", [128, 128], BF16)
                    Z_bd = psb("Z_bd", [128, 128], BF16)
                    Zst = psb("Zst", [128, 64], BF16)
                    T_st = psb("T_st", [128, 64], BF16)
                    AVn = psb("AVn", [128, 64], BF16)
                    rdr = psb("rdr", [1, 2], F32)
                    scol = psb("scol", [128, 1], F32)
                    rds = psb("rds", [128, 1], F32)
                    s1r = psb("s1r", [128, 8], F32)
                    rs3 = psb("rs3", [128, 1], F32)
                    nmax = psb("nmax", [128, 1], F32)

                    k.new_block()

                    def pz(g):
                        for tile in (k2T_bd, Kl_bd, Ql_bd, V_bd, VT_bd, KVT_bd, Z_bd):
                            g.memset(tile[:, :], 0.0)
                        g.nop().then_inc(s_gp.h, 1)
                    bd_zero = s_gp.bump()
                    k.emit("gpsimd", pz)

                    prev_pf_free = None
                    prev_sv = {}
                    prev_av = {}
                    prev_tr2_ok = None
                    prev_kg_last = None
                    prev_k3e = None
                    prev_k1T = None
                    prev_svev = None
                    avn_prev = {}

                    sub = debug_stage if isinstance(debug_stage, str) else ""
                    npairs = 1 if sub.startswith("p2c") and sub != "p2c" else PAIRS
                    for pr in range(npairs):
                        par = pr % 2
                        def c0(g, pr=pr, gate=prev_kg_last):
                            if pr == 0:
                                g.wait_ge(s_gp.h, bd_zero)
                            if gate is not None:
                                g.wait_ge(s_mm.h, gate)
                            g.tensor_copy(Kl_bd[0:64, 0:64], QlKlT[0:64, 8 + pr, :])
                            g.tensor_copy(Kl_bd[64:128, 64:128], QlKlT[64:128, 8 + pr, :])
                            g.tensor_copy(Ql_bd[0:64, 0:64], QlKlT[0:64, pr, :])
                            g.tensor_copy(Ql_bd[64:128, 64:128],
                                          QlKlT[64:128, pr, :]).then_inc(s_gp.h, 1)
                        klbd_ok = s_gp.bump()
                        k.emit("gpsimd", c0)
                        if sub == "p2c_0":
                            break

                        def c1(t, pr=pr, gate=prev_pf_free):
                            if gate is not None:
                                t.wait_ge(s_nv.h, gate)
                            for hh in range(2):
                                mm = t.matmul(pF_[hh * 64:(hh + 1) * 64, 0:64],
                                              QlKlT[hh * 64:(hh + 1) * 64, pr, :],
                                              QlKlT[hh * 64:(hh + 1) * 64, 8 + pr, :],
                                              start=True, stop=True)
                            mm.then_inc(s_mm.h, 1)
                        l2_ms = s_mm.bump()
                        k.emit("tensor", c1)
                        if sub == "p2c_1":
                            break

                        def c2(v, l2_ms=l2_ms):
                            v.wait_ge(s_mm.h, l2_ms)
                            v.reduce_max(nmax[:, :], pF_[:, 0:64], axis=AX.X,
                                         negate=True).then_inc(s_nv.h, 1)
                        nmax_ms = s_nv.bump()
                        k.emit("vector", c2)
                        if sub == "p2c_2":
                            break

                        def c3(sc, nmax_ms=nmax_ms):
                            sc.wait_ge(s_nv.h, nmax_ms)
                            sc.activation(k2e[:, :], pF_[:, 0:64], AF.Exp,
                                          bias=nmax[:, :], scale=1.0,
                                          accum_out=k2s[:, :]).then_inc(s_ev2.h, 1)
                        k2e_ms = s_ev2.bump()
                        k.emit("scalar", c3)
                        if sub == "p2c_3":
                            break

                        ec4 = s_nv.bump()
                        def c4(v, k2e_ms=k2e_ms, gate=prev_kg_last, ec4=ec4):
                            v.wait_ge(s_ev2.h, k2e_ms)
                            if gate is not None:
                                v.wait_ge(s_mm.h, gate)
                            v.reciprocal(rds[:, :], k2s[:, :]).then_inc(s_nv.h, 1)
                            v.wait_ge(s_nv.h, ec4)
                            v.tensor_scalar(k2n[:, :], k2e[:, :], rds[:, :], None,
                                            op0=ALU.mult).then_inc(s_nv.h, 1)
                        k2n_ms = s_nv.bump()
                        k.emit("vector", c4)
                        if sub == "p2c_4":
                            break

                        def c5(t, k2n_ms=k2n_ms, gate=prev_tr2_ok):
                            t.wait_ge(s_nv.h, k2n_ms)
                            if gate is not None:
                                t.wait_ge(s_nv.h, gate)
                            t.transpose(pTr2[0:64, 0, 0:64], k2n[0:64, :], id64s[0:64, :])
                            mm = t.transpose(pTr2[64:128, 0, 64:128], k2n[64:128, :],
                                             id64s[64:128, :])
                            mm.then_inc(s_nt.h, 1)
                        k2t_ps = s_nt.bump()
                        k.emit("tensor", c5)
                        if sub == "p2c_5":
                            break

                        def c6(v, k2t_ps=k2t_ps):
                            v.wait_ge(s_nt.h, k2t_ps)
                            v.tensor_copy(k2T_st[0:64, :], pTr2[0:64, 0, 0:64])
                            v.tensor_copy(k2T_st[64:128, :], pTr2[64:128, 0, 64:128])
                            v.tensor_copy(k2T_bd[0:64, 0:64], pTr2[0:64, 0, 0:64])
                            v.tensor_copy(k2T_bd[64:128, 64:128],
                                          pTr2[64:128, 0, 64:128]).then_inc(s_nv.h, 1)
                        k2t_ok = s_nv.bump()
                        k.emit("vector", c6)
                        if sub == "p2c_6":
                            break

                        def c7a(v, k2t_ok=k2t_ok):
                            v.reduce_sum(scol[:, :], k2T_st[:, :], axis=AX.X
                                         ).then_inc(s_nv.h, 1)
                        scol_ms = s_nv.bump()
                        k.emit("vector", c7a)

                        def c7b(t, scol_ms=scol_ms):
                            t.wait_ge(s_nv.h, scol_ms)
                            t.transpose(pF_[0:1, 256:384], scol[:, :],
                                        id128f[:, :]).then_inc(s_mm.h, 1)
                        den_ms = s_mm.bump()
                        k.emit("tensor", c7b)
                        if sub == "p2c_7":
                            break

                        ec8 = s_nv.bump()
                        def c8(v, den_ms=den_ms, ec8=ec8):
                            v.wait_ge(s_mm.h, den_ms)
                            v.reduce_max(rdr[0:1, 0:1], pF_[0:1, 256:320], axis=AX.X)
                            v.reduce_max(rdr[0:1, 1:2], pF_[0:1, 320:384],
                                         axis=AX.X).then_inc(s_nv.h, 1)
                            v.wait_ge(s_nv.h, ec8)
                            v.reciprocal(rdr[0:1, :], rdr[0:1, :]).then_inc(s_nv.h, 1)
                        rd_ms = s_nv.bump()
                        k.emit("vector", c8)
                        if sub == "p2c_8":
                            break

                        def c9(t, rd_ms=rd_ms):
                            t.wait_ge(s_nv.h, rd_ms)
                            t.matmul(pF_[:, 192:193], sel0[0:1, :], rdr[0:1, 0:1],
                                     start=True, stop=False)
                            t.matmul(pF_[:, 192:193], sel1[0:1, :], rdr[0:1, 1:2],
                                     start=False, stop=True).then_inc(s_mm.h, 1)
                        rdb_ms = s_mm.bump()
                        k.emit("tensor", c9)
                        if sub == "p2c_9":
                            break

                        ec10 = s_nv.bump()
                        ec10b = s_nv.bump()
                        def c10(v, rdb_ms=rdb_ms, k2t_ok=k2t_ok, ec10=ec10, ec10b=ec10b):
                            v.wait_ge(s_mm.h, rdb_ms)
                            v.wait_ge(s_nv.h, k2t_ok)
                            v.tensor_copy(rds[:, :], pF_[:, 192:193]).then_inc(s_nv.h, 1)
                            v.wait_ge(s_nv.h, ec10)
                            v.tensor_scalar(V_st[:, :], k2T_st[:, :], rds[:, :], None,
                                            op0=ALU.mult).then_inc(s_nv.h, 1)
                            v.wait_ge(s_nv.h, ec10b)
                            v.tensor_copy(V_bd[0:64, 0:64], V_st[0:64, :])
                            v.tensor_copy(V_bd[64:128, 64:128],
                                          V_st[64:128, :]).then_inc(s_nv.h, 1)
                        v_ok = s_nv.bump()
                        k.emit("vector", c10)
                        if sub == "p2cS":
                            break

                        newton = []
                        vok = {0: v_ok}
                        for it in range(6):
                            slot = 1 + (it % 2)

                            def n_b(t, it=it, slot=slot, vok=vok):
                                t.wait_ge(s_nv.h, vok[it])
                                t.transpose(pTr2[0:64, slot, 0:64], V_st[0:64, :],
                                            id64s[0:64, :])
                                t.transpose(pTr2[64:128, slot, 64:128], V_st[64:128, :],
                                            id64s[64:128, :])
                                t.matmul(pF_[:, 256:320], k2T_bd[:, :], V_st[:, :],
                                         start=True, stop=True)
                                t.matmul(pF_[:, 320:384], V_bd[:, :], k2T_st[:, :],
                                         start=True, stop=True).then_inc(s_nt.h, 1)
                            kv_ms = s_nt.bump()

                            def n_c(v, it=it, slot=slot, kv_ms=kv_ms):
                                v.wait_ge(s_nt.h, kv_ms)
                                v.tensor_tensor(T_st[:, :], cIs[:, 0, :], pF_[:, 256:320],
                                                op=ALU.subtract)
                                v.tensor_copy(KVT_bd[0:64, 0:64], pF_[0:64, 320:384])
                                v.tensor_copy(KVT_bd[64:128, 64:128], pF_[64:128, 320:384])
                                v.tensor_copy(VT_bd[0:64, 0:64], pTr2[0:64, slot, 0:64])
                                v.tensor_copy(VT_bd[64:128, 64:128],
                                              pTr2[64:128, slot, 64:128]).then_inc(s_nv.h, 1)
                            t1_ms = s_nv.bump()

                            def n_d(t, t1_ms=t1_ms):
                                t.wait_ge(s_nv.h, t1_ms)
                                t.matmul(pF_[:, 384:448], KVT_bd[:, :], T_st[:, :],
                                         start=True, stop=True).then_inc(s_nt.h, 1)
                            u1_ms = s_nt.bump()

                            def n_e(v, u1_ms=u1_ms):
                                v.wait_ge(s_nt.h, u1_ms)
                                v.tensor_tensor(T_st[:, :], cIs[:, 1, :], pF_[:, 384:448],
                                                op=ALU.subtract).then_inc(s_nv.h, 1)
                            t2_ms = s_nv.bump()

                            def n_f(t, t2_ms=t2_ms):
                                t.wait_ge(s_nv.h, t2_ms)
                                t.matmul(pF_[:, 448:512], KVT_bd[:, :], T_st[:, :],
                                         start=True, stop=True).then_inc(s_nt.h, 1)
                            u2_ms = s_nt.bump()

                            def n_g(v, u2_ms=u2_ms):
                                v.wait_ge(s_nt.h, u2_ms)
                                v.tensor_tensor(T_st[:, :], cIs[:, 2, :], pF_[:, 448:512],
                                                op=ALU.subtract).then_inc(s_nv.h, 1)
                            t3_ms = s_nv.bump()

                            def n_h(t, t3_ms=t3_ms):
                                t.wait_ge(s_nv.h, t3_ms)
                                t.matmul(pF_[:, 256:320], VT_bd[:, :], T_st[:, :],
                                         start=True, stop=True).then_inc(s_nt.h, 1)
                            vn_ms = s_nt.bump()

                            eni = s_nv.bump() if it < 5 else None
                            def n_i(v, vn_ms=vn_ms, it=it, eni=eni):
                                v.wait_ge(s_nt.h, vn_ms)
                                if it < 5:
                                    v.tensor_scalar(V_st[:, :], pF_[:, 256:320], 0.25, None,
                                                    op0=ALU.mult).then_inc(s_nv.h, 1)
                                    v.wait_ge(s_nv.h, eni)
                                    v.tensor_copy(V_bd[0:64, 0:64], V_st[0:64, :])
                                    v.tensor_copy(V_bd[64:128, 64:128],
                                                  V_st[64:128, :]).then_inc(s_nv.h, 1)
                                else:
                                    v.tensor_scalar(V_st[:, :], pF_[:, 256:320], 0.25,
                                                    None, op0=ALU.mult).then_inc(s_nv.h, 1)
                            vok[it + 1] = s_nv.bump()

                            newton += [("tensor", n_b), ("vector", n_c),
                                       ("tensor", n_d), ("vector", n_e), ("tensor", n_f),
                                       ("vector", n_g), ("tensor", n_h), ("vector", n_i)]

                        kpath = []
                        k1g = {}
                        k1e_ms = {}
                        k1n_ms = {}
                        k1t_ms = {}
                        k1T_ms = {}
                        k3g = {}
                        k3e_ms = {}
                        for g8 in range(8):
                            isk1 = g8 < 4
                            gg = g8 % 4

                            def kg_t(t, pr=pr, g8=g8, isk1=isk1, gg=gg,
                                     klbd_ok=klbd_ok, pk3e=prev_k3e,
                                     k1e_ms=k1e_ms, k3e_ms=k3e_ms):
                                if g8 == 0:
                                    t.wait_ge(s_gp.h, klbd_ok)
                                if g8 >= 2:
                                    t.wait_ge(s_ev2.h,
                                              (k1e_ms if g8 - 2 < 4 else k3e_ms)[
                                                  (g8 - 2) % 4])
                                elif pk3e is not None:
                                    t.wait_ge(s_ev2.h, pk3e[2 + g8])
                                pX = pA if g8 % 2 == 0 else pB
                                cc = (pr if isk1 else 8 + pr)
                                bd = Kl_bd if isk1 else Ql_bd
                                for j in range(4):
                                    nch = 4 * gg + j
                                    mm = t.matmul(pX[:, j * 128:(j + 1) * 128],
                                                  QKT[:, cc, nch * 128:(nch + 1) * 128],
                                                  bd[:, :], start=True, stop=True)
                                mm.then_inc(s_mm.h, 1)
                            (k1g if isk1 else k3g)[gg] = s_mm.bump()
                            kpath.append(("tensor", kg_t))

                            if isk1:
                                def k1_s(sc, pr=pr, gg=gg, par=par, k1g=k1g, k1n_ms=k1n_ms):
                                    sc.wait_ge(s_mm.h, k1g[gg])
                                    if gg >= 2:
                                        sc.wait_ge(s_ev.h, k1n_ms[gg - 2])
                                    pX = pA if gg % 2 == 0 else pB
                                    sc.activation(e1f[:, gg % 2, :], pX[:, :],
                                                  AF.Exp).then_inc(s_ev2.h, 1)
                                k1e_ms[gg] = s_ev2.bump()
                                kpath.append(("scalar", k1_s))

                                ek1, ek2 = s_ev.bump(), s_ev.bump()
                                def k1_v(v, gg=gg, par=par, k1e_ms=k1e_ms,
                                         k1t_ms=k1t_ms, ek1=ek1, ek2=ek2):
                                    v.wait_ge(s_ev2.h, k1e_ms[gg])
                                    if gg >= 2:
                                        v.wait_ge(s_tp.h, k1t_ms[gg - 2])
                                    v.tensor_reduce(s1r[:, :],
                                                    e1f[:, gg % 2, :].rearrange(
                                                        "p (a m) -> p a m", a=8),
                                                    axis=AX.X, op=ALU.add).then_inc(s_ev.h, 1)
                                    v.wait_ge(s_ev.h, ek1)
                                    v.reciprocal(s1r[:, :], s1r[:, :]).then_inc(s_ev.h, 1)
                                    v.wait_ge(s_ev.h, ek2)
                                    for j in range(8):
                                        op = v.tensor_scalar(
                                            e1n[:, gg % 2, j * 64:(j + 1) * 64],
                                            e1f[:, gg % 2, j * 64:(j + 1) * 64],
                                            s1r[:, j:j + 1], None, op0=ALU.mult)
                                    op.then_inc(s_ev.h, 1)
                                k1n_ms[gg] = s_ev.bump()
                                kpath.append(("vector", k1_v))

                                def k1_tr(t, gg=gg, par=par, pk1T=prev_k1T, k1n_ms=k1n_ms, k1T_ms=k1T_ms):
                                    t.wait_ge(s_ev.h, k1n_ms[gg])
                                    if gg >= 1:
                                        t.wait_ge(s_ev2.h, k1T_ms[gg - 1])
                                    elif pk1T is not None:
                                        t.wait_ge(s_ev2.h, pk1T[3])
                                    for j in range(4):
                                        mm = t.transpose(pTrb[:, 4 * (gg % 2) + j, :],
                                                         e1n[:, gg % 2, j * 128:(j + 1) * 128],
                                                         id128[:, :])
                                    mm.then_inc(s_tp.h, 1)
                                k1t_ms[gg] = s_tp.bump()
                                kpath.append(("tensor", k1_tr))

                                def k1_e(sc, gg=gg, par=par, pr=pr, k1t_ms=k1t_ms):
                                    sc.wait_ge(s_tp.h, k1t_ms[gg])
                                    if gg == 0 and pr >= 2 and prev_sv.get(pr - 2):
                                        sc.wait_ge(s_mm.h, prev_sv[pr - 2])
                                    for j in range(4):
                                        nch = 4 * gg + j
                                        mm = sc.copy(E1T[:, par, nch * 128:(nch + 1) * 128],
                                                     pTrb[:, 4 * (gg % 2) + j, :])
                                    mm.then_inc(s_ev2.h, 1)
                                k1T_ms[gg] = s_ev2.bump()
                                kpath.append(("scalar", k1_e))
                            else:
                                def k3_s(sc, gg=gg, par=par, pr=pr, k3g=k3g):
                                    sc.wait_ge(s_mm.h, k3g[gg])
                                    if gg == 0 and pr >= 2 and prev_av.get(pr - 2):
                                        sc.wait_ge(s_mm.h, prev_av[pr - 2])
                                    pX = pA if gg % 2 == 0 else pB
                                    sc.activation(
                                        E3T[:, par, 4 * gg:4 * gg + 4, :].rearrange(
                                            "p a m -> p (a m)"),
                                        pX[:, :], AF.Exp).then_inc(s_ev2.h, 1)
                                k3e_ms[gg] = s_ev2.bump()
                                kpath.append(("scalar", k3_s))

                        def s3av_t(t, pr=pr, par=par, psvev=prev_svev, k3e_ms=k3e_ms):
                            t.wait_ge(s_ev2.h, k3e_ms[3])
                            if pr >= 1:
                                t.wait_ge(s_nv.h, avn_prev[pr - 1])
                            if psvev is not None:
                                t.wait_ge(s_ev.h, psvev[3])
                            for nch in range(16):
                                t.matmul(pC[:, 0:1], E3T[:, par, nch, :], ones128[:, :],
                                         start=(nch == 0), stop=(nch == 15))
                                mm = t.matmul(pD[:, 0:128], E3T[:, par, nch, :],
                                              Vrm[:, nch, pr * 128:(pr + 1) * 128],
                                              start=(nch == 0), stop=(nch == 15))
                            mm.then_inc(s_mm.h, 1)
                        av_ms = s_mm.bump()
                        prev_av[pr] = av_ms
                        kpath.append(("tensor", s3av_t))

                        eav = s_nv.bump()
                        def av_v(v, av_ms=av_ms, eav=eav):
                            v.wait_ge(s_mm.h, av_ms)
                            v.reciprocal(rs3[:, :], pC[:, 0:1]).then_inc(s_nv.h, 1)
                            v.wait_ge(s_nv.h, eav)
                            v.tensor_scalar(AVn[0:64, :], pD[0:64, 0:64],
                                            rs3[0:64, :], None, op0=ALU.mult)
                            v.tensor_scalar(AVn[64:128, :], pD[64:128, 64:128],
                                            rs3[64:128, :], None,
                                            op0=ALU.mult).then_inc(s_nv.h, 1)
                        avn_ms = s_nv.bump()
                        avn_prev[pr] = avn_ms
                        kpath.append(("vector", av_v))

                        if sub == "p2cN":
                            for eng, fn in newton:
                                k.emit(eng, fn)
                            break
                        if sub == "p2cK":
                            for eng, fn in newton + kpath:
                                k.emit(eng, fn)
                            break
                        merged = []
                        ni, ki = 0, 0
                        while ni < len(newton) or ki < len(kpath):
                            for _ in range(3):
                                if ni < len(newton):
                                    merged.append(newton[ni]); ni += 1
                            if ki < len(kpath):
                                merged.append(kpath[ki]); ki += 1
                        for eng, fn in merged:
                            k.emit(eng, fn)

                        def z_t(t, vok6=vok[6], avn_ms=avn_ms):
                            t.wait_ge(s_nv.h, vok6)
                            t.transpose(pTr2[0:64, 3, 0:64], V_st[0:64, :], id64s[0:64, :])
                            mm = t.transpose(pTr2[64:128, 3, 64:128], V_st[64:128, :],
                                             id64s[64:128, :])
                            mm.then_inc(s_nt.h, 1)
                        v6t_ps = s_nt.bump()
                        k.emit("tensor", z_t)

                        def z_v(v, v6t_ps=v6t_ps):
                            v.wait_ge(s_nt.h, v6t_ps)
                            v.tensor_copy(VT_bd[0:64, 0:64], pTr2[0:64, 3, 0:64])
                            v.tensor_copy(VT_bd[64:128, 64:128],
                                          pTr2[64:128, 3, 64:128]).then_inc(s_nv.h, 1)
                        v6t_ok = s_nv.bump()
                        k.emit("vector", z_v)

                        def z_mm(t, v6t_ok=v6t_ok, avn_ms=avn_ms):
                            t.wait_ge(s_nv.h, v6t_ok)
                            t.wait_ge(s_nv.h, avn_ms)
                            t.matmul(pF_[:, 384:448], VT_bd[:, :], AVn[:, :],
                                     start=True, stop=True).then_inc(s_nt.h, 1)
                        zmm_ms = s_nt.bump()
                        k.emit("tensor", z_mm)

                        def z_ev(v, zmm_ms=zmm_ms):
                            v.wait_ge(s_nt.h, zmm_ms)
                            v.tensor_copy(Zst[:, :], pF_[:, 384:448]).then_inc(s_nv.h, 1)
                        zst_ms = s_nv.bump()
                        k.emit("vector", z_ev)

                        def z_bd(g, zst_ms=zst_ms):
                            g.wait_ge(s_nv.h, zst_ms)
                            g.tensor_copy(Z_bd[0:64, 0:64], Zst[0:64, :])
                            g.tensor_copy(Z_bd[64:128, 64:128],
                                          Zst[64:128, :]).then_inc(s_gp.h, 1)
                        zbd_ms = s_gp.bump()
                        k.emit("gpsimd", z_bd)
                        prev_pf_free = zst_ms
                        prev_tr2_ok = vok[6]

                        sv_ev_ms = {}
                        for nh2 in range(4):
                            def sv_t(t, pr=pr, par=par, nh2=nh2, zbd_ms=zbd_ms, k1T_ms=k1T_ms, sv_ev_ms=sv_ev_ms):
                                if nh2 == 0:
                                    t.wait_ge(s_gp.h, zbd_ms)
                                    t.wait_ge(s_ev2.h, k1T_ms[3])
                                if nh2 >= 2:
                                    t.wait_ge(s_ev.h, sv_ev_ms[nh2 - 2])
                                pX = pE_ if nh2 % 2 == 0 else pC
                                mm = t.matmul(pX[:, :], Z_bd[:, :],
                                              E1T[:, par, nh2 * 512:(nh2 + 1) * 512],
                                              start=True, stop=True)
                                mm.then_inc(s_mm.h, 1)
                            svmm = s_mm.bump()
                            k.emit("tensor", sv_t)

                            def sv_e(v, pr=pr, nh2=nh2, svmm=svmm):
                                v.wait_ge(s_mm.h, svmm)
                                pX = pE_ if nh2 % 2 == 0 else pC
                                v.tensor_copy(SVT[:, pr, nh2 * 512:(nh2 + 1) * 512],
                                              pX[:, :]).then_inc(s_ev.h, 1)
                            sv_ev_ms[nh2] = s_ev.bump()
                            k.emit("vector", sv_e)
                        prev_sv[pr] = svmm
                        prev_kg_last = k3g[3]
                        prev_k3e = dict(k3e_ms)
                        prev_k1T = dict(k1T_ms)
                        prev_svev = dict(sv_ev_ms)

                    sv_done_v = s_ev.v
                    k.run_block()

                    if sub.startswith("p2c"):
                        return nc

                    with ExitStack() as estk:
                        esb = lambda n, s, d: estk.enter_context(nc.sbuf_tensor(k.uname(n), s, d))
                        wprojb = esb("wprojb", [128, 8, DM], BF16)
                        xc2 = esb("xc2", [128, 2, DM], F32)
                        x1t = esb("x1t", [128, 2, DM], F32)
                        stat2 = esb("stat2", [128, 16], F32)
                        mv3 = esb("mv3", [128, 2], F32)
                        rstd2 = esb("rstd2", [128, 1], F32)
                        sjunk2 = esb("sjunk2", [128, 4], F32)

                        k.new_block()
                        gq0 = s_gq.v

                        def e_gq(g):
                            for d in range(8):
                                g.dma_start(wprojb[:, d, :],
                                            wproj_d[d * 128:(d + 1) * 128, :]).then_inc(s_gq.h, 16)
                        s_gq.bump(16 * 8)
                        k.emit("gpsimd", e_gq)

                        pj_mm = {}
                        pj_ap = {}
                        st_ms = {}
                        for i in range(16):
                            def e_sync(sync, i=i, b=b):
                                if i >= 2:
                                    sync.wait_ge(s_ln.h, pj_ap[i - 2])
                                sync.dma_start(
                                    xc2[:, i % 2, :],
                                    x_d[b * N + 128 * i:b * N + 128 * (i + 1), :]
                                ).then_inc(s_x.h, 16)
                                if i >= 1:
                                    sync.wait_ge(s_ln.h, pj_ap[i - 1])
                                    sync.dma_start(
                                        x1s[b * N + 128 * (i - 1):b * N + 128 * i, :],
                                        x1t[:, (i - 1) % 2, :]).then_inc(s_st.h, 16)
                            x_at = s_x.bump(16)
                            if i >= 1:
                                s_st.bump(16)
                            k.emit("sync", e_sync)

                            def e_t(t, i=i, gq0=gq0):
                                if i == 0:
                                    t.wait_ge(s_gq.h, gq0 + 128)
                                for dh in range(2):
                                    gI = 2 * i + dh
                                    if gI >= 2:
                                        t.wait_ge(s_ln.h, pj_ap.get(i - 1, 0))
                                    pX = pA if gI % 2 == 0 else pB
                                    for pr2 in range(8):
                                        mm = t.matmul(pX[:, :],
                                                      SVT[:, pr2, i * 128:(i + 1) * 128],
                                                      wprojb[:, pr2, dh * 512:(dh + 1) * 512],
                                                      start=(pr2 == 0), stop=(pr2 == 7))
                                    mm.then_inc(s_mm.h, 1)
                            pj_mm[i] = (s_mm.bump(), s_mm.bump())
                            k.emit("tensor", e_t)

                            ee1, ee2 = s_ln.bump(), s_ln.bump()
                            def e_v1(v, i=i, x_at=x_at, ee1=ee1, ee2=ee2):
                                v.wait_ge(s_mm.h, pj_mm[i][1])
                                v.bn_stats(stat2[:, 0:6], pA[:, :]).then_inc(s_ln.h, 1)
                                v.wait_ge(s_ln.h, ee1)
                                v.bn_stats(stat2[:, 6:12], pB[:, :]).then_inc(s_ln.h, 1)
                                v.wait_ge(s_ln.h, ee2)
                                v.bn_aggr(mv3[:, :], stat2[:, 0:12]).then_inc(s_ln.h, 1)
                            pst = s_ln.bump()
                            k.emit("vector", e_v1)

                            def e_sq(sc, i=i, pst=pst):
                                sc.wait_ge(s_ln.h, pst)
                                sc.activation(sjunk2[:, 0:1], mv3[:, 1:2], AF.Sqrt,
                                              bias=epsc[:, 0:1], scale=1.0).then_inc(s_sq.h, 1)
                            sqm = s_sq.bump()
                            k.emit("scalar", e_sq)

                            ee3 = s_ln.bump()
                            def e_v2(v, i=i, sqm=sqm, x_at=x_at, ee3=ee3):
                                v.wait_ge(s_sq.h, sqm)
                                v.wait_ge(s_x.h, x_at)
                                v.reciprocal(rstd2[:, :], sjunk2[:, 0:1]).then_inc(s_ln.h, 1)
                                v.wait_ge(s_ln.h, ee3)
                                for dh in range(2):
                                    pX = pA if dh == 0 else pB
                                    v.tensor_scalar(x1t[:, i % 2, dh * 512:(dh + 1) * 512],
                                                    pX[:, :], mv3[:, 0:1], rstd2[:, :],
                                                    op0=ALU.subtract, op1=ALU.mult)
                                op = v.tensor_tensor(x1t[:, i % 2, :], x1t[:, i % 2, :],
                                                     xc2[:, i % 2, :], op=ALU.add)
                                op.then_inc(s_ln.h, 1)
                            pj_ap[i] = s_ln.bump()
                            k.emit("vector", e_v2)

                        def e_fin(sync, b=b):
                            sync.wait_ge(s_ln.h, pj_ap[15])
                            sync.dma_start(x1s[b * N + 128 * 15:b * N + 128 * 16, :],
                                           x1t[:, 15 % 2, :]).then_inc(s_st.h, 16)
                        s_st.bump(16)
                        k.emit("sync", e_fin)
                        k.run_block()

        if debug_stage in ("attn", "attn1"):
            k.new_block()
            st_target = s_st.v
            def fin(sync):
                sync.wait_ge(s_st.h, st_target)
            k.emit("sync", fin)
            k.run_block()
            return nc

        with ExitStack() as rstk:
            rsb = lambda n, s, d: rstk.enter_context(nc.sbuf_tensor(k.uname(n), s, d))
            xc1 = rsb("xc1", [128, 2, DM], F32)
            zf = rsb("zf", [128, 2, DM], F32)
            zbb = rsb("zbb", [128, 2, DM], BF16)
            znbt = rsb("znbt", [128, 2, DM], BF16)
            zT32 = rsb("zT32", [128, 2, 8, 128], F32)
            stat3 = rsb("stat3", [128, 16], F32)
            mvA = rsb("mvA", [128, 2], F32)
            mvB = rsb("mvB", [128, 2], F32)
            rstdA = rsb("rstdA", [128, 1], F32)
            rstdB = rsb("rstdB", [128, 1], F32)
            sjk = rsb("sjk", [128, 4], F32)
            affs = rsb("affs", [128, 8], F32)
            max8 = rsb("max8", [128, 8], F32)
            idx8 = rsb("idx8", [128, 8], U32)
            esf = rsb("esf", [128, 1], F32)
            ohf = rsb("ohf", [128, 8], F32)
            ohsel = rsb("ohsel", [128, 8], F32)
            rkred = rsb("rkred", [128, 1], F32)
            of32 = rsb("of32", [128, 1], F32)
            ovr = rsb("ovr", [128, 1], F32)
            ogf = rsb("ogf", [128, 1], F32)
            cnt8 = rsb("cnt8", [1, 8], F32)

            DBG = debug_stage == "dump"
            k.new_block()
            _breg = {}

            def r_breg(g):
                _breg["r"] = g.alloc_register(name="sc_bound")
                g.reg_mov(_breg["r"], E * CAP - 1)
            k.emit("gpsimd", r_breg)
            if wprep_pos < len(wprep_units):
                k.emit("gpsimd", emit_wprep(wprep_units[wprep_pos:]))
                wprep_pos = len(wprep_units)

            r_ap1 = {}
            r_ap2 = {}
            r_tp = {}
            r_ev = {}
            r_sc = {}
            r_ch = {}
            for c in range(NCH):
                def r_sync(sync, c=c):
                    if c >= 2:
                        sync.wait_ge(s_ln.h, r_ap2[c - 2])
                    sync.dma_start(xc1[:, c % 2, :],
                                   x1s[128 * c:128 * (c + 1), :]).then_inc(s_x.h, 16)
                x_at = s_x.bump(16)
                k.emit("sync", r_sync)

                er1, er2 = s_ln.bump(), s_ln.bump()
                def r_v1(v, c=c, x_at=x_at, er1=er1, er2=er2):
                    v.wait_ge(s_x.h, x_at)
                    v.bn_stats(stat3[:, 0:6], xc1[:, c % 2, 0:512]).then_inc(s_ln.h, 1)
                    v.wait_ge(s_ln.h, er1)
                    v.bn_stats(stat3[:, 6:12], xc1[:, c % 2, 512:1024]).then_inc(s_ln.h, 1)
                    v.wait_ge(s_ln.h, er2)
                    v.bn_aggr(mvA[:, :], stat3[:, 0:12]).then_inc(s_ln.h, 1)
                st1 = s_ln.bump()
                k.emit("vector", r_v1)

                def r_s1(sc, c=c, st1=st1):
                    sc.wait_ge(s_ln.h, st1)
                    sc.activation(sjk[:, 0:1], mvA[:, 1:2], AF.Sqrt,
                                  bias=epsc[:, 0:1], scale=1.0).then_inc(s_sq.h, 1)
                sq1 = s_sq.bump()
                k.emit("scalar", r_s1)

                er3, er4, er5 = s_ln.bump(), s_ln.bump(), s_ln.bump()
                def r_v2(v, c=c, sq1=sq1, er3=er3, er4=er4, er5=er5):
                    v.wait_ge(s_sq.h, sq1)
                    if c >= 2:
                        v.wait_ge(s_tp.h, r_tp[c - 2])
                        v.wait_ge(s_st.h, ST_ZB[c - 2])
                    v.reciprocal(rstdA[:, :], sjk[:, 0:1]).then_inc(s_ln.h, 1)
                    v.wait_ge(s_ln.h, er3)
                    v.tensor_scalar(zf[:, c % 2, :], xc1[:, c % 2, :], mvA[:, 0:1],
                                    rstdA[:, :], op0=ALU.subtract, op1=ALU.mult)
                    v.tensor_copy(zbb[:, c % 2, :], zf[:, c % 2, :])
                    v.bn_stats(stat3[:, 0:6], zf[:, c % 2, 0:512]).then_inc(s_ln.h, 1)
                    v.wait_ge(s_ln.h, er4)
                    v.bn_stats(stat3[:, 6:12], zf[:, c % 2, 512:1024]).then_inc(s_ln.h, 1)
                    v.wait_ge(s_ln.h, er5)
                    v.bn_aggr(mvB[:, :], stat3[:, 0:12]).then_inc(s_ln.h, 1)
                r_ap1[c] = s_ln.bump()
                k.emit("vector", r_v2)

                def r_st(sync, c=c):
                    sync.wait_ge(s_ln.h, r_ap1[c])
                    sync.dma_start(zs[128 * c:128 * (c + 1), :],
                                   zbb[:, c % 2, :]).then_inc(s_st.h, 16)
                if c == 0:
                    ST_ZB = {}
                ST_ZB[c] = s_st.bump(16)
                k.emit("sync", r_st)

                def r_s2(sc, c=c):
                    sc.wait_ge(s_ln.h, r_ap1[c])
                    sc.activation(sjk[:, 1:2], mvB[:, 1:2], AF.Sqrt,
                                  bias=epsc[:, 0:1], scale=1.0).then_inc(s_sq.h, 1)
                sq2 = s_sq.bump()
                k.emit("scalar", r_s2)

                er6 = s_ln.bump()
                def r_v3(v, c=c, sq2=sq2, er6=er6):
                    v.wait_ge(s_sq.h, sq2)
                    if c >= 2:
                        v.wait_ge(s_sc.h, r_sc[c - 2])
                    v.reciprocal(rstdB[:, :], sjk[:, 1:2]).then_inc(s_ln.h, 1)
                    v.wait_ge(s_ln.h, er6)
                    v.tensor_scalar(znbt[:, c % 2, :], zf[:, c % 2, :], mvB[:, 0:1],
                                    rstdB[:, :], op0=ALU.subtract, op1=ALU.mult
                                    ).then_inc(s_ln.h, 1)
                r_ap2[c] = s_ln.bump()
                k.emit("vector", r_v3)

                def r_tr(t, c=c):
                    t.wait_ge(s_ln.h, r_ap1[c])
                    if c >= 2:
                        t.wait_ge(s_ev.h, r_ev[c - 2])
                    bks = (pA, pB) if c % 2 == 0 else (pC, pD)
                    for d in range(8):
                        pX = bks[d // 4]
                        mm = t.transpose(pX[:, (d % 4) * 128:(d % 4 + 1) * 128],
                                         zf[:, c % 2, d * 128:(d + 1) * 128],
                                         id128f[:, :])
                    mm.then_inc(s_tp.h, 1)
                r_tp[c] = s_tp.bump()
                k.emit("tensor", r_tr)

                def r_tev(v, c=c):
                    v.wait_ge(s_tp.h, r_tp[c])
                    bks = (pA, pB) if c % 2 == 0 else (pC, pD)
                    for d in range(8):
                        pX = bks[d // 4]
                        mm = v.tensor_copy(zT32[:, c % 2, d, :],
                                           pX[:, (d % 4) * 128:(d % 4 + 1) * 128])
                    mm.then_inc(s_ev.h, 1)
                r_ev[c] = s_ev.bump()
                k.emit("vector", r_tev)

                def r_aff(t, c=c):
                    t.wait_ge(s_ev.h, r_ev[c])
                    if c >= 1:
                        t.wait_ge(s_ln.h, AFF_CONS[c - 1])
                    for d in range(8):
                        mm = t.matmul(pE_[:, 0:8], zT32[:, c % 2, d, :], CT[:, d, :],
                                      start=(d == 0), stop=(d == 7))
                    mm.then_inc(s_mm.h, 1)
                aff_ms = s_mm.bump()
                k.emit("tensor", r_aff)

                ea1, ea2, ea3, ea4 = (s_ln.bump(), s_ln.bump(), s_ln.bump(),
                                      s_ln.bump())
                def r_v4(v, c=c, aff_ms=aff_ms, ea1=ea1, ea2=ea2, ea3=ea3, ea4=ea4):
                    v.wait_ge(s_mm.h, aff_ms)

                    v.tensor_copy(affs[:, :], pE_[:, 0:8]).then_inc(s_ln.h, 1)
                    v.wait_ge(s_ln.h, ea1)
                    v.max(max8[:, :], affs[:, :]).then_inc(s_ln.h, 1)
                    v.wait_ge(s_ln.h, ea2)
                    v.max_index(idx8[:, :], max8[:, :], affs[:, :]).then_inc(s_ln.h, 1)
                    v.wait_ge(s_ln.h, ea3)
                    v.tensor_copy(esf[:, :], idx8[:, 0:1]).then_inc(s_ln.h, 1)
                    v.wait_ge(s_ln.h, ea4)
                    v.tensor_scalar(ohf[:, :], iota8[:, :], esf[:, :], None,
                                    op0=ALU.is_equal).then_inc(s_ln.h, 1)
                if c == 0:
                    AFF_CONS = {}
                AFF_CONS[c] = s_ln.bump()
                k.emit("vector", r_v4)

                def r_sg(v, c=c):
                    v.tensor_copy(mmax_all[:, c:c + 1], max8[:, 0:1]).then_inc(s_ln.h, 1)
                if c == 0:
                    SG_MS = {}
                SG_MS[c] = s_ln.bump()
                k.emit("vector", r_sg)

                def r_rk(t, c=c):
                    t.wait_ge(s_ln.h, AFF_CONS[c])
                    if c >= 1:
                        t.wait_ge(s_rt.h, r_ch[c - 1])
                    t.matmul(pF_[:, 0:8], ltu[:, :], ohf[:, :], start=True, stop=False)
                    t.matmul(pF_[:, 0:8], ones1f[0:1, :], Rrow[0:1, :],
                             start=False, stop=True)
                    t.matmul(pF_[0:1, 8:16], ones128f[:, :], ohf[:, :],
                             start=True, stop=True).then_inc(s_mm.h, 1)
                rk_ms = s_mm.bump()
                k.emit("tensor", r_rk)

                eb = [s_ln.bump() for _ in range(2)]
                eb2a = s_ln.bump()
                eb.append(s_ln.bump())
                eb6 = [s_ln.bump() for _ in range(2)]
                eb += [s_ln.bump() for _ in range(3)]
                def r_v5(v, c=c, rk_ms=rk_ms, eb=eb, eb6=eb6, eb2a=eb2a):
                    v.wait_ge(s_mm.h, rk_ms)
                    v.tensor_tensor(ohsel[:, :], pF_[:, 0:8], ohf[:, :],
                                    op=ALU.mult).then_inc(s_ln.h, 1)
                    v.wait_ge(s_ln.h, eb[0])
                    v.tensor_reduce(rkred[:, :], ohsel[:, :], axis=AX.X,
                                    op=ALU.add).then_inc(s_ln.h, 1)
                    v.wait_ge(s_ln.h, eb[1])
                    v.tensor_scalar(of32[:, :], esf[:, :], float(CAP), rkred[:, :],
                                    op0=ALU.mult, op1=ALU.add)
                    v.tensor_scalar(ovr[:, :], rkred[:, :], -(float(CAP) - 1.0), 0.0,
                                    op0=ALU.add, op1=ALU.max).then_inc(s_ln.h, 1)
                    v.wait_ge(s_ln.h, eb2a)
                    v.tensor_scalar(ovr[:, :], ovr[:, :], 1.0, None,
                                    op0=ALU.min).then_inc(s_ln.h, 1)
                    v.wait_ge(s_ln.h, eb[2])
                    v.tensor_scalar(ogf[:, :], of32[:, :], -1.0, float(E * CAP),
                                    op0=ALU.mult, op1=ALU.add).then_inc(s_ln.h, 1)
                    v.wait_ge(s_ln.h, eb6[0])
                    v.tensor_tensor(ogf[:, :], ogf[:, :], ovr[:, :],
                                    op=ALU.mult).then_inc(s_ln.h, 1)
                    v.wait_ge(s_ln.h, eb6[1])
                    v.tensor_tensor(ogf[:, :], ogf[:, :], of32[:, :],
                                    op=ALU.add).then_inc(s_ln.h, 1)
                    v.wait_ge(s_ln.h, eb[3])
                    v.tensor_copy(og_all[:, c:c + 1], ogf[:, :])
                    v.scalar_tensor_tensor(of32[:, :], ovr[:, :], 1.0e9, of32[:, :],
                                           op0=ALU.mult, op1=ALU.add).then_inc(s_ln.h, 1)
                    v.wait_ge(s_ln.h, eb[4])
                    v.tensor_copy(o_all[:, c:c + 1], of32[:, :])
                    v.tensor_copy(cnt8[0:1, :], pF_[0:1, 8:16]).then_inc(s_ln.h, 1)
                    v.wait_ge(s_ln.h, eb[5])
                    if DBG:
                        v.tensor_copy(dbg_rk[:, 0, c:c + 1], rkred[:, :])
                        v.tensor_copy(dbg_rk[:, 1, c:c + 1], ovr[:, :])
                        v.tensor_copy(dbg_rk[:, 2, c:c + 1], ogf[:, :])
                        v.tensor_copy(dbg_rk[:, 3, c:c + 1], esf[:, :])
                    v.tensor_tensor(Rrow[0:1, :], Rrow[0:1, :], cnt8[0:1, :],
                                    op=ALU.add).then_inc(s_rt.h, 1)
                r_ch[c] = s_rt.bump()
                k.emit("vector", r_v5)

                def r_scat(g, c=c):
                    if c == 0:
                        g.wait_ge(s_st.h, ZFILL_AT)
                    g.wait_ge(s_ln.h, r_ap2[c])
                    g.wait_ge(s_rt.h, r_ch[c])
                    g.indirect_dma_start(
                        out=znbkt, out_offset=IndirectOffsetOnAxis(o_all[:, c:c + 1], 0),
                        in_=znbt[:, c % 2, :], in_offset=None,
                        bounds_check=_breg["r"],
                        oob_is_err=False).then_inc(s_sc.h, 16)
                r_sc[c] = s_sc.bump(16)
                k.emit("gpsimd", r_scat)

            def r_tzero(sync):
                for cz in range((E * CAP + 128) // 128):
                    sync.dma_start(tbkt[128 * cz:128 * (cz + 1), :],
                                   zot[:, :]).then_inc(s_st.h, 16)
            s_st.bump(16 * ((E * CAP + 128) // 128))
            k.emit("sync", r_tzero)

            def r_sig_all(sc):
                sc.wait_ge(s_ln.h, SG_MS[NCH - 1])
                sc.activation(alpha_all[:, :], mmax_all[:, :],
                              AF.Sigmoid).then_inc(s_sq.h, 1)
            s_sq.bump()
            k.emit("scalar", r_sig_all)

            en1, en2 = s_ln.bump(), s_ln.bump()
            def r_nse(v, en1=en1, en2=en2):
                v.wait_ge(s_rt.h, r_ch[NCH - 1])
                v.tensor_scalar(nse_f[0:1, :], Rrow[0:1, :], 255.0, 1.0 / 256.0,
                                op0=ALU.add, op1=ALU.mult).then_inc(s_ln.h, 1)
                v.wait_ge(s_ln.h, en1)
                v.tensor_scalar(nse_f[0:1, :], nse_f[0:1, :], 0.25, None,
                                op0=ALU.add).then_inc(s_ln.h, 1)
                v.wait_ge(s_ln.h, en2)
                v.tensor_copy(nse_i[0:1, :], nse_f[0:1, :]).then_inc(s_rt.h, 1)
            NSE_MS = s_rt.bump()
            k.emit("vector", r_nse)
            k.run_block()

        with ExitStack() as fstk:
            fsb = lambda n, s, d: fstk.enter_context(nc.sbuf_tensor(k.uname(n), s, d))
            W1T = fsb("W1T", [128, 8, FF], BF16)
            W2T = fsb("W2T", [128, NFC, DM], BF16)
            znl = fsb("znl", [128, 2, 2, DM], BF16)
            znTt = fsb("znTt", [128, 2, 8, 256], BF16)
            hTt = fsb("hTt", [128, 2, 256], BF16)
            t_sb = fsb("t_sb", [128, 2, 2, DM], BF16)
            b2row = fsb("b2row", [1, 2, DM], BF16)
            PT4 = [(pA, pB), (pC, pD)]

            k.new_block()
            REGS = {}

            def mk_setup(eng_name):
                def setup(e):
                    REGS[eng_name] = (e.alloc_register(name="rns" + eng_name),
                                      e.alloc_register(name="rs" + eng_name))
                return setup
            for en_ in ("sync", "tensor", "vector", "scalar"):
                k.emit(en_, mk_setup(en_))

            def cond(e_obj, eng_name, ex, s):
                rns, rs = REGS[eng_name]
                if s == 0:
                    e_obj.reg_load(rns, nse_i[0:1, ex:ex + 1])
                e_obj.reg_mov(rs, s)
                return e_obj.If_lt(rs, rns)

            T_ST = {}
            B2_AT = {}
            W_AT = {}
            ZN_AT = {}

            for ex in range(E):
                def f_b2(g, ex=ex):
                    if ex >= 2:
                        g.wait_ge(s_w2.h, NFC * NSC * (ex - 1))
                    g.dma_start(b2row[:, ex % 2, :],
                                b2_d[ex:ex + 1, :]).then_inc(s_gq.h, 16)
                B2_AT[ex] = s_gq.bump(16)
                k.emit("gpsimd", f_b2)

                def f_wload(sync, ex=ex):
                    if ex >= 1:
                        sync.wait_ge(s_w1.h, NFC * NSC * ex)
                        sync.wait_ge(s_w2.h, NFC * NSC * ex)
                    for d in range(8):
                        sync.dma_start(W1T[:, d, :], w1b[ex, :, d * 128:(d + 1) * 128],
                                       transpose=True).then_inc(s_w.h, 16)
                    for f in range(NFC):
                        sync.dma_start(W2T[:, f, :], w2b[ex, :, f * 128:(f + 1) * 128],
                                       transpose=True).then_inc(s_w.h, 16)
                    for s in range(2):
                        sg = NSC * ex + s
                        if sg >= 2:
                            sync.wait_ge(s_trg.h, 2 * (sg - 1))
                        for gi in range(2):
                            sync.dma_start(
                                znl[:, sg % 2, gi, :],
                                znbkt[ex * CAP + s * 256 + gi * 128:
                                      ex * CAP + s * 256 + gi * 128 + 128, :]
                            ).then_inc(s_zn.h, 16)
                s_w.bump(16 * 40)
                W_AT[ex] = s_w.v
                W_BASE = W_AT[ex] - 16 * 40
                if ex == 0:
                    W_BASES = {}
                W_BASES[ex] = W_BASE
                ZN_AT[NSC * ex] = s_zn.bump(32)
                ZN_AT[NSC * ex + 1] = s_zn.bump(32)
                k.emit("sync", f_wload)

                def f_tpro(t, ex=ex):
                    t.wait_ge(s_w.h, W_BASES[ex] + 16 * 8)
                    for gi in range(2):
                        sg = NSC * ex
                        grp = 2 * sg + gi
                        pT = pTrb if grp % 2 == 0 else pTr2
                        t.wait_ge(s_zn.h, ZN_AT[sg])
                        if grp >= 2:
                            t.wait_ge(s_znT.h, grp - 1)
                        for d in range(8):
                            mm = t.transpose(pT[:, d, :],
                                             znl[:, sg % 2, gi, d * 128:(d + 1) * 128],
                                             id128[:, :])
                        mm.then_inc(s_trg.h, 1)
                s_trg.bump(2)
                k.emit("tensor", f_tpro)

                for s in range(NSC):
                    sg = NSC * ex + s

                    has_pre = s + 2 < NSC

                    def f_sync(sync, ex=ex, s=s, sg=sg, has_pre=has_pre):
                        with cond(sync, "sync", ex, s):
                            if has_pre:
                                sync.wait_ge(s_trg.h, 2 * (sg + 1))
                                for gi in range(2):
                                    sync.dma_start(
                                        znl[:, sg % 2, gi, :],
                                        znbkt[ex * CAP + (s + 2) * 256 + gi * 128:
                                              ex * CAP + (s + 2) * 256 + gi * 128 + 128, :]
                                    ).then_inc(s_zn.h, 16)
                            sync.wait_ge(s_te.h, sg + 1)
                            for gi in range(2):
                                sync.dma_start(
                                    tbkt[ex * CAP + s * 256 + gi * 128:
                                         ex * CAP + s * 256 + gi * 128 + 128, :],
                                    t_sb[:, sg % 2, gi, :]).then_inc(s_st.h, 16)
                        with sync.Else():
                            if has_pre:
                                sync.sem_inc(s_zn.h, 32)
                            sync.sem_inc(s_st.h, 32)
                    if has_pre:
                        ZN_AT[sg + 2] = s_zn.bump(32)
                    T_ST[sg] = s_st.bump(32)
                    k.emit("sync", f_sync)

                    has_tr = s + 1 < NSC

                    def f_tensor(t, ex=ex, s=s, sg=sg, has_tr=has_tr):
                        def w2_group(kk):
                            ff = kk % NFC
                            t.wait_ge(s_he.h, kk + 1)
                            if s == 0:
                                t.wait_ge(s_w.h, W_BASES[ex] + 16 * (8 + ff + 1))
                            if ff == 0:
                                if sg >= 1:
                                    t.wait_ge(s_te.h, sg)
                                t.wait_ge(s_gq.h, B2_AT[ex])
                                for sg2 in range(2):
                                    for dh in range(2):
                                        t.matmul(PT4[sg2][dh][:, :], ones1r[:, :],
                                                 b2row[0:1, ex % 2, dh * 512:(dh + 1) * 512],
                                                 start=True, stop=False)
                            for sg2 in range(2):
                                for dh in range(2):
                                    mm = t.matmul(PT4[sg2][dh][:, :],
                                                  hTt[:, kk % 2, sg2 * 128:(sg2 + 1) * 128],
                                                  W2T[:, ff, dh * 512:(dh + 1) * 512],
                                                  start=False, stop=(ff == NFC - 1))
                            mm.then_inc(s_w2.h, 1)

                        def tr_group(sgp, gi):
                            grp = 2 * sgp + gi
                            pT = pTrb if grp % 2 == 0 else pTr2
                            t.wait_ge(s_zn.h, ZN_AT[sgp])
                            t.wait_ge(s_znT.h, grp - 1)
                            for d in range(8):
                                mm = t.transpose(pT[:, d, :],
                                                 znl[:, sgp % 2, gi, d * 128:(d + 1) * 128],
                                                 id128[:, :])
                            mm.then_inc(s_trg.h, 1)

                        with cond(t, "tensor", ex, s):
                            t.wait_ge(s_znT.h, 2 * (sg + 1))
                            for f in range(NFC):
                                kk = NFC * sg + f
                                pw = pE_ if kk % 2 == 0 else pF_
                                if kk >= 2:
                                    t.wait_ge(s_he.h, kk - 1)
                                for d in range(8):
                                    mm = t.matmul(pw[:, 0:256],
                                                  W1T[:, d, f * 128:(f + 1) * 128],
                                                  znTt[:, sg % 2, d, :],
                                                  start=(d == 0), stop=(d == 7))
                                mm.then_inc(s_w1.h, 1)
                                if has_tr and f == 26:
                                    tr_group(sg + 1, 0)
                                if has_tr and f == 28:
                                    tr_group(sg + 1, 1)
                                if f >= 1:
                                    w2_group(kk - 1)
                            w2_group(NFC * sg + NFC - 1)
                        with t.Else():
                            t.sem_inc(s_w1.h, NFC)
                            t.sem_inc(s_w2.h, NFC)
                            if has_tr:
                                t.sem_inc(s_trg.h, 2)
                    s_w1.bump(NFC)
                    s_w2.bump(NFC)
                    if has_tr:
                        s_trg.bump(2)
                    k.emit("tensor", f_tensor)

                    def f_vector(v, ex=ex, s=s, sg=sg):
                        with cond(v, "vector", ex, s):
                            if sg >= 2:
                                v.wait_ge(s_w1.h, NFC * (sg - 1))
                            for gi in range(2):
                                grp = 2 * sg + gi
                                pT = pTrb if grp % 2 == 0 else pTr2
                                v.wait_ge(s_trg.h, grp + 1)
                                for d in range(8):
                                    op = v.tensor_scalar(
                                        znTt[:, sg % 2, d, gi * 128:(gi + 1) * 128],
                                        pT[:, d, :], GT[:, d, ex:ex + 1],
                                        BT[:, d, ex:ex + 1],
                                        op0=ALU.mult, op1=ALU.add)
                                op.then_inc(s_znT.h, 1)
                        with v.Else():
                            v.sem_inc(s_znT.h, 2)
                    s_znT.bump(2)
                    k.emit("vector", f_vector)

                    def f_scalar(sc, ex=ex, s=s, sg=sg):
                        with cond(sc, "scalar", ex, s):
                            for f in range(NFC):
                                kk = NFC * sg + f
                                pw = pE_ if kk % 2 == 0 else pF_
                                sc.wait_ge(s_w1.h, kk + 1)
                                if kk >= 2:
                                    sc.wait_ge(s_w2.h, kk - 1)
                                sc.activation(hTt[:, kk % 2, :], pw[:, 0:256], AF.Relu,
                                              bias=B1T[:, f, ex:ex + 1],
                                              scale=1.0).then_inc(s_he.h, 1)
                            sc.wait_ge(s_w2.h, NFC * (sg + 1))
                            if sg >= 2:
                                sc.wait_ge(s_st.h, T_ST[sg - 2])
                            for sg2 in range(2):
                                for dh in range(2):
                                    op = sc.copy(t_sb[:, sg % 2, sg2, dh * 512:(dh + 1) * 512],
                                                 PT4[sg2][dh][:, :])
                            op.then_inc(s_te.h, 1)
                        with sc.Else():
                            sc.sem_inc(s_he.h, NFC)
                            sc.sem_inc(s_te.h, 1)
                    s_he.bump(NFC)
                    s_te.bump(1)
                    k.emit("scalar", f_scalar)

            k.run_block()

        with ExitStack() as estk2:
            qsb = lambda n, s, d: estk2.enter_context(nc.sbuf_tensor(k.uname(n), s, d))
            xq = qsb("xq", [128, 2, DM], F32)
            zq = qsb("zq", [128, 2, DM], BF16)
            tq = qsb("tq", [128, 2, DM], BF16)
            fq = qsb("fq", [128, 2, DM], F32)
            oq = qsb("oq", [128, 2, DM], F32)
            stat4 = qsb("stat4", [128, 16], F32)
            mv4 = qsb("mv4", [128, 2], F32)
            rstd4 = qsb("rstd4", [128, 1], F32)
            sjk4 = qsb("sjk4", [128, 4], F32)

            k.new_block()
            q_ap = {}
            q_st = {}
            for c in range(NCH):
                def q_sync(sync, c=c):
                    if c >= 2:
                        sync.wait_ge(s_ln.h, q_ap[c - 2])
                    sync.dma_start(xq[:, c % 2, :],
                                   x1s[128 * c:128 * (c + 1), :]).then_inc(s_x.h, 16)
                    sync.dma_start(zq[:, c % 2, :],
                                   zs[128 * c:128 * (c + 1), :]).then_inc(s_x.h, 16)
                x_at = s_x.bump(32)
                k.emit("sync", q_sync)

                def q_g(g, c=c):
                    if c >= 2:
                        g.wait_ge(s_ln.h, q_ap[c - 2])
                    g.indirect_dma_start(
                        out=tq[:, c % 2, :], out_offset=None, in_=tbkt,
                        in_offset=IndirectOffsetOnAxis(og_all[:, c:c + 1], 0)
                    ).then_inc(s_sc.h, 16)
                t_at = s_sc.bump(16)
                k.emit("gpsimd", q_g)

                eq0, eq1, eq2 = s_ln.bump(), s_ln.bump(), s_ln.bump()
                def q_v1(v, c=c, x_at=x_at, t_at=t_at, eq0=eq0, eq1=eq1, eq2=eq2):
                    v.wait_ge(s_x.h, x_at)
                    v.wait_ge(s_sc.h, t_at)
                    v.scalar_tensor_tensor(fq[:, c % 2, :], tq[:, c % 2, :],
                                           alpha_all[:, c:c + 1], zq[:, c % 2, :],
                                           op0=ALU.mult, op1=ALU.add).then_inc(s_ln.h, 1)
                    v.wait_ge(s_ln.h, eq0)
                    v.bn_stats(stat4[:, 0:6], fq[:, c % 2, 0:512]).then_inc(s_ln.h, 1)
                    v.wait_ge(s_ln.h, eq1)
                    v.bn_stats(stat4[:, 6:12], fq[:, c % 2, 512:1024]).then_inc(s_ln.h, 1)
                    v.wait_ge(s_ln.h, eq2)
                    v.bn_aggr(mv4[:, :], stat4[:, 0:12]).then_inc(s_ln.h, 1)
                stq = s_ln.bump()
                k.emit("vector", q_v1)

                def q_s(sc, c=c, stq=stq):
                    sc.wait_ge(s_ln.h, stq)
                    sc.activation(sjk4[:, 0:1], mv4[:, 1:2], AF.Sqrt,
                                  bias=epsc[:, 0:1], scale=1.0).then_inc(s_sq.h, 1)
                sqq = s_sq.bump()
                k.emit("scalar", q_s)

                eq3 = s_ln.bump()
                def q_v2(v, c=c, sqq=sqq, eq3=eq3):
                    v.wait_ge(s_sq.h, sqq)
                    if c >= 2:
                        v.wait_ge(s_st.h, q_st[c - 2])
                    v.reciprocal(rstd4[:, :], sjk4[:, 0:1]).then_inc(s_ln.h, 1)
                    v.wait_ge(s_ln.h, eq3)
                    v.tensor_scalar(oq[:, c % 2, :], fq[:, c % 2, :], mv4[:, 0:1],
                                    rstd4[:, :], op0=ALU.subtract, op1=ALU.mult)
                    v.tensor_tensor(oq[:, c % 2, :], oq[:, c % 2, :], xq[:, c % 2, :],
                                    op=ALU.add).then_inc(s_ln.h, 1)
                q_ap[c] = s_ln.bump()
                k.emit("vector", q_v2)

                def q_store(sync, c=c):
                    sync.wait_ge(s_ln.h, q_ap[c])
                    sync.dma_start(out_d[128 * c:128 * (c + 1), :],
                                   oq[:, c % 2, :]).then_inc(s_st.h, 16)
                q_st[c] = s_st.bump(16)
                k.emit("sync", q_store)

            fin_target = s_st.v
            def q_fin(sync):
                sync.wait_ge(s_st.h, fin_target)
            k.emit("sync", q_fin)
            k.run_block()

            if debug_stage == "dump":
                k.new_block()
                base = s_x.v

                def dmp(sync, base=base):
                    n = 0
                    pairs = ([(d_zn, znbkt, E * CAP // 128)] +
                             [(d_t, tbkt, E * CAP // 128 + 1)] +
                             [(d_z, zs, ROWS // 128)])
                    for dst, srcb, nch2 in pairs:
                        for cc2 in range(nch2):
                            if n >= 2:
                                sync.wait_ge(s_x.h, base + 16 * (2 * (n - 1)))
                            sync.dma_start(tq[:, n % 2, :],
                                           srcb[128 * cc2:128 * (cc2 + 1), :]
                                           ).then_inc(s_x.h, 16)
                            sync.wait_ge(s_x.h, base + 32 * n + 16)
                            sync.dma_start(dst[128 * cc2:128 * (cc2 + 1), :],
                                           tq[:, n % 2, :]).then_inc(s_x.h, 16)
                            n += 1
                    sync.dma_start(d_o[:, 0, :], o_all[:, :]).then_inc(s_x.h, 16)
                    sync.dma_start(d_o[:, 1, :], og_all[:, :]).then_inc(s_x.h, 16)
                    sync.dma_start(d_al[:, :], alpha_all[:, :]).then_inc(s_x.h, 16)
                    sync.dma_start(d_rk[:, :, :], dbg_rk[:, :, :]).then_inc(s_x.h, 16)
                    sync.wait_ge(s_x.h, base + 32 * n + 64)
                ntotal = E * CAP // 128 + E * CAP // 128 + 1 + ROWS // 128
                s_x.bump(32 * ntotal + 64)
                k.emit("sync", dmp)
                k.run_block()

    return nc


def kernel(**inputs):
    x = np.ascontiguousarray(inputs["L0_attn_x"], np.float32)
    nc = build_program()
    in_maps = []
    for c in range(8):
        shard = {
            "x": np.ascontiguousarray(x[4 * c:4 * (c + 1)].reshape(ROWS, DM)),
            "wqkv": np.ascontiguousarray(inputs["L0_attn_w_qkv"], np.float32),
            "wproj": np.ascontiguousarray(inputs["L0_attn_w_proj"], np.float32),
            "cent": np.ascontiguousarray(inputs["L0_ffn_centroids"], np.float32),
            "lng": np.ascontiguousarray(inputs["L0_ffn_ln_g"], np.float32),
            "lnb": np.ascontiguousarray(inputs["L0_ffn_ln_b"], np.float32),
            "w1": np.ascontiguousarray(inputs["L0_ffn_w1"], np.float32),
            "b1": np.ascontiguousarray(inputs["L0_ffn_b1"], np.float32),
            "w2": np.ascontiguousarray(inputs["L0_ffn_w2"], np.float32),
            "b2": np.ascontiguousarray(inputs["L0_ffn_b2"], np.float32),
        }
        in_maps.append(shard)
    res = run_bass_kernel_spmd(nc, in_maps, core_ids=list(range(8)))
    out = np.concatenate([r["out"].reshape(4, N, DM) for r in res.results], axis=0)
    return out


# version 11
# speedup vs baseline: 1.1269x; 1.1269x over previous
import numpy as np
import ml_dtypes
from contextlib import ExitStack

import concourse.bass as bass
import concourse.mybir as mybir
from concourse.bass_utils import run_bass_kernel_spmd
from bass_rust import IndirectOffsetOnAxis

F32 = mybir.dt.float32
BF16 = mybir.dt.bfloat16
U32 = mybir.dt.uint32
AF = mybir.ActivationFunctionType
ALU = mybir.AluOpType
AX = mybir.AxisListType

B = 4
N = 2048
DM = 1024
H = 16
PAIRS = 8
HD = 64
MLM = 64
SEG = 32
ROWS = B * N
NCH = ROWS // 128
E = 8
FF = 4096
CAP = 4096
NSC = CAP // 256
NFC = FF // 128
EPS = 1e-5

ENGS = ("sync", "tensor", "vector", "scalar", "gpsimd")


class Sem:
    def __init__(self, stack, nc, name):
        self.h = stack.enter_context(nc.semaphore(name))
        self.v = 0

    def bump(self, k=1):
        self.v += k
        return self.v


class MK:
    def __init__(self, nc, stack, debug_stage=None):
        self.nc = nc
        self.stack = stack
        self.debug_stage = debug_stage
        self.q = None

    def new_block(self):
        self.q = {k: [] for k in ENGS}

    def emit(self, eng, fn):
        self.q[eng].append(fn)

    def run_block(self):
        q = self.q
        with self.nc.Block() as blk:
            for eng in ENGS:
                fns = q[eng]
                if not fns:
                    continue
                def body(e, fns=fns):
                    for fn in fns:
                        fn(e)
                getattr(blk, eng)(body)
        self.q = None

    _uid = 0

    def uname(self, name):
        MK._uid += 1
        return f"{name}_u{MK._uid}"

    def sb(self, name, shape, dt):
        return self.stack.enter_context(self.nc.sbuf_tensor(name, shape, dt))

    def ps(self, name, shape, dt):
        return self.stack.enter_context(self.nc.psum_tensor(name, shape, dt))

    def sem(self, name):
        return Sem(self.stack, self.nc, name)


def build_program(debug_stage=None):
    nc = bass.Bass("TRN2", target_bir_lowering=False, detect_race_conditions=False)

    x_d = nc.dram_tensor("x", [ROWS, DM], F32, kind="ExternalInput").ap()
    wqkv_d = nc.dram_tensor("wqkv", [DM, 3 * DM], F32, kind="ExternalInput").ap()
    wproj_d = nc.dram_tensor("wproj", [DM, DM], F32, kind="ExternalInput").ap()
    cent_d = nc.dram_tensor("cent", [E, DM], F32, kind="ExternalInput").ap()
    lng_d = nc.dram_tensor("lng", [E, DM], F32, kind="ExternalInput").ap()
    lnb_d = nc.dram_tensor("lnb", [E, DM], F32, kind="ExternalInput").ap()
    w1_d = nc.dram_tensor("w1", [E, FF, DM], F32, kind="ExternalInput").ap()
    b1_d = nc.dram_tensor("b1", [E, FF], F32, kind="ExternalInput").ap()
    w2_d = nc.dram_tensor("w2", [E, DM, FF], F32, kind="ExternalInput").ap()
    b2_d = nc.dram_tensor("b2", [E, DM], F32, kind="ExternalInput").ap()
    out_d = nc.dram_tensor("out", [ROWS, DM], F32, kind="ExternalOutput").ap()

    dbg_d = None
    if debug_stage in ("p2abd",):
        dbg_d = nc.dram_tensor("dbg", [128, 4, 2048], BF16, kind="ExternalOutput").ap()
    x1_kind = "ExternalOutput" if debug_stage == "attn" else "Internal"
    x1s = nc.dram_tensor("x1s", [ROWS, DM], F32, kind=x1_kind).ap()
    zs = nc.dram_tensor("zs", [ROWS, DM], BF16, kind="Internal").ap()
    znbkt = nc.dram_tensor("znbkt", [E * CAP, DM], BF16, kind="Internal").ap()
    tbkt = nc.dram_tensor("tbkt", [E * CAP + 128, DM], BF16, kind="Internal").ap()
    if debug_stage == "dump":
        d_zn = nc.dram_tensor("d_zn", [E * CAP, DM], BF16, kind="ExternalOutput").ap()
        d_t = nc.dram_tensor("d_t", [E * CAP + 128, DM], BF16, kind="ExternalOutput").ap()
        d_z = nc.dram_tensor("d_z", [ROWS, DM], BF16, kind="ExternalOutput").ap()
        d_o = nc.dram_tensor("d_o", [128, 2, NCH], U32, kind="ExternalOutput").ap()
        d_rk = nc.dram_tensor("d_rk", [128, 4, NCH], F32, kind="ExternalOutput").ap()
        d_al = nc.dram_tensor("d_al", [128, NCH], F32, kind="ExternalOutput").ap()
    w1b = nc.dram_tensor("w1b", [E, FF, DM], BF16, kind="Internal").ap()
    w2b = nc.dram_tensor("w2b", [E, DM, FF], BF16, kind="Internal").ap()

    with ExitStack() as stack:
        k = MK(nc, stack, debug_stage)
        sb, ps, sem = k.sb, k.ps, k.sem

        id128 = sb("id128", [128, 128], BF16)
        id64s = sb("id64s", [128, 64], BF16)
        id128f = sb("id128f", [128, 128], F32)
        ltu = sb("ltu", [128, 128], F32)
        ones128f = sb("ones128f", [128, 1], F32)
        iota8 = sb("iota8", [128, 8], F32)
        ones128 = sb("ones128", [128, 1], BF16)
        ones1r = sb("ones1r", [1, 128], BF16)
        ones1f = sb("ones1f", [1, 128], F32)
        sel0 = sb("sel0", [1, 128], F32)
        sel1 = sb("sel1", [1, 128], F32)
        cIs = sb("cIs", [128, 3, 64], F32)
        CT = sb("CT", [128, 8, E], F32)
        GT = sb("GT", [128, 8, E], F32)
        BT = sb("BT", [128, 8, E], F32)
        B1T = sb("B1T", [128, NFC, E], F32)
        alpha_all = sb("alpha_all", [128, NCH], F32)
        mmax_all = sb("mmax_all", [128, NCH], F32)
        dbg_rk = sb("dbg_rk", [128, 4, NCH], F32)
        o_all = sb("o_all", [128, NCH], U32)
        og_all = sb("og_all", [128, NCH], U32)
        Rrow = sb("Rrow", [1, E], F32)
        nse_f = sb("nse_f", [1, E], F32)
        nse_i = sb("nse_i", [1, E], mybir.dt.int32)
        epsc = sb("epsc", [128, 1], F32)
        zot = sb("zot", [128, DM], BF16)

        pA = ps("pA", [128, 512], F32)
        pB = ps("pB", [128, 512], F32)
        pC = ps("pC", [128, 512], F32)
        pD = ps("pD", [128, 512], F32)
        pE_ = ps("pE", [128, 512], F32)
        pF_ = ps("pF", [128, 512], F32)
        pTrb = ps("pTrb", [128, 8, 128], BF16)
        pTr2 = ps("pTr2", [128, 8, 128], BF16)

        s_x = sem("s_x")
        s_st = sem("s_st")
        s_gq = sem("s_gq")
        s_wp = sem("s_wp")
        s_ln = sem("s_ln")
        s_sq = sem("s_sq")
        s_tp = sem("s_tp")
        s_ev = sem("s_ev")
        s_ev2 = sem("s_ev2")
        s_mm = sem("s_mm")
        s_nt = sem("s_nt")
        s_nv = sem("s_nv")
        s_gp = sem("s_gp")
        s_rt = sem("s_rt")
        s_w = sem("s_w")
        s_zn = sem("s_zn")
        s_trg = sem("s_trg")
        s_znT = sem("s_znT")
        s_w1 = sem("s_w1")
        s_he = sem("s_he")
        s_w2 = sem("s_w2")
        s_te = sem("s_te")
        s_sc = sem("s_sc")

        wprep_units = [("w1", e) for e in range(E)] + [("w2", e) for e in range(E)]

        def emit_wprep(units):
            def body(g, units=units):
                for kind, e in units:
                    if kind == "w1":
                        g.dma_start(w1b[e, :, :], w1_d[e, :, :]).then_inc(s_wp.h, 16)
                    else:
                        g.dma_start(w2b[e, :, :], w2_d[e, :, :]).then_inc(s_wp.h, 16)
            s_wp.bump(16 * len(units))
            return body

        p0stk = ExitStack()
        rowbuf = p0stk.enter_context(nc.sbuf_tensor("rowbuf", [1, 4 * DM], F32))
        k.new_block()

        def p0_g(g):
            def ident_into(t, val=1.0, halfsize=None):
                g.memset(t, 0.0)
                if halfsize is None:
                    sq = t.shape[-1]
                    g.affine_select(out=t, in_=t, compare_op=ALU.not_equal, fill=val,
                                    base=0, pattern=[[-1, sq]], channel_multiplier=1)
            ident_into(id128[:, :])
            ident_into(id128f[:, :])
            g.memset(id64s[:, :], 0.0)
            for hh in range(2):
                g.affine_select(out=id64s[hh * 64:(hh + 1) * 64, :],
                                in_=id64s[hh * 64:(hh + 1) * 64, :],
                                compare_op=ALU.not_equal, fill=1.0, base=0,
                                pattern=[[-1, 64]], channel_multiplier=1)
            g.memset(ltu[:, :], 1.0)
            g.affine_select(out=ltu[:, :], in_=ltu[:, :], compare_op=ALU.is_ge,
                            fill=0.0, base=-1, pattern=[[1, 128]], channel_multiplier=-1)
            g.iota(iota8[:, :], [[1, 8]], base=0, channel_multiplier=0,
                   allow_small_or_imprecise_dtypes=True)
            g.memset(ones128[:, :], 1.0)
            g.memset(ones128f[:, :], 1.0)
            g.memset(ones1r[:, :], 1.0)
            g.memset(ones1f[:, :], 1.0)
            g.memset(sel0[:, :], 1.0)
            g.affine_select(out=sel0[:, :], in_=sel0[:, :], compare_op=ALU.is_ge,
                            fill=0.0, base=63, pattern=[[-1, 128]], channel_multiplier=0)
            g.memset(sel1[:, :], 1.0)
            g.affine_select(out=sel1[:, :], in_=sel1[:, :], compare_op=ALU.is_ge,
                            fill=0.0, base=-64, pattern=[[1, 128]], channel_multiplier=0)
            g.memset(cIs[:, :, :], 0.0)
            for j, c in enumerate([7.0, 15.0, 13.0]):
                for hh in range(2):
                    g.affine_select(out=cIs[hh * 64:(hh + 1) * 64, j, :],
                                    in_=cIs[hh * 64:(hh + 1) * 64, j, :],
                                    compare_op=ALU.not_equal, fill=c, base=0,
                                    pattern=[[-1, 64]], channel_multiplier=1)
            g.memset(Rrow[:, :], 0.0)
            g.memset(epsc[:, :], EPS)
            g.memset(zot[:, :], 0.0)
            g.nop().then_inc(s_gp.h, 1)
        P0G = s_gp.bump()
        k.emit("gpsimd", p0_g)

        p0_plan = []
        ev_ms = []
        for e in range(E):
            p0_plan.append(("4rows", e))
        for seg in range(1, 4):
            for e in range(E):
                p0_plan.append(("b1seg", e, seg))

        def p0_sync(sync):
            for it, item in enumerate(p0_plan):
                if it >= 1:
                    sync.wait_ge(s_ev.h, it)
                if item[0] == "4rows":
                    e = item[1]
                    sync.dma_start(rowbuf[0:1, 0:DM], cent_d[e:e + 1, :]).then_inc(s_x.h, 16)
                    sync.dma_start(rowbuf[0:1, DM:2 * DM], lng_d[e:e + 1, :]).then_inc(s_x.h, 16)
                    sync.dma_start(rowbuf[0:1, 2 * DM:3 * DM], lnb_d[e:e + 1, :]).then_inc(s_x.h, 16)
                    sync.dma_start(rowbuf[0:1, 3 * DM:4 * DM], b1_d[e:e + 1, 0:1024]).then_inc(s_x.h, 16)
                else:
                    e, seg = item[1], item[2]
                    sync.dma_start(rowbuf[0:1, 0:DM],
                                   b1_d[e:e + 1, seg * 1024:(seg + 1) * 1024]).then_inc(s_x.h, 16)
        nload = 0
        p0_loadcnt = []
        for item in p0_plan:
            nload += 4 if item[0] == "4rows" else 1
            p0_loadcnt.append(nload)
        s_x.bump(16 * nload)

        def p0_t(t):
            for it, item in enumerate(p0_plan):
                t.wait_ge(s_x.h, 16 * p0_loadcnt[it])
                if it >= 2:
                    t.wait_ge(s_ev.h, it - 1)
                pX = pA if it % 2 == 0 else pB
                nr = 4 if item[0] == "4rows" else 1
                for c in range(8):
                    for r in range(nr):
                        mm = t.matmul(pX[:, 4 * c + r:4 * c + r + 1],
                                      rowbuf[0:1, r * DM + c * 128:r * DM + (c + 1) * 128],
                                      ones1f[0:1, 0:1], start=True, stop=True)
                mm.then_inc(s_mm.h, 1)
        p0_mm0 = s_mm.v
        s_mm.bump(len(p0_plan))

        def p0_v(v):
            for it, item in enumerate(p0_plan):
                v.wait_ge(s_mm.h, p0_mm0 + it + 1)
                pX = pA if it % 2 == 0 else pB
                if item[0] == "4rows":
                    e = item[1]
                    for c in range(8):
                        v.tensor_copy(CT[:, c, e:e + 1], pX[:, 4 * c:4 * c + 1])
                        v.tensor_copy(GT[:, c, e:e + 1], pX[:, 4 * c + 1:4 * c + 2])
                        v.tensor_copy(BT[:, c, e:e + 1], pX[:, 4 * c + 2:4 * c + 3])
                        op = v.tensor_copy(B1T[:, c, e:e + 1], pX[:, 4 * c + 3:4 * c + 4])
                    op.then_inc(s_ev.h, 1)
                else:
                    e, seg = item[1], item[2]
                    for c in range(8):
                        op = v.tensor_copy(B1T[:, seg * 8 + c, e:e + 1],
                                           pX[:, 4 * c:4 * c + 1])
                    op.then_inc(s_ev.h, 1)
        s_ev.bump(len(p0_plan))

        k.emit("sync", p0_sync)
        k.emit("tensor", p0_t)
        k.emit("vector", p0_v)
        k.run_block()
        p0stk.close()

        if debug_stage == "p0":
            return nc

        wprep_pos = 0
        CTbf = None

        nbatch = 1 if debug_stage in ("p2ab", "p2abd", "p2c", "attn1") else B
        for b in range(nbatch):
            with ExitStack() as bstk:
                bsb = lambda n, s, d: bstk.enter_context(nc.sbuf_tensor(k.uname(n), s, d))
                QKT = bsb("QKT", [128, 16, N], BF16)
                Vrm = bsb("Vrm", [128, 16, DM], BF16)
                xnlT = bsb("xnlT", [128, 8, MLM], BF16)
                QlKlT = bsb("QlKlT", [128, 16, MLM], BF16)

                with ExitStack() as astk:
                    asb = lambda n, s, d: astk.enter_context(nc.sbuf_tensor(k.uname(n), s, d))
                    xnT = asb("xnT", [128, 8, N], BF16)
                    wqkvb = asb("wqkvb", [128, 8, 3 * DM], BF16)
                    xc = asb("xc", [128, 2, DM], F32)
                    xnrm = asb("xnrm", [128, 2, DM], BF16)
                    stat = asb("stat", [128, 16], F32)
                    mv2 = asb("mv2", [128, 2], F32)
                    rstd = asb("rstd", [128, 1], F32)
                    sjunk = asb("sjunk", [128, 4], F32)
                    lred = asb("lred", [128, MLM], F32)

                    k.new_block()
                    xrow0 = b * N

                    gq0 = s_gq.v
                    def a_gq(g, gq0=gq0):
                        for d in range(8):
                            g.dma_start(wqkvb[:, d, :],
                                        wqkv_d[d * 128:(d + 1) * 128, :]).then_inc(s_gq.h, 16)
                    s_gq.bump(16 * 8)
                    k.emit("gpsimd", a_gq)

                    ln_stats = {}
                    ln_apply = {}
                    tp_ms = {}
                    evV = {}
                    evS = {}
                    for i in range(16):
                        x_at = s_x.v + 16

                        def c_sync(sync, i=i, xrow0=xrow0):
                            if i >= 2:
                                sync.wait_ge(s_ln.h, ln_apply[i - 2])
                            sync.dma_start(
                                xc[:, i % 2, :],
                                x_d[xrow0 + 128 * i:xrow0 + 128 * (i + 1), :]).then_inc(s_x.h, 16)
                        s_x.bump(16)
                        k.emit("sync", c_sync)

                        e1_, e2_ = s_ln.bump(), s_ln.bump()
                        def c_stats(v, i=i, x_at=x_at, e1_=e1_, e2_=e2_):
                            v.wait_ge(s_x.h, x_at)
                            v.bn_stats(stat[:, 0:6], xc[:, i % 2, 0:512]).then_inc(s_ln.h, 1)
                            v.wait_ge(s_ln.h, e1_)
                            v.bn_stats(stat[:, 6:12], xc[:, i % 2, 512:1024]).then_inc(s_ln.h, 1)
                            v.wait_ge(s_ln.h, e2_)
                            v.bn_aggr(mv2[:, :], stat[:, 0:12]).then_inc(s_ln.h, 1)
                        ln_stats[i] = s_ln.bump()
                        k.emit("vector", c_stats)

                        def c_sqrt(sc, i=i):
                            sc.wait_ge(s_ln.h, ln_stats[i])
                            sc.activation(sjunk[:, 0:1], mv2[:, 1:2], AF.Sqrt,
                                          bias=epsc[:, 0:1], scale=1.0).then_inc(s_sq.h, 1)
                        sq_i = s_sq.bump()
                        k.emit("scalar", c_sqrt)

                        er_ = s_ln.bump()
                        def c_apply(v, i=i, sq_i=sq_i, er_=er_):
                            v.wait_ge(s_sq.h, sq_i)
                            if i >= 2:
                                v.wait_ge(s_tp.h, tp_ms[i - 2])
                            v.reciprocal(rstd[:, :], sjunk[:, 0:1]).then_inc(s_ln.h, 1)
                            v.wait_ge(s_ln.h, er_)
                            v.tensor_scalar(xnrm[:, i % 2, :], xc[:, i % 2, :],
                                            mv2[:, 0:1], rstd[:, :],
                                            op0=ALU.subtract, op1=ALU.mult
                                            ).then_inc(s_ln.h, 1)
                        ln_apply[i] = s_ln.bump()
                        k.emit("vector", c_apply)

                        def c_tr(t, i=i):
                            t.wait_ge(s_ln.h, ln_apply[i])
                            if i >= 2:
                                t.wait_ge(s_ev.h, evV[i - 2])
                            pT = pTrb if i % 2 == 0 else pTr2
                            for d in range(8):
                                mm = t.transpose(pT[:, d, :],
                                                 xnrm[:, i % 2, d * 128:(d + 1) * 128],
                                                 id128[:, :])
                            mm.then_inc(s_tp.h, 1)
                        tp_ms[i] = s_tp.bump()
                        k.emit("tensor", c_tr)

                        def c_evV(v, i=i):
                            v.wait_ge(s_tp.h, tp_ms[i])
                            pT = pTrb if i % 2 == 0 else pTr2
                            for d in range(8):
                                mm = v.tensor_copy(xnT[:, d, i * 128:(i + 1) * 128],
                                                   pT[:, d, :])
                            mm.then_inc(s_ev.h, 1)
                        evV[i] = s_ev.bump()
                        k.emit("vector", c_evV)
                        evS[i] = s_ev2.v

                    if b == 0:
                        def zfill(sync):
                            for cz in range(E * CAP // 128):
                                sync.dma_start(
                                    znbkt[128 * cz:128 * (cz + 1), :],
                                    zot[:, :]).then_inc(s_st.h, 16)

                        globals()
                        k.emit("sync", zfill)
                        ZFILL_AT = s_st.bump(16 * (E * CAP // 128))

                    lm_pieces = []
                    for d in range(8):
                        el_ = None
                        def c_lm(v, d=d, el_=None):
                            if d == 0:
                                v.wait_ge(s_ev.h, evV[15])
                            v.tensor_reduce(lred[:, :],
                                            xnT[:, d, :].rearrange("p (m s) -> p m s", m=MLM),
                                            axis=AX.X, op=ALU.add).then_inc(s_ln.h, 1)
                            v.wait_ge(s_ln.h, el_)
                            v.tensor_scalar(xnlT[:, d, :], lred[:, :], 1.0 / SEG, None,
                                            op0=ALU.mult).then_inc(s_ln.h, 1)
                        lm_pieces.append(c_lm)
                    lm_ms = []

                    qk_mm = {}
                    qk_ev = {}
                    wq_at = s_gq.v

                    for nh in range(4):
                        for cc in range(16):
                            gidx = nh * 16 + cc

                            def qk_t(t, nh=nh, cc=cc, gidx=gidx, wq_at=wq_at):
                                if gidx == 0:
                                    t.wait_ge(s_gq.h, wq_at)
                                if cc == 0:
                                    for j in range(4):
                                        t.wait_ge(s_ev.h, evV[4 * nh + j])
                                if gidx >= 2:
                                    t.wait_ge(s_ev.h, qk_ev[gidx - 2])
                                pQ = pA if gidx % 2 == 0 else pB
                                for d in range(8):
                                    mm = t.matmul(pQ[:, :],
                                                  wqkvb[:, d, cc * 128:(cc + 1) * 128],
                                                  xnT[:, d, nh * 512:(nh + 1) * 512],
                                                  start=(d == 0), stop=(d == 7))
                                mm.then_inc(s_mm.h, 1)
                            qk_mm[gidx] = s_mm.bump()
                            k.emit("tensor", qk_t)

                            def qk_e(v, nh=nh, cc=cc, gidx=gidx):
                                v.wait_ge(s_mm.h, qk_mm[gidx])
                                pQ = pA if gidx % 2 == 0 else pB
                                scale = 0.125 if cc < 8 else 1.0
                                v.tensor_scalar(QKT[:, cc, nh * 512:(nh + 1) * 512],
                                                pQ[:, :], scale, None,
                                                op0=ALU.mult).then_inc(s_ev.h, 1)
                            qk_ev[gidx] = s_ev.bump()
                            k.emit("vector", qk_e)
                            if 8 <= gidx < 16:
                                el_ = s_ln.bump()
                                fn = lm_pieces[gidx - 8]
                                k.emit("vector",
                                       lambda v, fn=fn, el_=el_: fn(v, el_=el_))
                                lm_ms.append(s_ln.bump())

                    v_mm = {}
                    v_ev = {}
                    for i in range(16):
                        for dh in range(2):
                            vc = i * 2 + dh

                            def v_t(t, i=i, dh=dh, vc=vc):
                                if vc >= 2:
                                    t.wait_ge(s_ev2.h, v_ev[vc - 2])
                                pV = pC if vc % 2 == 0 else pD
                                for d in range(8):
                                    mm = t.matmul(pV[:, :], xnT[:, d, i * 128:(i + 1) * 128],
                                                  wqkvb[:, d, 2048 + dh * 512:2048 + (dh + 1) * 512],
                                                  start=(d == 0), stop=(d == 7))
                                mm.then_inc(s_mm.h, 1)
                            v_mm[vc] = s_mm.bump()
                            k.emit("tensor", v_t)

                            def v_e(sc, i=i, dh=dh, vc=vc):
                                sc.wait_ge(s_mm.h, v_mm[vc])
                                pV = pC if vc % 2 == 0 else pD
                                sc.copy(Vrm[:, i, dh * 512:(dh + 1) * 512],
                                        pV[:, :]).then_inc(s_ev2.h, 1)
                            v_ev[vc] = s_ev2.bump()
                            k.emit("scalar", v_e)

                    ql_mm = {}
                    ql_ev = {}
                    for cc in range(16):
                        def ql_t(t, cc=cc):
                            if cc == 0:
                                t.wait_ge(s_ln.h, lm_ms[-1])
                            if cc >= 2:
                                t.wait_ge(s_ev.h, ql_ev[cc - 2])
                            pQ = pE_ if cc % 2 == 0 else pF_
                            for d in range(8):
                                mm = t.matmul(pQ[:, 0:MLM],
                                              wqkvb[:, d, cc * 128:(cc + 1) * 128],
                                              xnlT[:, d, :], start=(d == 0), stop=(d == 7))
                            mm.then_inc(s_mm.h, 1)
                        ql_mm[cc] = s_mm.bump()
                        k.emit("tensor", ql_t)

                        def ql_e(v, cc=cc):
                            v.wait_ge(s_mm.h, ql_mm[cc])
                            pQ = pE_ if cc % 2 == 0 else pF_
                            scale = 0.125 if cc < 8 else 1.0
                            v.tensor_scalar(QlKlT[:, cc, :], pQ[:, 0:MLM], scale, None,
                                            op0=ALU.mult).then_inc(s_ev.h, 1)
                        ql_ev[cc] = s_ev.bump()
                        k.emit("vector", ql_e)

                    nun = 4
                    k.emit("gpsimd", emit_wprep(wprep_units[wprep_pos:wprep_pos + nun]))
                    wprep_pos += nun

                    k.run_block()

                if debug_stage == "p2ab":
                    return nc
                if debug_stage == "p2abd":
                    k.new_block()
                    def dump(sync):
                        sync.dma_start(dbg_d[:, 0, :], xnT[:, 0, :]).then_inc(s_st.h, 16)
                        sync.dma_start(dbg_d[:, 1, :], QKT[:, 0, :]).then_inc(s_st.h, 16)
                        sync.dma_start(dbg_d[:, 2, :], QKT[:, 8, :]).then_inc(s_st.h, 16)
                        sync.dma_start(dbg_d[:, 3, 0:1024], Vrm[:, 0, :]).then_inc(s_st.h, 16)
                        sync.dma_start(dbg_d[:, 3, 1024:1088], QlKlT[:, 0, :]).then_inc(s_st.h, 16)
                        sync.dma_start(dbg_d[:, 3, 1088:1152], xnlT[:, 0, :]).then_inc(s_st.h, 16)
                        sync.wait_ge(s_st.h, s_st_target)
                    s_st_target = s_st.bump(16 * 6)
                    k.emit("sync", dump)
                    k.run_block()
                    return nc

                with ExitStack() as pstk:
                    psb = lambda n, s, d: pstk.enter_context(nc.sbuf_tensor(k.uname(n), s, d))
                    SVT = psb("SVT", [128, 8, N], BF16)
                    E1T = psb("E1T", [128, 2, N], BF16)
                    E3T = psb("E3T", [128, 2, 16, 128], BF16)
                    e1f = psb("e1f", [128, 2, 512], F32)
                    e1n = psb("e1n", [128, 2, 512], BF16)
                    k2e = psb("k2e", [128, 64], F32)
                    k2s = psb("k2s", [128, 1], F32)
                    k2n = psb("k2n", [128, 64], BF16)
                    k2T_st = psb("k2T_st", [128, 64], BF16)
                    k2T_bd = psb("k2T_bd", [128, 128], BF16)
                    Kl_bd = psb("Kl_bd", [128, 128], BF16)
                    Ql_bd = psb("Ql_bd", [128, 128], BF16)
                    V_st = psb("V_st", [128, 64], BF16)
                    V_bd = psb("V_bd", [128, 128], BF16)
                    VT_bd = psb("VT_bd", [128, 128], BF16)
                    KVT_bd = psb("KVT_bd", [128, 128], BF16)
                    Z_bd = psb("Z_bd", [128, 128], BF16)
                    Zst = psb("Zst", [128, 64], BF16)
                    T_st = psb("T_st", [128, 64], BF16)
                    AVn = psb("AVn", [128, 64], BF16)
                    rdr = psb("rdr", [1, 2], F32)
                    scol = psb("scol", [128, 1], F32)
                    rds = psb("rds", [128, 1], F32)
                    s1r = psb("s1r", [128, 8], F32)
                    rs3 = psb("rs3", [128, 1], F32)
                    nmax = psb("nmax", [128, 1], F32)

                    k.new_block()

                    def pz(g):
                        for tile in (k2T_bd, Kl_bd, Ql_bd, V_bd, VT_bd, KVT_bd, Z_bd):
                            g.memset(tile[:, :], 0.0)
                        g.nop().then_inc(s_gp.h, 1)
                    bd_zero = s_gp.bump()
                    k.emit("gpsimd", pz)

                    prev_pf_free = None
                    prev_sv = {}
                    prev_av = {}
                    prev_tr2_ok = None
                    prev_kg_last = None
                    prev_k3e = None
                    prev_k1T = None
                    prev_svev = None
                    avn_prev = {}

                    sub = debug_stage if isinstance(debug_stage, str) else ""
                    npairs = 1 if sub.startswith("p2c") and sub != "p2c" else PAIRS
                    for pr in range(npairs):
                        par = pr % 2
                        def c0(g, pr=pr, gate=prev_kg_last):
                            if pr == 0:
                                g.wait_ge(s_gp.h, bd_zero)
                            if gate is not None:
                                g.wait_ge(s_mm.h, gate)
                            g.tensor_copy(Kl_bd[0:64, 0:64], QlKlT[0:64, 8 + pr, :])
                            g.tensor_copy(Kl_bd[64:128, 64:128], QlKlT[64:128, 8 + pr, :])
                            g.tensor_copy(Ql_bd[0:64, 0:64], QlKlT[0:64, pr, :])
                            g.tensor_copy(Ql_bd[64:128, 64:128],
                                          QlKlT[64:128, pr, :]).then_inc(s_gp.h, 1)
                        klbd_ok = s_gp.bump()
                        k.emit("gpsimd", c0)
                        if sub == "p2c_0":
                            break

                        def c1(t, pr=pr, gate=prev_pf_free):
                            if gate is not None:
                                t.wait_ge(s_nv.h, gate)
                            for hh in range(2):
                                mm = t.matmul(pF_[hh * 64:(hh + 1) * 64, 0:64],
                                              QlKlT[hh * 64:(hh + 1) * 64, pr, :],
                                              QlKlT[hh * 64:(hh + 1) * 64, 8 + pr, :],
                                              start=True, stop=True)
                            mm.then_inc(s_mm.h, 1)
                        l2_ms = s_mm.bump()
                        k.emit("tensor", c1)
                        if sub == "p2c_1":
                            break

                        def c2(v, l2_ms=l2_ms):
                            v.wait_ge(s_mm.h, l2_ms)
                            v.reduce_max(nmax[:, :], pF_[:, 0:64], axis=AX.X,
                                         negate=True).then_inc(s_nv.h, 1)
                        nmax_ms = s_nv.bump()
                        k.emit("vector", c2)
                        if sub == "p2c_2":
                            break

                        def c3(sc, nmax_ms=nmax_ms):
                            sc.wait_ge(s_nv.h, nmax_ms)
                            sc.activation(k2e[:, :], pF_[:, 0:64], AF.Exp,
                                          bias=nmax[:, :], scale=1.0,
                                          accum_out=k2s[:, :]).then_inc(s_ev2.h, 1)
                        k2e_ms = s_ev2.bump()
                        k.emit("scalar", c3)
                        if sub == "p2c_3":
                            break

                        ec4 = s_nv.bump()
                        def c4(v, k2e_ms=k2e_ms, gate=prev_kg_last, ec4=ec4):
                            v.wait_ge(s_ev2.h, k2e_ms)
                            if gate is not None:
                                v.wait_ge(s_mm.h, gate)
                            v.reciprocal(rds[:, :], k2s[:, :]).then_inc(s_nv.h, 1)
                            v.wait_ge(s_nv.h, ec4)
                            v.tensor_scalar(k2n[:, :], k2e[:, :], rds[:, :], None,
                                            op0=ALU.mult).then_inc(s_nv.h, 1)
                        k2n_ms = s_nv.bump()
                        k.emit("vector", c4)
                        if sub == "p2c_4":
                            break

                        def c5(t, k2n_ms=k2n_ms, gate=prev_tr2_ok):
                            t.wait_ge(s_nv.h, k2n_ms)
                            if gate is not None:
                                t.wait_ge(s_nv.h, gate)
                            t.transpose(pTr2[0:64, 0, 0:64], k2n[0:64, :], id64s[0:64, :])
                            mm = t.transpose(pTr2[64:128, 0, 64:128], k2n[64:128, :],
                                             id64s[64:128, :])
                            mm.then_inc(s_nt.h, 1)
                        k2t_ps = s_nt.bump()
                        k.emit("tensor", c5)
                        if sub == "p2c_5":
                            break

                        def c6(v, k2t_ps=k2t_ps):
                            v.wait_ge(s_nt.h, k2t_ps)
                            v.tensor_copy(k2T_st[0:64, :], pTr2[0:64, 0, 0:64])
                            v.tensor_copy(k2T_st[64:128, :], pTr2[64:128, 0, 64:128])
                            v.tensor_copy(k2T_bd[0:64, 0:64], pTr2[0:64, 0, 0:64])
                            v.tensor_copy(k2T_bd[64:128, 64:128],
                                          pTr2[64:128, 0, 64:128]).then_inc(s_nv.h, 1)
                        k2t_ok = s_nv.bump()
                        k.emit("vector", c6)
                        if sub == "p2c_6":
                            break

                        def c7a(v, k2t_ok=k2t_ok):
                            v.reduce_sum(scol[:, :], k2T_st[:, :], axis=AX.X
                                         ).then_inc(s_nv.h, 1)
                        scol_ms = s_nv.bump()
                        k.emit("vector", c7a)

                        def c7b(t, scol_ms=scol_ms):
                            t.wait_ge(s_nv.h, scol_ms)
                            t.transpose(pF_[0:1, 256:384], scol[:, :],
                                        id128f[:, :]).then_inc(s_mm.h, 1)
                        den_ms = s_mm.bump()
                        k.emit("tensor", c7b)
                        if sub == "p2c_7":
                            break

                        ec8 = s_nv.bump()
                        def c8(v, den_ms=den_ms, ec8=ec8):
                            v.wait_ge(s_mm.h, den_ms)
                            v.reduce_max(rdr[0:1, 0:1], pF_[0:1, 256:320], axis=AX.X)
                            v.reduce_max(rdr[0:1, 1:2], pF_[0:1, 320:384],
                                         axis=AX.X).then_inc(s_nv.h, 1)
                            v.wait_ge(s_nv.h, ec8)
                            v.reciprocal(rdr[0:1, :], rdr[0:1, :]).then_inc(s_nv.h, 1)
                        rd_ms = s_nv.bump()
                        k.emit("vector", c8)
                        if sub == "p2c_8":
                            break

                        def c9(t, rd_ms=rd_ms):
                            t.wait_ge(s_nv.h, rd_ms)
                            t.matmul(pF_[:, 192:193], sel0[0:1, :], rdr[0:1, 0:1],
                                     start=True, stop=False)
                            t.matmul(pF_[:, 192:193], sel1[0:1, :], rdr[0:1, 1:2],
                                     start=False, stop=True).then_inc(s_mm.h, 1)
                        rdb_ms = s_mm.bump()
                        k.emit("tensor", c9)
                        if sub == "p2c_9":
                            break

                        ec10 = s_nv.bump()
                        ec10b = s_nv.bump()
                        def c10(v, rdb_ms=rdb_ms, k2t_ok=k2t_ok, ec10=ec10, ec10b=ec10b):
                            v.wait_ge(s_mm.h, rdb_ms)
                            v.wait_ge(s_nv.h, k2t_ok)
                            v.tensor_copy(rds[:, :], pF_[:, 192:193]).then_inc(s_nv.h, 1)
                            v.wait_ge(s_nv.h, ec10)
                            v.tensor_scalar(V_st[:, :], k2T_st[:, :], rds[:, :], None,
                                            op0=ALU.mult).then_inc(s_nv.h, 1)
                            v.wait_ge(s_nv.h, ec10b)
                            v.tensor_copy(V_bd[0:64, 0:64], V_st[0:64, :])
                            v.tensor_copy(V_bd[64:128, 64:128],
                                          V_st[64:128, :]).then_inc(s_nv.h, 1)
                        v_ok = s_nv.bump()
                        k.emit("vector", c10)
                        if sub == "p2cS":
                            break

                        newton = []
                        vok = {0: v_ok}
                        for it in range(6):
                            slot = 1 + (it % 2)

                            def n_b(t, it=it, slot=slot, vok=vok):
                                t.wait_ge(s_nv.h, vok[it])
                                t.transpose(pTr2[0:64, slot, 0:64], V_st[0:64, :],
                                            id64s[0:64, :])
                                t.transpose(pTr2[64:128, slot, 64:128], V_st[64:128, :],
                                            id64s[64:128, :])
                                t.matmul(pF_[:, 256:320], k2T_bd[:, :], V_st[:, :],
                                         start=True, stop=True)
                                t.matmul(pF_[:, 320:384], V_bd[:, :], k2T_st[:, :],
                                         start=True, stop=True).then_inc(s_nt.h, 1)
                            kv_ms = s_nt.bump()

                            def n_c(v, it=it, slot=slot, kv_ms=kv_ms):
                                v.wait_ge(s_nt.h, kv_ms)
                                v.tensor_tensor(T_st[:, :], cIs[:, 0, :], pF_[:, 256:320],
                                                op=ALU.subtract)
                                v.tensor_copy(KVT_bd[0:64, 0:64], pF_[0:64, 320:384])
                                v.tensor_copy(KVT_bd[64:128, 64:128], pF_[64:128, 320:384])
                                v.tensor_copy(VT_bd[0:64, 0:64], pTr2[0:64, slot, 0:64])
                                v.tensor_copy(VT_bd[64:128, 64:128],
                                              pTr2[64:128, slot, 64:128]).then_inc(s_nv.h, 1)
                            t1_ms = s_nv.bump()

                            def n_d(t, t1_ms=t1_ms):
                                t.wait_ge(s_nv.h, t1_ms)
                                t.matmul(pF_[:, 384:448], KVT_bd[:, :], T_st[:, :],
                                         start=True, stop=True).then_inc(s_nt.h, 1)
                            u1_ms = s_nt.bump()

                            def n_e(v, u1_ms=u1_ms):
                                v.wait_ge(s_nt.h, u1_ms)
                                v.tensor_tensor(T_st[:, :], cIs[:, 1, :], pF_[:, 384:448],
                                                op=ALU.subtract).then_inc(s_nv.h, 1)
                            t2_ms = s_nv.bump()

                            def n_f(t, t2_ms=t2_ms):
                                t.wait_ge(s_nv.h, t2_ms)
                                t.matmul(pF_[:, 448:512], KVT_bd[:, :], T_st[:, :],
                                         start=True, stop=True).then_inc(s_nt.h, 1)
                            u2_ms = s_nt.bump()

                            def n_g(v, u2_ms=u2_ms):
                                v.wait_ge(s_nt.h, u2_ms)
                                v.tensor_tensor(T_st[:, :], cIs[:, 2, :], pF_[:, 448:512],
                                                op=ALU.subtract).then_inc(s_nv.h, 1)
                            t3_ms = s_nv.bump()

                            def n_h(t, t3_ms=t3_ms):
                                t.wait_ge(s_nv.h, t3_ms)
                                t.matmul(pF_[:, 256:320], VT_bd[:, :], T_st[:, :],
                                         start=True, stop=True).then_inc(s_nt.h, 1)
                            vn_ms = s_nt.bump()

                            eni = s_nv.bump() if it < 5 else None
                            def n_i(v, vn_ms=vn_ms, it=it, eni=eni):
                                v.wait_ge(s_nt.h, vn_ms)
                                if it < 5:
                                    v.tensor_scalar(V_st[:, :], pF_[:, 256:320], 0.25, None,
                                                    op0=ALU.mult).then_inc(s_nv.h, 1)
                                    v.wait_ge(s_nv.h, eni)
                                    v.tensor_copy(V_bd[0:64, 0:64], V_st[0:64, :])
                                    v.tensor_copy(V_bd[64:128, 64:128],
                                                  V_st[64:128, :]).then_inc(s_nv.h, 1)
                                else:
                                    v.tensor_scalar(V_st[:, :], pF_[:, 256:320], 0.25,
                                                    None, op0=ALU.mult).then_inc(s_nv.h, 1)
                            vok[it + 1] = s_nv.bump()

                            newton += [("tensor", n_b), ("vector", n_c),
                                       ("tensor", n_d), ("vector", n_e), ("tensor", n_f),
                                       ("vector", n_g), ("tensor", n_h), ("vector", n_i)]

                        kpath = []
                        k1g = {}
                        k1e_ms = {}
                        k1n_ms = {}
                        k1t_ms = {}
                        k1T_ms = {}
                        k3g = {}
                        k3e_ms = {}
                        for g8 in range(8):
                            isk1 = g8 < 4
                            gg = g8 % 4

                            def kg_t(t, pr=pr, g8=g8, isk1=isk1, gg=gg,
                                     klbd_ok=klbd_ok, pk3e=prev_k3e,
                                     k1e_ms=k1e_ms, k3e_ms=k3e_ms):
                                if g8 == 0:
                                    t.wait_ge(s_gp.h, klbd_ok)
                                if g8 >= 2:
                                    t.wait_ge(s_ev2.h,
                                              (k1e_ms if g8 - 2 < 4 else k3e_ms)[
                                                  (g8 - 2) % 4])
                                elif pk3e is not None:
                                    t.wait_ge(s_ev2.h, pk3e[2 + g8])
                                pX = pA if g8 % 2 == 0 else pB
                                cc = (pr if isk1 else 8 + pr)
                                bd = Kl_bd if isk1 else Ql_bd
                                for j in range(4):
                                    nch = 4 * gg + j
                                    mm = t.matmul(pX[:, j * 128:(j + 1) * 128],
                                                  QKT[:, cc, nch * 128:(nch + 1) * 128],
                                                  bd[:, :], start=True, stop=True)
                                mm.then_inc(s_mm.h, 1)
                            (k1g if isk1 else k3g)[gg] = s_mm.bump()
                            kpath.append(("tensor", kg_t))

                            if isk1:
                                def k1_s(sc, pr=pr, gg=gg, par=par, k1g=k1g, k1n_ms=k1n_ms):
                                    sc.wait_ge(s_mm.h, k1g[gg])
                                    if gg >= 2:
                                        sc.wait_ge(s_ev.h, k1n_ms[gg - 2])
                                    pX = pA if gg % 2 == 0 else pB
                                    sc.activation(e1f[:, gg % 2, :], pX[:, :],
                                                  AF.Exp).then_inc(s_ev2.h, 1)
                                k1e_ms[gg] = s_ev2.bump()
                                kpath.append(("scalar", k1_s))

                                ek1, ek2 = s_ev.bump(), s_ev.bump()
                                def k1_v(v, gg=gg, par=par, k1e_ms=k1e_ms,
                                         k1t_ms=k1t_ms, ek1=ek1, ek2=ek2):
                                    v.wait_ge(s_ev2.h, k1e_ms[gg])
                                    if gg >= 2:
                                        v.wait_ge(s_tp.h, k1t_ms[gg - 2])
                                    v.tensor_reduce(s1r[:, :],
                                                    e1f[:, gg % 2, :].rearrange(
                                                        "p (a m) -> p a m", a=8),
                                                    axis=AX.X, op=ALU.add).then_inc(s_ev.h, 1)
                                    v.wait_ge(s_ev.h, ek1)
                                    v.reciprocal(s1r[:, :], s1r[:, :]).then_inc(s_ev.h, 1)
                                    v.wait_ge(s_ev.h, ek2)
                                    for j in range(8):
                                        op = v.tensor_scalar(
                                            e1n[:, gg % 2, j * 64:(j + 1) * 64],
                                            e1f[:, gg % 2, j * 64:(j + 1) * 64],
                                            s1r[:, j:j + 1], None, op0=ALU.mult)
                                    op.then_inc(s_ev.h, 1)
                                k1n_ms[gg] = s_ev.bump()
                                kpath.append(("vector", k1_v))

                                def k1_tr(t, gg=gg, par=par, pk1T=prev_k1T, k1n_ms=k1n_ms, k1T_ms=k1T_ms):
                                    t.wait_ge(s_ev.h, k1n_ms[gg])
                                    if gg >= 1:
                                        t.wait_ge(s_ev2.h, k1T_ms[gg - 1])
                                    elif pk1T is not None:
                                        t.wait_ge(s_ev2.h, pk1T[3])
                                    for j in range(4):
                                        mm = t.transpose(pTrb[:, 4 * (gg % 2) + j, :],
                                                         e1n[:, gg % 2, j * 128:(j + 1) * 128],
                                                         id128[:, :])
                                    mm.then_inc(s_tp.h, 1)
                                k1t_ms[gg] = s_tp.bump()
                                kpath.append(("tensor", k1_tr))

                                def k1_e(sc, gg=gg, par=par, pr=pr, k1t_ms=k1t_ms):
                                    sc.wait_ge(s_tp.h, k1t_ms[gg])
                                    if gg == 0 and pr >= 2 and prev_sv.get(pr - 2):
                                        sc.wait_ge(s_mm.h, prev_sv[pr - 2])
                                    for j in range(4):
                                        nch = 4 * gg + j
                                        mm = sc.copy(E1T[:, par, nch * 128:(nch + 1) * 128],
                                                     pTrb[:, 4 * (gg % 2) + j, :])
                                    mm.then_inc(s_ev2.h, 1)
                                k1T_ms[gg] = s_ev2.bump()
                                kpath.append(("scalar", k1_e))
                            else:
                                def k3_s(sc, gg=gg, par=par, pr=pr, k3g=k3g):
                                    sc.wait_ge(s_mm.h, k3g[gg])
                                    if gg == 0 and pr >= 2 and prev_av.get(pr - 2):
                                        sc.wait_ge(s_mm.h, prev_av[pr - 2])
                                    pX = pA if gg % 2 == 0 else pB
                                    sc.activation(
                                        E3T[:, par, 4 * gg:4 * gg + 4, :].rearrange(
                                            "p a m -> p (a m)"),
                                        pX[:, :], AF.Exp).then_inc(s_ev2.h, 1)
                                k3e_ms[gg] = s_ev2.bump()
                                kpath.append(("scalar", k3_s))

                        def s3av_t(t, pr=pr, par=par, psvev=prev_svev, k3e_ms=k3e_ms):
                            t.wait_ge(s_ev2.h, k3e_ms[3])
                            if pr >= 1:
                                t.wait_ge(s_nv.h, avn_prev[pr - 1])
                            if psvev is not None:
                                t.wait_ge(s_ev.h, psvev[3])
                            for nch in range(16):
                                t.matmul(pC[:, 0:1], E3T[:, par, nch, :], ones128[:, :],
                                         start=(nch == 0), stop=(nch == 15))
                                mm = t.matmul(pD[:, 0:128], E3T[:, par, nch, :],
                                              Vrm[:, nch, pr * 128:(pr + 1) * 128],
                                              start=(nch == 0), stop=(nch == 15))
                            mm.then_inc(s_mm.h, 1)
                        av_ms = s_mm.bump()
                        prev_av[pr] = av_ms
                        kpath.append(("tensor", s3av_t))

                        eav = s_nv.bump()
                        def av_v(v, av_ms=av_ms, eav=eav):
                            v.wait_ge(s_mm.h, av_ms)
                            v.reciprocal(rs3[:, :], pC[:, 0:1]).then_inc(s_nv.h, 1)
                            v.wait_ge(s_nv.h, eav)
                            v.tensor_scalar(AVn[0:64, :], pD[0:64, 0:64],
                                            rs3[0:64, :], None, op0=ALU.mult)
                            v.tensor_scalar(AVn[64:128, :], pD[64:128, 64:128],
                                            rs3[64:128, :], None,
                                            op0=ALU.mult).then_inc(s_nv.h, 1)
                        avn_ms = s_nv.bump()
                        avn_prev[pr] = avn_ms
                        kpath.append(("vector", av_v))

                        if sub == "p2cN":
                            for eng, fn in newton:
                                k.emit(eng, fn)
                            break
                        if sub == "p2cK":
                            for eng, fn in newton + kpath:
                                k.emit(eng, fn)
                            break
                        merged = []
                        ni, ki = 0, 0
                        while ni < len(newton) or ki < len(kpath):
                            for _ in range(3):
                                if ni < len(newton):
                                    merged.append(newton[ni]); ni += 1
                            if ki < len(kpath):
                                merged.append(kpath[ki]); ki += 1
                        for eng, fn in merged:
                            k.emit(eng, fn)

                        def z_t(t, vok6=vok[6], avn_ms=avn_ms):
                            t.wait_ge(s_nv.h, vok6)
                            t.transpose(pTr2[0:64, 3, 0:64], V_st[0:64, :], id64s[0:64, :])
                            mm = t.transpose(pTr2[64:128, 3, 64:128], V_st[64:128, :],
                                             id64s[64:128, :])
                            mm.then_inc(s_nt.h, 1)
                        v6t_ps = s_nt.bump()
                        k.emit("tensor", z_t)

                        def z_v(v, v6t_ps=v6t_ps):
                            v.wait_ge(s_nt.h, v6t_ps)
                            v.tensor_copy(VT_bd[0:64, 0:64], pTr2[0:64, 3, 0:64])
                            v.tensor_copy(VT_bd[64:128, 64:128],
                                          pTr2[64:128, 3, 64:128]).then_inc(s_nv.h, 1)
                        v6t_ok = s_nv.bump()
                        k.emit("vector", z_v)

                        def z_mm(t, v6t_ok=v6t_ok, avn_ms=avn_ms):
                            t.wait_ge(s_nv.h, v6t_ok)
                            t.wait_ge(s_nv.h, avn_ms)
                            t.matmul(pF_[:, 384:448], VT_bd[:, :], AVn[:, :],
                                     start=True, stop=True).then_inc(s_nt.h, 1)
                        zmm_ms = s_nt.bump()
                        k.emit("tensor", z_mm)

                        def z_ev(v, zmm_ms=zmm_ms):
                            v.wait_ge(s_nt.h, zmm_ms)
                            v.tensor_copy(Zst[:, :], pF_[:, 384:448]).then_inc(s_nv.h, 1)
                        zst_ms = s_nv.bump()
                        k.emit("vector", z_ev)

                        def z_bd(g, zst_ms=zst_ms):
                            g.wait_ge(s_nv.h, zst_ms)
                            g.tensor_copy(Z_bd[0:64, 0:64], Zst[0:64, :])
                            g.tensor_copy(Z_bd[64:128, 64:128],
                                          Zst[64:128, :]).then_inc(s_gp.h, 1)
                        zbd_ms = s_gp.bump()
                        k.emit("gpsimd", z_bd)
                        prev_pf_free = zst_ms
                        prev_tr2_ok = vok[6]

                        sv_ev_ms = {}
                        for nh2 in range(4):
                            def sv_t(t, pr=pr, par=par, nh2=nh2, zbd_ms=zbd_ms, k1T_ms=k1T_ms, sv_ev_ms=sv_ev_ms):
                                if nh2 == 0:
                                    t.wait_ge(s_gp.h, zbd_ms)
                                    t.wait_ge(s_ev2.h, k1T_ms[3])
                                if nh2 >= 2:
                                    t.wait_ge(s_ev.h, sv_ev_ms[nh2 - 2])
                                pX = pE_ if nh2 % 2 == 0 else pC
                                mm = t.matmul(pX[:, :], Z_bd[:, :],
                                              E1T[:, par, nh2 * 512:(nh2 + 1) * 512],
                                              start=True, stop=True)
                                mm.then_inc(s_mm.h, 1)
                            svmm = s_mm.bump()
                            k.emit("tensor", sv_t)

                            def sv_e(v, pr=pr, nh2=nh2, svmm=svmm):
                                v.wait_ge(s_mm.h, svmm)
                                pX = pE_ if nh2 % 2 == 0 else pC
                                v.tensor_copy(SVT[:, pr, nh2 * 512:(nh2 + 1) * 512],
                                              pX[:, :]).then_inc(s_ev.h, 1)
                            sv_ev_ms[nh2] = s_ev.bump()
                            k.emit("vector", sv_e)
                        prev_sv[pr] = svmm
                        prev_kg_last = k3g[3]
                        prev_k3e = dict(k3e_ms)
                        prev_k1T = dict(k1T_ms)
                        prev_svev = dict(sv_ev_ms)

                    sv_done_v = s_ev.v
                    k.run_block()

                    if sub.startswith("p2c"):
                        return nc

                    with ExitStack() as estk:
                        esb = lambda n, s, d: estk.enter_context(nc.sbuf_tensor(k.uname(n), s, d))
                        wprojb = esb("wprojb", [128, 8, DM], BF16)
                        xc2 = esb("xc2", [128, 2, DM], F32)
                        x1t = esb("x1t", [128, 2, DM], F32)
                        stat2 = esb("stat2", [128, 16], F32)
                        mv3 = esb("mv3", [128, 2], F32)
                        rstd2 = esb("rstd2", [128, 1], F32)
                        sjunk2 = esb("sjunk2", [128, 4], F32)

                        k.new_block()
                        gq0 = s_gq.v

                        def e_gq(g):
                            for d in range(8):
                                g.dma_start(wprojb[:, d, :],
                                            wproj_d[d * 128:(d + 1) * 128, :]).then_inc(s_gq.h, 16)
                        s_gq.bump(16 * 8)
                        k.emit("gpsimd", e_gq)

                        pj_mm = {}
                        pj_ap = {}
                        st_ms = {}
                        for i in range(16):
                            def e_sync(sync, i=i, b=b):
                                if i >= 2:
                                    sync.wait_ge(s_ln.h, pj_ap[i - 2])
                                sync.dma_start(
                                    xc2[:, i % 2, :],
                                    x_d[b * N + 128 * i:b * N + 128 * (i + 1), :]
                                ).then_inc(s_x.h, 16)
                                if i >= 1:
                                    sync.wait_ge(s_ln.h, pj_ap[i - 1])
                                    sync.dma_start(
                                        x1s[b * N + 128 * (i - 1):b * N + 128 * i, :],
                                        x1t[:, (i - 1) % 2, :]).then_inc(s_st.h, 16)
                            x_at = s_x.bump(16)
                            if i >= 1:
                                s_st.bump(16)
                            k.emit("sync", e_sync)

                            def e_t(t, i=i, gq0=gq0):
                                if i == 0:
                                    t.wait_ge(s_gq.h, gq0 + 128)
                                for dh in range(2):
                                    gI = 2 * i + dh
                                    if gI >= 2:
                                        t.wait_ge(s_ln.h, pj_ap.get(i - 1, 0))
                                    pX = pA if gI % 2 == 0 else pB
                                    for pr2 in range(8):
                                        mm = t.matmul(pX[:, :],
                                                      SVT[:, pr2, i * 128:(i + 1) * 128],
                                                      wprojb[:, pr2, dh * 512:(dh + 1) * 512],
                                                      start=(pr2 == 0), stop=(pr2 == 7))
                                    mm.then_inc(s_mm.h, 1)
                            pj_mm[i] = (s_mm.bump(), s_mm.bump())
                            k.emit("tensor", e_t)

                            ee1, ee2 = s_ln.bump(), s_ln.bump()
                            def e_v1(v, i=i, x_at=x_at, ee1=ee1, ee2=ee2):
                                v.wait_ge(s_mm.h, pj_mm[i][1])
                                v.bn_stats(stat2[:, 0:6], pA[:, :]).then_inc(s_ln.h, 1)
                                v.wait_ge(s_ln.h, ee1)
                                v.bn_stats(stat2[:, 6:12], pB[:, :]).then_inc(s_ln.h, 1)
                                v.wait_ge(s_ln.h, ee2)
                                v.bn_aggr(mv3[:, :], stat2[:, 0:12]).then_inc(s_ln.h, 1)
                            pst = s_ln.bump()
                            k.emit("vector", e_v1)

                            def e_sq(sc, i=i, pst=pst):
                                sc.wait_ge(s_ln.h, pst)
                                sc.activation(sjunk2[:, 0:1], mv3[:, 1:2], AF.Sqrt,
                                              bias=epsc[:, 0:1], scale=1.0).then_inc(s_sq.h, 1)
                            sqm = s_sq.bump()
                            k.emit("scalar", e_sq)

                            ee3 = s_ln.bump()
                            def e_v2(v, i=i, sqm=sqm, x_at=x_at, ee3=ee3):
                                v.wait_ge(s_sq.h, sqm)
                                v.wait_ge(s_x.h, x_at)
                                v.reciprocal(rstd2[:, :], sjunk2[:, 0:1]).then_inc(s_ln.h, 1)
                                v.wait_ge(s_ln.h, ee3)
                                for dh in range(2):
                                    pX = pA if dh == 0 else pB
                                    v.tensor_scalar(x1t[:, i % 2, dh * 512:(dh + 1) * 512],
                                                    pX[:, :], mv3[:, 0:1], rstd2[:, :],
                                                    op0=ALU.subtract, op1=ALU.mult)
                                op = v.tensor_tensor(x1t[:, i % 2, :], x1t[:, i % 2, :],
                                                     xc2[:, i % 2, :], op=ALU.add)
                                op.then_inc(s_ln.h, 1)
                            pj_ap[i] = s_ln.bump()
                            k.emit("vector", e_v2)

                        def e_fin(sync, b=b):
                            sync.wait_ge(s_ln.h, pj_ap[15])
                            sync.dma_start(x1s[b * N + 128 * 15:b * N + 128 * 16, :],
                                           x1t[:, 15 % 2, :]).then_inc(s_st.h, 16)
                        s_st.bump(16)
                        k.emit("sync", e_fin)
                        k.run_block()

        if debug_stage in ("attn", "attn1"):
            k.new_block()
            st_target = s_st.v
            def fin(sync):
                sync.wait_ge(s_st.h, st_target)
            k.emit("sync", fin)
            k.run_block()
            return nc

        with ExitStack() as rstk:
            rsb = lambda n, s, d: rstk.enter_context(nc.sbuf_tensor(k.uname(n), s, d))
            xc1 = rsb("xc1", [128, 2, DM], F32)
            zf = rsb("zf", [128, 2, DM], F32)
            zbb = rsb("zbb", [128, 2, DM], BF16)
            znbt = rsb("znbt", [128, 2, DM], BF16)
            zT32 = rsb("zT32", [128, 2, 8, 128], F32)
            stat3 = rsb("stat3", [128, 16], F32)
            mvA = rsb("mvA", [128, 2], F32)
            mvB = rsb("mvB", [128, 2], F32)
            rstdA = rsb("rstdA", [128, 1], F32)
            rstdB = rsb("rstdB", [128, 1], F32)
            sjk = rsb("sjk", [128, 4], F32)
            affs = rsb("affs", [128, 8], F32)
            max8 = rsb("max8", [128, 8], F32)
            idx8 = rsb("idx8", [128, 8], U32)
            esf = rsb("esf", [128, 1], F32)
            ohf = rsb("ohf", [128, 8], F32)
            ohsel = rsb("ohsel", [128, 8], F32)
            rkred = rsb("rkred", [128, 1], F32)
            of32 = rsb("of32", [128, 1], F32)
            ovr = rsb("ovr", [128, 1], F32)
            ogf = rsb("ogf", [128, 1], F32)
            cnt8 = rsb("cnt8", [1, 8], F32)

            DBG = debug_stage == "dump"
            k.new_block()
            _breg = {}

            def r_breg(g):
                _breg["r"] = g.alloc_register(name="sc_bound")
                g.reg_mov(_breg["r"], E * CAP - 1)
            k.emit("gpsimd", r_breg)
            if wprep_pos < len(wprep_units):
                k.emit("gpsimd", emit_wprep(wprep_units[wprep_pos:]))
                wprep_pos = len(wprep_units)

            r_ap1 = {}
            r_ap2 = {}
            r_tp = {}
            r_ev = {}
            r_sc = {}
            r_ch = {}
            for c in range(NCH):
                def r_sync(sync, c=c):
                    if c >= 2:
                        sync.wait_ge(s_ln.h, r_ap2[c - 2])
                    sync.dma_start(xc1[:, c % 2, :],
                                   x1s[128 * c:128 * (c + 1), :]).then_inc(s_x.h, 16)
                x_at = s_x.bump(16)
                k.emit("sync", r_sync)

                er1, er2 = s_ln.bump(), s_ln.bump()
                def r_v1(v, c=c, x_at=x_at, er1=er1, er2=er2):
                    v.wait_ge(s_x.h, x_at)
                    v.bn_stats(stat3[:, 0:6], xc1[:, c % 2, 0:512]).then_inc(s_ln.h, 1)
                    v.wait_ge(s_ln.h, er1)
                    v.bn_stats(stat3[:, 6:12], xc1[:, c % 2, 512:1024]).then_inc(s_ln.h, 1)
                    v.wait_ge(s_ln.h, er2)
                    v.bn_aggr(mvA[:, :], stat3[:, 0:12]).then_inc(s_ln.h, 1)
                st1 = s_ln.bump()
                k.emit("vector", r_v1)

                def r_s1(sc, c=c, st1=st1):
                    sc.wait_ge(s_ln.h, st1)
                    sc.activation(sjk[:, 0:1], mvA[:, 1:2], AF.Sqrt,
                                  bias=epsc[:, 0:1], scale=1.0).then_inc(s_sq.h, 1)
                sq1 = s_sq.bump()
                k.emit("scalar", r_s1)

                er3, er4, er5 = s_ln.bump(), s_ln.bump(), s_ln.bump()
                def r_v2(v, c=c, sq1=sq1, er3=er3, er4=er4, er5=er5):
                    v.wait_ge(s_sq.h, sq1)
                    if c >= 2:
                        v.wait_ge(s_tp.h, r_tp[c - 2])
                        v.wait_ge(s_st.h, ST_ZB[c - 2])
                    v.reciprocal(rstdA[:, :], sjk[:, 0:1]).then_inc(s_ln.h, 1)
                    v.wait_ge(s_ln.h, er3)
                    v.tensor_scalar(zf[:, c % 2, :], xc1[:, c % 2, :], mvA[:, 0:1],
                                    rstdA[:, :], op0=ALU.subtract, op1=ALU.mult)
                    v.tensor_copy(zbb[:, c % 2, :], zf[:, c % 2, :])
                    v.bn_stats(stat3[:, 0:6], zf[:, c % 2, 0:512]).then_inc(s_ln.h, 1)
                    v.wait_ge(s_ln.h, er4)
                    v.bn_stats(stat3[:, 6:12], zf[:, c % 2, 512:1024]).then_inc(s_ln.h, 1)
                    v.wait_ge(s_ln.h, er5)
                    v.bn_aggr(mvB[:, :], stat3[:, 0:12]).then_inc(s_ln.h, 1)
                r_ap1[c] = s_ln.bump()
                k.emit("vector", r_v2)

                def r_st(sync, c=c):
                    sync.wait_ge(s_ln.h, r_ap1[c])
                    sync.dma_start(zs[128 * c:128 * (c + 1), :],
                                   zbb[:, c % 2, :]).then_inc(s_st.h, 16)
                if c == 0:
                    ST_ZB = {}
                ST_ZB[c] = s_st.bump(16)
                k.emit("sync", r_st)

                def r_s2(sc, c=c):
                    sc.wait_ge(s_ln.h, r_ap1[c])
                    sc.activation(sjk[:, 1:2], mvB[:, 1:2], AF.Sqrt,
                                  bias=epsc[:, 0:1], scale=1.0).then_inc(s_sq.h, 1)
                sq2 = s_sq.bump()
                k.emit("scalar", r_s2)

                er6 = s_ln.bump()
                def r_v3(v, c=c, sq2=sq2, er6=er6):
                    v.wait_ge(s_sq.h, sq2)
                    if c >= 2:
                        v.wait_ge(s_sc.h, r_sc[c - 2])
                    v.reciprocal(rstdB[:, :], sjk[:, 1:2]).then_inc(s_ln.h, 1)
                    v.wait_ge(s_ln.h, er6)
                    v.tensor_scalar(znbt[:, c % 2, :], zf[:, c % 2, :], mvB[:, 0:1],
                                    rstdB[:, :], op0=ALU.subtract, op1=ALU.mult
                                    ).then_inc(s_ln.h, 1)
                r_ap2[c] = s_ln.bump()
                k.emit("vector", r_v3)

                def r_tr(t, c=c):
                    t.wait_ge(s_ln.h, r_ap1[c])
                    if c >= 2:
                        t.wait_ge(s_ev.h, r_ev[c - 2])
                    bks = (pA, pB) if c % 2 == 0 else (pC, pD)
                    for d in range(8):
                        pX = bks[d // 4]
                        mm = t.transpose(pX[:, (d % 4) * 128:(d % 4 + 1) * 128],
                                         zf[:, c % 2, d * 128:(d + 1) * 128],
                                         id128f[:, :])
                    mm.then_inc(s_tp.h, 1)
                r_tp[c] = s_tp.bump()
                k.emit("tensor", r_tr)

                def r_tev(v, c=c):
                    v.wait_ge(s_tp.h, r_tp[c])
                    bks = (pA, pB) if c % 2 == 0 else (pC, pD)
                    for d in range(8):
                        pX = bks[d // 4]
                        mm = v.tensor_copy(zT32[:, c % 2, d, :],
                                           pX[:, (d % 4) * 128:(d % 4 + 1) * 128])
                    mm.then_inc(s_ev.h, 1)
                r_ev[c] = s_ev.bump()
                k.emit("vector", r_tev)

                def r_aff(t, c=c):
                    t.wait_ge(s_ev.h, r_ev[c])
                    if c >= 1:
                        t.wait_ge(s_ln.h, AFF_CONS[c - 1])
                    for d in range(8):
                        mm = t.matmul(pE_[:, 0:8], zT32[:, c % 2, d, :], CT[:, d, :],
                                      start=(d == 0), stop=(d == 7))
                    mm.then_inc(s_mm.h, 1)
                aff_ms = s_mm.bump()
                k.emit("tensor", r_aff)

                ea1, ea2, ea3, ea4 = (s_ln.bump(), s_ln.bump(), s_ln.bump(),
                                      s_ln.bump())
                def r_v4(v, c=c, aff_ms=aff_ms, ea1=ea1, ea2=ea2, ea3=ea3, ea4=ea4):
                    v.wait_ge(s_mm.h, aff_ms)

                    v.tensor_copy(affs[:, :], pE_[:, 0:8]).then_inc(s_ln.h, 1)
                    v.wait_ge(s_ln.h, ea1)
                    v.max(max8[:, :], affs[:, :]).then_inc(s_ln.h, 1)
                    v.wait_ge(s_ln.h, ea2)
                    v.max_index(idx8[:, :], max8[:, :], affs[:, :]).then_inc(s_ln.h, 1)
                    v.wait_ge(s_ln.h, ea3)
                    v.tensor_copy(esf[:, :], idx8[:, 0:1]).then_inc(s_ln.h, 1)
                    v.wait_ge(s_ln.h, ea4)
                    v.tensor_scalar(ohf[:, :], iota8[:, :], esf[:, :], None,
                                    op0=ALU.is_equal).then_inc(s_ln.h, 1)
                if c == 0:
                    AFF_CONS = {}
                AFF_CONS[c] = s_ln.bump()
                k.emit("vector", r_v4)

                def r_sg(v, c=c):
                    v.tensor_copy(mmax_all[:, c:c + 1], max8[:, 0:1]).then_inc(s_ln.h, 1)
                if c == 0:
                    SG_MS = {}
                SG_MS[c] = s_ln.bump()
                k.emit("vector", r_sg)

                def r_rk(t, c=c):
                    t.wait_ge(s_ln.h, AFF_CONS[c])
                    if c >= 1:
                        t.wait_ge(s_rt.h, r_ch[c - 1])
                    t.matmul(pF_[:, 0:8], ltu[:, :], ohf[:, :], start=True, stop=False)
                    t.matmul(pF_[:, 0:8], ones1f[0:1, :], Rrow[0:1, :],
                             start=False, stop=True)
                    t.matmul(pF_[0:1, 8:16], ones128f[:, :], ohf[:, :],
                             start=True, stop=True).then_inc(s_mm.h, 1)
                rk_ms = s_mm.bump()
                k.emit("tensor", r_rk)

                eb = [s_ln.bump() for _ in range(2)]
                eb2a = s_ln.bump()
                eb.append(s_ln.bump())
                eb6 = [s_ln.bump() for _ in range(2)]
                eb += [s_ln.bump() for _ in range(3)]
                def r_v5(v, c=c, rk_ms=rk_ms, eb=eb, eb6=eb6, eb2a=eb2a):
                    v.wait_ge(s_mm.h, rk_ms)
                    v.tensor_tensor(ohsel[:, :], pF_[:, 0:8], ohf[:, :],
                                    op=ALU.mult).then_inc(s_ln.h, 1)
                    v.wait_ge(s_ln.h, eb[0])
                    v.tensor_reduce(rkred[:, :], ohsel[:, :], axis=AX.X,
                                    op=ALU.add).then_inc(s_ln.h, 1)
                    v.wait_ge(s_ln.h, eb[1])
                    v.tensor_scalar(of32[:, :], esf[:, :], float(CAP), rkred[:, :],
                                    op0=ALU.mult, op1=ALU.add)
                    v.tensor_scalar(ovr[:, :], rkred[:, :], -(float(CAP) - 1.0), 0.0,
                                    op0=ALU.add, op1=ALU.max).then_inc(s_ln.h, 1)
                    v.wait_ge(s_ln.h, eb2a)
                    v.tensor_scalar(ovr[:, :], ovr[:, :], 1.0, None,
                                    op0=ALU.min).then_inc(s_ln.h, 1)
                    v.wait_ge(s_ln.h, eb[2])
                    v.tensor_scalar(ogf[:, :], of32[:, :], -1.0, float(E * CAP),
                                    op0=ALU.mult, op1=ALU.add).then_inc(s_ln.h, 1)
                    v.wait_ge(s_ln.h, eb6[0])
                    v.tensor_tensor(ogf[:, :], ogf[:, :], ovr[:, :],
                                    op=ALU.mult).then_inc(s_ln.h, 1)
                    v.wait_ge(s_ln.h, eb6[1])
                    v.tensor_tensor(ogf[:, :], ogf[:, :], of32[:, :],
                                    op=ALU.add).then_inc(s_ln.h, 1)
                    v.wait_ge(s_ln.h, eb[3])
                    v.tensor_copy(og_all[:, c:c + 1], ogf[:, :])
                    v.scalar_tensor_tensor(of32[:, :], ovr[:, :], 1.0e9, of32[:, :],
                                           op0=ALU.mult, op1=ALU.add).then_inc(s_ln.h, 1)
                    v.wait_ge(s_ln.h, eb[4])
                    v.tensor_copy(o_all[:, c:c + 1], of32[:, :])
                    v.tensor_copy(cnt8[0:1, :], pF_[0:1, 8:16]).then_inc(s_ln.h, 1)
                    v.wait_ge(s_ln.h, eb[5])
                    if DBG:
                        v.tensor_copy(dbg_rk[:, 0, c:c + 1], rkred[:, :])
                        v.tensor_copy(dbg_rk[:, 1, c:c + 1], ovr[:, :])
                        v.tensor_copy(dbg_rk[:, 2, c:c + 1], ogf[:, :])
                        v.tensor_copy(dbg_rk[:, 3, c:c + 1], esf[:, :])
                    v.tensor_tensor(Rrow[0:1, :], Rrow[0:1, :], cnt8[0:1, :],
                                    op=ALU.add).then_inc(s_rt.h, 1)
                r_ch[c] = s_rt.bump()
                k.emit("vector", r_v5)

                def r_scat(g, c=c):
                    if c == 0:
                        g.wait_ge(s_st.h, ZFILL_AT)
                    g.wait_ge(s_ln.h, r_ap2[c])
                    g.wait_ge(s_rt.h, r_ch[c])
                    g.indirect_dma_start(
                        out=znbkt, out_offset=IndirectOffsetOnAxis(o_all[:, c:c + 1], 0),
                        in_=znbt[:, c % 2, :], in_offset=None,
                        bounds_check=_breg["r"],
                        oob_is_err=False).then_inc(s_sc.h, 16)
                r_sc[c] = s_sc.bump(16)
                k.emit("gpsimd", r_scat)

            def r_tzero(sync):
                for cz in range((E * CAP + 128) // 128):
                    sync.dma_start(tbkt[128 * cz:128 * (cz + 1), :],
                                   zot[:, :]).then_inc(s_st.h, 16)
            s_st.bump(16 * ((E * CAP + 128) // 128))
            k.emit("sync", r_tzero)

            def r_sig_all(sc):
                sc.wait_ge(s_ln.h, SG_MS[NCH - 1])
                sc.activation(alpha_all[:, :], mmax_all[:, :],
                              AF.Sigmoid).then_inc(s_sq.h, 1)
            s_sq.bump()
            k.emit("scalar", r_sig_all)

            en1, en2 = s_ln.bump(), s_ln.bump()
            def r_nse(v, en1=en1, en2=en2):
                v.wait_ge(s_rt.h, r_ch[NCH - 1])
                v.tensor_scalar(nse_f[0:1, :], Rrow[0:1, :], 255.0, 1.0 / 256.0,
                                op0=ALU.add, op1=ALU.mult).then_inc(s_ln.h, 1)
                v.wait_ge(s_ln.h, en1)
                v.tensor_scalar(nse_f[0:1, :], nse_f[0:1, :], 0.25, None,
                                op0=ALU.add).then_inc(s_ln.h, 1)
                v.wait_ge(s_ln.h, en2)
                v.tensor_copy(nse_i[0:1, :], nse_f[0:1, :]).then_inc(s_rt.h, 1)
            NSE_MS = s_rt.bump()
            k.emit("vector", r_nse)
            k.run_block()

        with ExitStack() as fstk:
            fsb = lambda n, s, d: fstk.enter_context(nc.sbuf_tensor(k.uname(n), s, d))
            W1T = fsb("W1T", [128, 8, FF], BF16)
            W2T = fsb("W2T", [128, NFC, DM], BF16)
            znl = fsb("znl", [128, 2, 2, DM], BF16)
            znTt = fsb("znTt", [128, 2, 8, 256], BF16)
            hTt = fsb("hTt", [128, 2, 256], BF16)
            t_sb = fsb("t_sb", [128, 2, 2, DM], BF16)
            b2row = fsb("b2row", [1, 2, DM], BF16)
            PT4 = [(pA, pB), (pC, pD)]

            k.new_block()
            REGS = {}

            def mk_setup(eng_name):
                def setup(e):
                    REGS[eng_name] = (e.alloc_register(name="rns" + eng_name),
                                      e.alloc_register(name="rs" + eng_name))
                return setup
            for en_ in ("sync", "tensor", "vector", "scalar"):
                k.emit(en_, mk_setup(en_))

            def cond(e_obj, eng_name, ex, s):
                rns, rs = REGS[eng_name]
                if s == 0:
                    e_obj.reg_load(rns, nse_i[0:1, ex:ex + 1])
                e_obj.reg_mov(rs, s)
                return e_obj.If_lt(rs, rns)

            T_ST = {}
            B2_AT = {}
            W_AT = {}
            ZN_AT = {}

            for ex in range(E):
                def f_b2(g, ex=ex):
                    if ex >= 2:
                        g.wait_ge(s_w2.h, NFC * NSC * (ex - 1))
                    g.dma_start(b2row[:, ex % 2, :],
                                b2_d[ex:ex + 1, :]).then_inc(s_gq.h, 16)
                B2_AT[ex] = s_gq.bump(16)
                k.emit("gpsimd", f_b2)

                def f_wload(sync, ex=ex):
                    if ex >= 1:
                        sync.wait_ge(s_w1.h, NFC * NSC * ex)
                        sync.wait_ge(s_w2.h, NFC * NSC * ex)
                    for s in range(2):
                        sg = NSC * ex + s
                        if sg >= 2:
                            sync.wait_ge(s_trg.h, 2 * (sg - 1))
                        for gi in range(2):
                            sync.dma_start(
                                znl[:, sg % 2, gi, :],
                                znbkt[ex * CAP + s * 256 + gi * 128:
                                      ex * CAP + s * 256 + gi * 128 + 128, :]
                            ).then_inc(s_zn.h, 16)
                    for d in range(8):
                        sync.dma_start(W1T[:, d, :], w1b[ex, :, d * 128:(d + 1) * 128],
                                       transpose=True).then_inc(s_w.h, 16)
                    for f in range(NFC):
                        sync.dma_start(W2T[:, f, :], w2b[ex, :, f * 128:(f + 1) * 128],
                                       transpose=True).then_inc(s_w.h, 16)
                s_w.bump(16 * 40)
                ZN_AT[NSC * ex] = s_zn.bump(32)
                ZN_AT[NSC * ex + 1] = s_zn.bump(32)
                W_AT[ex] = s_w.v
                W_BASE = W_AT[ex] - 16 * 40
                if ex == 0:
                    W_BASES = {}
                W_BASES[ex] = W_BASE
                k.emit("sync", f_wload)

                def f_tpro(t, ex=ex):
                    t.wait_ge(s_w.h, W_BASES[ex] + 16 * 8)
                    for gi in range(2):
                        sg = NSC * ex
                        grp = 2 * sg + gi
                        pT = pTrb if grp % 2 == 0 else pTr2
                        t.wait_ge(s_zn.h, ZN_AT[sg])
                        if grp >= 2:
                            t.wait_ge(s_znT.h, grp - 1)
                        for d in range(8):
                            mm = t.transpose(pT[:, d, :],
                                             znl[:, sg % 2, gi, d * 128:(d + 1) * 128],
                                             id128[:, :])
                        mm.then_inc(s_trg.h, 1)
                s_trg.bump(2)
                k.emit("tensor", f_tpro)

                for s in range(NSC):
                    sg = NSC * ex + s

                    has_pre = s + 2 < NSC

                    def f_sync(sync, ex=ex, s=s, sg=sg, has_pre=has_pre):
                        with cond(sync, "sync", ex, s):
                            if has_pre:
                                sync.wait_ge(s_trg.h, 2 * (sg + 1))
                                for gi in range(2):
                                    sync.dma_start(
                                        znl[:, sg % 2, gi, :],
                                        znbkt[ex * CAP + (s + 2) * 256 + gi * 128:
                                              ex * CAP + (s + 2) * 256 + gi * 128 + 128, :]
                                    ).then_inc(s_zn.h, 16)
                            sync.wait_ge(s_te.h, sg + 1)
                            for gi in range(2):
                                sync.dma_start(
                                    tbkt[ex * CAP + s * 256 + gi * 128:
                                         ex * CAP + s * 256 + gi * 128 + 128, :],
                                    t_sb[:, sg % 2, gi, :]).then_inc(s_st.h, 16)
                        with sync.Else():
                            if has_pre:
                                sync.sem_inc(s_zn.h, 32)
                            sync.sem_inc(s_st.h, 32)
                    if has_pre:
                        ZN_AT[sg + 2] = s_zn.bump(32)
                    T_ST[sg] = s_st.bump(32)
                    k.emit("sync", f_sync)

                    has_tr = s + 1 < NSC

                    def f_tensor(t, ex=ex, s=s, sg=sg, has_tr=has_tr):
                        def w2_group(kk):
                            ff = kk % NFC
                            t.wait_ge(s_he.h, kk + 1)
                            if s == 0:
                                t.wait_ge(s_w.h, W_BASES[ex] + 16 * (8 + ff + 1))
                            if ff == 0:
                                if sg >= 1:
                                    t.wait_ge(s_te.h, sg)
                                t.wait_ge(s_gq.h, B2_AT[ex])
                                for sg2 in range(2):
                                    for dh in range(2):
                                        t.matmul(PT4[sg2][dh][:, :], ones1r[:, :],
                                                 b2row[0:1, ex % 2, dh * 512:(dh + 1) * 512],
                                                 start=True, stop=False)
                            for sg2 in range(2):
                                for dh in range(2):
                                    mm = t.matmul(PT4[sg2][dh][:, :],
                                                  hTt[:, kk % 2, sg2 * 128:(sg2 + 1) * 128],
                                                  W2T[:, ff, dh * 512:(dh + 1) * 512],
                                                  start=False, stop=(ff == NFC - 1))
                            mm.then_inc(s_w2.h, 1)

                        def tr_group(sgp, gi):
                            grp = 2 * sgp + gi
                            pT = pTrb if grp % 2 == 0 else pTr2
                            t.wait_ge(s_zn.h, ZN_AT[sgp])
                            t.wait_ge(s_znT.h, grp - 1)
                            for d in range(8):
                                mm = t.transpose(pT[:, d, :],
                                                 znl[:, sgp % 2, gi, d * 128:(d + 1) * 128],
                                                 id128[:, :])
                            mm.then_inc(s_trg.h, 1)

                        with cond(t, "tensor", ex, s):
                            t.wait_ge(s_znT.h, 2 * (sg + 1))
                            for f in range(NFC):
                                kk = NFC * sg + f
                                pw = pE_ if kk % 2 == 0 else pF_
                                if kk >= 2:
                                    t.wait_ge(s_he.h, kk - 1)
                                for d in range(8):
                                    mm = t.matmul(pw[:, 0:256],
                                                  W1T[:, d, f * 128:(f + 1) * 128],
                                                  znTt[:, sg % 2, d, :],
                                                  start=(d == 0), stop=(d == 7))
                                mm.then_inc(s_w1.h, 1)
                                if has_tr and f == 26:
                                    tr_group(sg + 1, 0)
                                if has_tr and f == 28:
                                    tr_group(sg + 1, 1)
                                if f >= 1:
                                    w2_group(kk - 1)
                            w2_group(NFC * sg + NFC - 1)
                        with t.Else():
                            t.sem_inc(s_w1.h, NFC)
                            t.sem_inc(s_w2.h, NFC)
                            if has_tr:
                                t.sem_inc(s_trg.h, 2)
                    s_w1.bump(NFC)
                    s_w2.bump(NFC)
                    if has_tr:
                        s_trg.bump(2)
                    k.emit("tensor", f_tensor)

                    def f_vector(v, ex=ex, s=s, sg=sg):
                        with cond(v, "vector", ex, s):
                            if sg >= 2:
                                v.wait_ge(s_w1.h, NFC * (sg - 1))
                            for gi in range(2):
                                grp = 2 * sg + gi
                                pT = pTrb if grp % 2 == 0 else pTr2
                                v.wait_ge(s_trg.h, grp + 1)
                                for d in range(8):
                                    op = v.tensor_scalar(
                                        znTt[:, sg % 2, d, gi * 128:(gi + 1) * 128],
                                        pT[:, d, :], GT[:, d, ex:ex + 1],
                                        BT[:, d, ex:ex + 1],
                                        op0=ALU.mult, op1=ALU.add)
                                op.then_inc(s_znT.h, 1)
                        with v.Else():
                            v.sem_inc(s_znT.h, 2)
                    s_znT.bump(2)
                    k.emit("vector", f_vector)

                    def f_scalar(sc, ex=ex, s=s, sg=sg):
                        with cond(sc, "scalar", ex, s):
                            for f in range(NFC):
                                kk = NFC * sg + f
                                pw = pE_ if kk % 2 == 0 else pF_
                                sc.wait_ge(s_w1.h, kk + 1)
                                if kk >= 2:
                                    sc.wait_ge(s_w2.h, kk - 1)
                                sc.activation(hTt[:, kk % 2, :], pw[:, 0:256], AF.Relu,
                                              bias=B1T[:, f, ex:ex + 1],
                                              scale=1.0).then_inc(s_he.h, 1)
                            sc.wait_ge(s_w2.h, NFC * (sg + 1))
                            if sg >= 2:
                                sc.wait_ge(s_st.h, T_ST[sg - 2])
                            for sg2 in range(2):
                                for dh in range(2):
                                    op = sc.copy(t_sb[:, sg % 2, sg2, dh * 512:(dh + 1) * 512],
                                                 PT4[sg2][dh][:, :])
                            op.then_inc(s_te.h, 1)
                        with sc.Else():
                            sc.sem_inc(s_he.h, NFC)
                            sc.sem_inc(s_te.h, 1)
                    s_he.bump(NFC)
                    s_te.bump(1)
                    k.emit("scalar", f_scalar)

            k.run_block()

        with ExitStack() as estk2:
            qsb = lambda n, s, d: estk2.enter_context(nc.sbuf_tensor(k.uname(n), s, d))
            xq = qsb("xq", [128, 2, DM], F32)
            zq = qsb("zq", [128, 2, DM], BF16)
            tq = qsb("tq", [128, 2, DM], BF16)
            fq = qsb("fq", [128, 2, DM], F32)
            oq = qsb("oq", [128, 2, DM], F32)
            stat4 = qsb("stat4", [128, 16], F32)
            mv4 = qsb("mv4", [128, 2], F32)
            rstd4 = qsb("rstd4", [128, 1], F32)
            sjk4 = qsb("sjk4", [128, 4], F32)

            k.new_block()
            q_ap = {}
            q_st = {}
            for c in range(NCH):
                def q_sync(sync, c=c):
                    if c >= 2:
                        sync.wait_ge(s_ln.h, q_ap[c - 2])
                    sync.dma_start(xq[:, c % 2, :],
                                   x1s[128 * c:128 * (c + 1), :]).then_inc(s_x.h, 16)
                    sync.dma_start(zq[:, c % 2, :],
                                   zs[128 * c:128 * (c + 1), :]).then_inc(s_x.h, 16)
                x_at = s_x.bump(32)
                k.emit("sync", q_sync)

                def q_g(g, c=c):
                    if c >= 2:
                        g.wait_ge(s_ln.h, q_ap[c - 2])
                    g.indirect_dma_start(
                        out=tq[:, c % 2, :], out_offset=None, in_=tbkt,
                        in_offset=IndirectOffsetOnAxis(og_all[:, c:c + 1], 0)
                    ).then_inc(s_sc.h, 16)
                t_at = s_sc.bump(16)
                k.emit("gpsimd", q_g)

                eq0, eq1, eq2 = s_ln.bump(), s_ln.bump(), s_ln.bump()
                def q_v1(v, c=c, x_at=x_at, t_at=t_at, eq0=eq0, eq1=eq1, eq2=eq2):
                    v.wait_ge(s_x.h, x_at)
                    v.wait_ge(s_sc.h, t_at)
                    v.scalar_tensor_tensor(fq[:, c % 2, :], tq[:, c % 2, :],
                                           alpha_all[:, c:c + 1], zq[:, c % 2, :],
                                           op0=ALU.mult, op1=ALU.add).then_inc(s_ln.h, 1)
                    v.wait_ge(s_ln.h, eq0)
                    v.bn_stats(stat4[:, 0:6], fq[:, c % 2, 0:512]).then_inc(s_ln.h, 1)
                    v.wait_ge(s_ln.h, eq1)
                    v.bn_stats(stat4[:, 6:12], fq[:, c % 2, 512:1024]).then_inc(s_ln.h, 1)
                    v.wait_ge(s_ln.h, eq2)
                    v.bn_aggr(mv4[:, :], stat4[:, 0:12]).then_inc(s_ln.h, 1)
                stq = s_ln.bump()
                k.emit("vector", q_v1)

                def q_s(sc, c=c, stq=stq):
                    sc.wait_ge(s_ln.h, stq)
                    sc.activation(sjk4[:, 0:1], mv4[:, 1:2], AF.Sqrt,
                                  bias=epsc[:, 0:1], scale=1.0).then_inc(s_sq.h, 1)
                sqq = s_sq.bump()
                k.emit("scalar", q_s)

                eq3 = s_ln.bump()
                def q_v2(v, c=c, sqq=sqq, eq3=eq3):
                    v.wait_ge(s_sq.h, sqq)
                    if c >= 2:
                        v.wait_ge(s_st.h, q_st[c - 2])
                    v.reciprocal(rstd4[:, :], sjk4[:, 0:1]).then_inc(s_ln.h, 1)
                    v.wait_ge(s_ln.h, eq3)
                    v.tensor_scalar(oq[:, c % 2, :], fq[:, c % 2, :], mv4[:, 0:1],
                                    rstd4[:, :], op0=ALU.subtract, op1=ALU.mult)
                    v.tensor_tensor(oq[:, c % 2, :], oq[:, c % 2, :], xq[:, c % 2, :],
                                    op=ALU.add).then_inc(s_ln.h, 1)
                q_ap[c] = s_ln.bump()
                k.emit("vector", q_v2)

                def q_store(sync, c=c):
                    sync.wait_ge(s_ln.h, q_ap[c])
                    sync.dma_start(out_d[128 * c:128 * (c + 1), :],
                                   oq[:, c % 2, :]).then_inc(s_st.h, 16)
                q_st[c] = s_st.bump(16)
                k.emit("sync", q_store)

            fin_target = s_st.v
            def q_fin(sync):
                sync.wait_ge(s_st.h, fin_target)
            k.emit("sync", q_fin)
            k.run_block()

            if debug_stage == "dump":
                k.new_block()
                base = s_x.v

                def dmp(sync, base=base):
                    n = 0
                    pairs = ([(d_zn, znbkt, E * CAP // 128)] +
                             [(d_t, tbkt, E * CAP // 128 + 1)] +
                             [(d_z, zs, ROWS // 128)])
                    for dst, srcb, nch2 in pairs:
                        for cc2 in range(nch2):
                            if n >= 2:
                                sync.wait_ge(s_x.h, base + 16 * (2 * (n - 1)))
                            sync.dma_start(tq[:, n % 2, :],
                                           srcb[128 * cc2:128 * (cc2 + 1), :]
                                           ).then_inc(s_x.h, 16)
                            sync.wait_ge(s_x.h, base + 32 * n + 16)
                            sync.dma_start(dst[128 * cc2:128 * (cc2 + 1), :],
                                           tq[:, n % 2, :]).then_inc(s_x.h, 16)
                            n += 1
                    sync.dma_start(d_o[:, 0, :], o_all[:, :]).then_inc(s_x.h, 16)
                    sync.dma_start(d_o[:, 1, :], og_all[:, :]).then_inc(s_x.h, 16)
                    sync.dma_start(d_al[:, :], alpha_all[:, :]).then_inc(s_x.h, 16)
                    sync.dma_start(d_rk[:, :, :], dbg_rk[:, :, :]).then_inc(s_x.h, 16)
                    sync.wait_ge(s_x.h, base + 32 * n + 64)
                ntotal = E * CAP // 128 + E * CAP // 128 + 1 + ROWS // 128
                s_x.bump(32 * ntotal + 64)
                k.emit("sync", dmp)
                k.run_block()

    return nc


def kernel(**inputs):
    x = np.ascontiguousarray(inputs["L0_attn_x"], np.float32)
    nc = build_program()
    in_maps = []
    for c in range(8):
        shard = {
            "x": np.ascontiguousarray(x[4 * c:4 * (c + 1)].reshape(ROWS, DM)),
            "wqkv": np.ascontiguousarray(inputs["L0_attn_w_qkv"], np.float32),
            "wproj": np.ascontiguousarray(inputs["L0_attn_w_proj"], np.float32),
            "cent": np.ascontiguousarray(inputs["L0_ffn_centroids"], np.float32),
            "lng": np.ascontiguousarray(inputs["L0_ffn_ln_g"], np.float32),
            "lnb": np.ascontiguousarray(inputs["L0_ffn_ln_b"], np.float32),
            "w1": np.ascontiguousarray(inputs["L0_ffn_w1"], np.float32),
            "b1": np.ascontiguousarray(inputs["L0_ffn_b1"], np.float32),
            "w2": np.ascontiguousarray(inputs["L0_ffn_w2"], np.float32),
            "b2": np.ascontiguousarray(inputs["L0_ffn_b2"], np.float32),
        }
        in_maps.append(shard)
    res = run_bass_kernel_spmd(nc, in_maps, core_ids=list(range(8)))
    out = np.concatenate([r["out"].reshape(4, N, DM) for r in res.results], axis=0)
    return out


# version 12
# speedup vs baseline: 1.1290x; 1.0019x over previous
import numpy as np
import ml_dtypes
from contextlib import ExitStack

import concourse.bass as bass
import concourse.mybir as mybir
from concourse.bass_utils import run_bass_kernel_spmd
from bass_rust import IndirectOffsetOnAxis

F32 = mybir.dt.float32
BF16 = mybir.dt.bfloat16
U32 = mybir.dt.uint32
AF = mybir.ActivationFunctionType
ALU = mybir.AluOpType
AX = mybir.AxisListType

B = 4
N = 2048
DM = 1024
H = 16
PAIRS = 8
HD = 64
MLM = 64
SEG = 32
ROWS = B * N
NCH = ROWS // 128
E = 8
FF = 4096
CAP = 4096
NSC = CAP // 256
NFC = FF // 128
EPS = 1e-5

ENGS = ("sync", "tensor", "vector", "scalar", "gpsimd")


class Sem:
    def __init__(self, stack, nc, name):
        self.h = stack.enter_context(nc.semaphore(name))
        self.v = 0

    def bump(self, k=1):
        self.v += k
        return self.v


class MK:
    def __init__(self, nc, stack, debug_stage=None):
        self.nc = nc
        self.stack = stack
        self.debug_stage = debug_stage
        self.q = None

    def new_block(self):
        self.q = {k: [] for k in ENGS}

    def emit(self, eng, fn):
        self.q[eng].append(fn)

    def run_block(self):
        q = self.q
        with self.nc.Block() as blk:
            for eng in ENGS:
                fns = q[eng]
                if not fns:
                    continue
                def body(e, fns=fns):
                    for fn in fns:
                        fn(e)
                getattr(blk, eng)(body)
        self.q = None

    _uid = 0

    def uname(self, name):
        MK._uid += 1
        return f"{name}_u{MK._uid}"

    def sb(self, name, shape, dt):
        return self.stack.enter_context(self.nc.sbuf_tensor(name, shape, dt))

    def ps(self, name, shape, dt):
        return self.stack.enter_context(self.nc.psum_tensor(name, shape, dt))

    def sem(self, name):
        return Sem(self.stack, self.nc, name)


def build_program(debug_stage=None):
    nc = bass.Bass("TRN2", target_bir_lowering=False, detect_race_conditions=False)

    x_d = nc.dram_tensor("x", [ROWS, DM], F32, kind="ExternalInput").ap()
    wqkv_d = nc.dram_tensor("wqkv", [DM, 3 * DM], F32, kind="ExternalInput").ap()
    wproj_d = nc.dram_tensor("wproj", [DM, DM], F32, kind="ExternalInput").ap()
    cent_d = nc.dram_tensor("cent", [E, DM], F32, kind="ExternalInput").ap()
    lng_d = nc.dram_tensor("lng", [E, DM], F32, kind="ExternalInput").ap()
    lnb_d = nc.dram_tensor("lnb", [E, DM], F32, kind="ExternalInput").ap()
    w1_d = nc.dram_tensor("w1", [E, FF, DM], F32, kind="ExternalInput").ap()
    b1_d = nc.dram_tensor("b1", [E, FF], F32, kind="ExternalInput").ap()
    w2_d = nc.dram_tensor("w2", [E, DM, FF], F32, kind="ExternalInput").ap()
    b2_d = nc.dram_tensor("b2", [E, DM], F32, kind="ExternalInput").ap()
    out_d = nc.dram_tensor("out", [ROWS, DM], F32, kind="ExternalOutput").ap()

    dbg_d = None
    if debug_stage in ("p2abd",):
        dbg_d = nc.dram_tensor("dbg", [128, 4, 2048], BF16, kind="ExternalOutput").ap()
    x1_kind = "ExternalOutput" if debug_stage == "attn" else "Internal"
    x1s = nc.dram_tensor("x1s", [ROWS, DM], F32, kind=x1_kind).ap()
    zs = nc.dram_tensor("zs", [ROWS, DM], BF16, kind="Internal").ap()
    znbkt = nc.dram_tensor("znbkt", [E * CAP, DM], BF16, kind="Internal").ap()
    tbkt = nc.dram_tensor("tbkt", [E * CAP + 128, DM], BF16, kind="Internal").ap()
    if debug_stage == "dump":
        d_zn = nc.dram_tensor("d_zn", [E * CAP, DM], BF16, kind="ExternalOutput").ap()
        d_t = nc.dram_tensor("d_t", [E * CAP + 128, DM], BF16, kind="ExternalOutput").ap()
        d_z = nc.dram_tensor("d_z", [ROWS, DM], BF16, kind="ExternalOutput").ap()
        d_o = nc.dram_tensor("d_o", [128, 2, NCH], U32, kind="ExternalOutput").ap()
        d_rk = nc.dram_tensor("d_rk", [128, 4, NCH], F32, kind="ExternalOutput").ap()
        d_al = nc.dram_tensor("d_al", [128, NCH], F32, kind="ExternalOutput").ap()
    w1b = nc.dram_tensor("w1b", [E, FF, DM], BF16, kind="Internal").ap()
    w2b = nc.dram_tensor("w2b", [E, DM, FF], BF16, kind="Internal").ap()

    with ExitStack() as stack:
        k = MK(nc, stack, debug_stage)
        sb, ps, sem = k.sb, k.ps, k.sem

        id128 = sb("id128", [128, 128], BF16)
        id64s = sb("id64s", [128, 64], BF16)
        id128f = sb("id128f", [128, 128], F32)
        ltu = sb("ltu", [128, 128], F32)
        ones128f = sb("ones128f", [128, 1], F32)
        iota8 = sb("iota8", [128, 8], F32)
        ones128 = sb("ones128", [128, 1], BF16)
        ones1r = sb("ones1r", [1, 128], BF16)
        ones1f = sb("ones1f", [1, 128], F32)
        sel0 = sb("sel0", [1, 128], F32)
        sel1 = sb("sel1", [1, 128], F32)
        cIs = sb("cIs", [128, 3, 64], F32)
        CT = sb("CT", [128, 8, E], F32)
        GT = sb("GT", [128, 8, E], F32)
        BT = sb("BT", [128, 8, E], F32)
        B1T = sb("B1T", [128, NFC, E], F32)
        alpha_all = sb("alpha_all", [128, NCH], F32)
        mmax_all = sb("mmax_all", [128, NCH], F32)
        dbg_rk = sb("dbg_rk", [128, 4, NCH], F32)
        o_all = sb("o_all", [128, NCH], U32)
        og_all = sb("og_all", [128, NCH], U32)
        Rrow = sb("Rrow", [1, E], F32)
        nse_f = sb("nse_f", [1, E], F32)
        nse_i = sb("nse_i", [1, E], mybir.dt.int32)
        epsc = sb("epsc", [128, 1], F32)
        zot = sb("zot", [128, DM], BF16)

        pA = ps("pA", [128, 512], F32)
        pB = ps("pB", [128, 512], F32)
        pC = ps("pC", [128, 512], F32)
        pD = ps("pD", [128, 512], F32)
        pE_ = ps("pE", [128, 512], F32)
        pF_ = ps("pF", [128, 512], F32)
        pTrb = ps("pTrb", [128, 8, 128], BF16)
        pTr2 = ps("pTr2", [128, 8, 128], BF16)

        s_x = sem("s_x")
        s_st = sem("s_st")
        s_gq = sem("s_gq")
        s_wp = sem("s_wp")
        s_ln = sem("s_ln")
        s_sq = sem("s_sq")
        s_tp = sem("s_tp")
        s_ev = sem("s_ev")
        s_ev2 = sem("s_ev2")
        s_mm = sem("s_mm")
        s_nt = sem("s_nt")
        s_nv = sem("s_nv")
        s_gp = sem("s_gp")
        s_rt = sem("s_rt")
        s_w = sem("s_w")
        s_zn = sem("s_zn")
        s_trg = sem("s_trg")
        s_znT = sem("s_znT")
        s_w1 = sem("s_w1")
        s_he = sem("s_he")
        s_w2 = sem("s_w2")
        s_te = sem("s_te")
        s_sc = sem("s_sc")

        wprep_units = [("w1", e) for e in range(E)] + [("w2", e) for e in range(E)]

        def emit_wprep(units):
            def body(g, units=units):
                for kind, e in units:
                    if kind == "w1":
                        g.dma_start(w1b[e, :, :], w1_d[e, :, :]).then_inc(s_wp.h, 16)
                    else:
                        g.dma_start(w2b[e, :, :], w2_d[e, :, :]).then_inc(s_wp.h, 16)
            s_wp.bump(16 * len(units))
            return body

        p0stk = ExitStack()
        rowbuf = p0stk.enter_context(nc.sbuf_tensor("rowbuf", [1, 4 * DM], F32))
        k.new_block()

        def p0_g(g):
            def ident_into(t, val=1.0, halfsize=None):
                g.memset(t, 0.0)
                if halfsize is None:
                    sq = t.shape[-1]
                    g.affine_select(out=t, in_=t, compare_op=ALU.not_equal, fill=val,
                                    base=0, pattern=[[-1, sq]], channel_multiplier=1)
            ident_into(id128[:, :])
            ident_into(id128f[:, :])
            g.memset(id64s[:, :], 0.0)
            for hh in range(2):
                g.affine_select(out=id64s[hh * 64:(hh + 1) * 64, :],
                                in_=id64s[hh * 64:(hh + 1) * 64, :],
                                compare_op=ALU.not_equal, fill=1.0, base=0,
                                pattern=[[-1, 64]], channel_multiplier=1)
            g.memset(ltu[:, :], 1.0)
            g.affine_select(out=ltu[:, :], in_=ltu[:, :], compare_op=ALU.is_ge,
                            fill=0.0, base=-1, pattern=[[1, 128]], channel_multiplier=-1)
            g.iota(iota8[:, :], [[1, 8]], base=0, channel_multiplier=0,
                   allow_small_or_imprecise_dtypes=True)
            g.memset(ones128[:, :], 1.0)
            g.memset(ones128f[:, :], 1.0)
            g.memset(ones1r[:, :], 1.0)
            g.memset(ones1f[:, :], 1.0)
            g.memset(sel0[:, :], 1.0)
            g.affine_select(out=sel0[:, :], in_=sel0[:, :], compare_op=ALU.is_ge,
                            fill=0.0, base=63, pattern=[[-1, 128]], channel_multiplier=0)
            g.memset(sel1[:, :], 1.0)
            g.affine_select(out=sel1[:, :], in_=sel1[:, :], compare_op=ALU.is_ge,
                            fill=0.0, base=-64, pattern=[[1, 128]], channel_multiplier=0)
            g.memset(cIs[:, :, :], 0.0)
            for j, c in enumerate([7.0, 15.0, 13.0]):
                for hh in range(2):
                    g.affine_select(out=cIs[hh * 64:(hh + 1) * 64, j, :],
                                    in_=cIs[hh * 64:(hh + 1) * 64, j, :],
                                    compare_op=ALU.not_equal, fill=c, base=0,
                                    pattern=[[-1, 64]], channel_multiplier=1)
            g.memset(Rrow[:, :], 0.0)
            g.memset(epsc[:, :], EPS)
            g.memset(zot[:, :], 0.0)
            g.nop().then_inc(s_gp.h, 1)
        P0G = s_gp.bump()
        k.emit("gpsimd", p0_g)

        p0_plan = []
        ev_ms = []
        for e in range(E):
            p0_plan.append(("4rows", e))
        for seg in range(1, 4):
            for e in range(E):
                p0_plan.append(("b1seg", e, seg))

        def p0_sync(sync):
            for it, item in enumerate(p0_plan):
                if it >= 1:
                    sync.wait_ge(s_ev.h, it)
                if item[0] == "4rows":
                    e = item[1]
                    sync.dma_start(rowbuf[0:1, 0:DM], cent_d[e:e + 1, :]).then_inc(s_x.h, 16)
                    sync.dma_start(rowbuf[0:1, DM:2 * DM], lng_d[e:e + 1, :]).then_inc(s_x.h, 16)
                    sync.dma_start(rowbuf[0:1, 2 * DM:3 * DM], lnb_d[e:e + 1, :]).then_inc(s_x.h, 16)
                    sync.dma_start(rowbuf[0:1, 3 * DM:4 * DM], b1_d[e:e + 1, 0:1024]).then_inc(s_x.h, 16)
                else:
                    e, seg = item[1], item[2]
                    sync.dma_start(rowbuf[0:1, 0:DM],
                                   b1_d[e:e + 1, seg * 1024:(seg + 1) * 1024]).then_inc(s_x.h, 16)
        nload = 0
        p0_loadcnt = []
        for item in p0_plan:
            nload += 4 if item[0] == "4rows" else 1
            p0_loadcnt.append(nload)
        s_x.bump(16 * nload)

        def p0_t(t):
            for it, item in enumerate(p0_plan):
                t.wait_ge(s_x.h, 16 * p0_loadcnt[it])
                if it >= 2:
                    t.wait_ge(s_ev.h, it - 1)
                pX = pA if it % 2 == 0 else pB
                nr = 4 if item[0] == "4rows" else 1
                for c in range(8):
                    for r in range(nr):
                        mm = t.matmul(pX[:, 4 * c + r:4 * c + r + 1],
                                      rowbuf[0:1, r * DM + c * 128:r * DM + (c + 1) * 128],
                                      ones1f[0:1, 0:1], start=True, stop=True)
                mm.then_inc(s_mm.h, 1)
        p0_mm0 = s_mm.v
        s_mm.bump(len(p0_plan))

        def p0_v(v):
            for it, item in enumerate(p0_plan):
                v.wait_ge(s_mm.h, p0_mm0 + it + 1)
                pX = pA if it % 2 == 0 else pB
                if item[0] == "4rows":
                    e = item[1]
                    for c in range(8):
                        v.tensor_copy(CT[:, c, e:e + 1], pX[:, 4 * c:4 * c + 1])
                        v.tensor_copy(GT[:, c, e:e + 1], pX[:, 4 * c + 1:4 * c + 2])
                        v.tensor_copy(BT[:, c, e:e + 1], pX[:, 4 * c + 2:4 * c + 3])
                        op = v.tensor_copy(B1T[:, c, e:e + 1], pX[:, 4 * c + 3:4 * c + 4])
                    op.then_inc(s_ev.h, 1)
                else:
                    e, seg = item[1], item[2]
                    for c in range(8):
                        op = v.tensor_copy(B1T[:, seg * 8 + c, e:e + 1],
                                           pX[:, 4 * c:4 * c + 1])
                    op.then_inc(s_ev.h, 1)
        s_ev.bump(len(p0_plan))

        k.emit("sync", p0_sync)
        k.emit("tensor", p0_t)
        k.emit("vector", p0_v)
        k.run_block()
        p0stk.close()

        if debug_stage == "p0":
            return nc

        wprep_pos = 0
        CTbf = None

        nbatch = 1 if debug_stage in ("p2ab", "p2abd", "p2c", "attn1") else B
        for b in range(nbatch):
            with ExitStack() as bstk:
                bsb = lambda n, s, d: bstk.enter_context(nc.sbuf_tensor(k.uname(n), s, d))
                QKT = bsb("QKT", [128, 16, N], BF16)
                Vrm = bsb("Vrm", [128, 16, DM], BF16)
                xnlT = bsb("xnlT", [128, 8, MLM], BF16)
                QlKlT = bsb("QlKlT", [128, 16, MLM], BF16)

                with ExitStack() as astk:
                    asb = lambda n, s, d: astk.enter_context(nc.sbuf_tensor(k.uname(n), s, d))
                    xnT = asb("xnT", [128, 8, N], BF16)
                    wqkvb = asb("wqkvb", [128, 8, 3 * DM], BF16)
                    xc = asb("xc", [128, 2, DM], F32)
                    xnrm = asb("xnrm", [128, 2, DM], BF16)
                    stat = asb("stat", [128, 16], F32)
                    mv2 = asb("mv2", [128, 2], F32)
                    rstd = asb("rstd", [128, 1], F32)
                    sjunk = asb("sjunk", [128, 4], F32)
                    lred = asb("lred", [128, MLM], F32)

                    k.new_block()
                    xrow0 = b * N

                    gq0 = s_gq.v
                    def a_gq(g, gq0=gq0):
                        for d in range(8):
                            g.dma_start(wqkvb[:, d, :],
                                        wqkv_d[d * 128:(d + 1) * 128, :]).then_inc(s_gq.h, 16)
                    s_gq.bump(16 * 8)
                    k.emit("gpsimd", a_gq)

                    ln_stats = {}
                    ln_apply = {}
                    tp_ms = {}
                    evV = {}
                    evS = {}
                    for i in range(16):
                        x_at = s_x.v + 16

                        def c_sync(sync, i=i, xrow0=xrow0):
                            if i >= 2:
                                sync.wait_ge(s_ln.h, ln_apply[i - 2])
                            sync.dma_start(
                                xc[:, i % 2, :],
                                x_d[xrow0 + 128 * i:xrow0 + 128 * (i + 1), :]).then_inc(s_x.h, 16)
                        s_x.bump(16)
                        k.emit("sync", c_sync)

                        e1_, e2_ = s_ln.bump(), s_ln.bump()
                        def c_stats(v, i=i, x_at=x_at, e1_=e1_, e2_=e2_):
                            v.wait_ge(s_x.h, x_at)
                            v.bn_stats(stat[:, 0:6], xc[:, i % 2, 0:512]).then_inc(s_ln.h, 1)
                            v.wait_ge(s_ln.h, e1_)
                            v.bn_stats(stat[:, 6:12], xc[:, i % 2, 512:1024]).then_inc(s_ln.h, 1)
                            v.wait_ge(s_ln.h, e2_)
                            v.bn_aggr(mv2[:, :], stat[:, 0:12]).then_inc(s_ln.h, 1)
                        ln_stats[i] = s_ln.bump()
                        k.emit("vector", c_stats)

                        def c_sqrt(sc, i=i):
                            sc.wait_ge(s_ln.h, ln_stats[i])
                            sc.activation(sjunk[:, 0:1], mv2[:, 1:2], AF.Sqrt,
                                          bias=epsc[:, 0:1], scale=1.0).then_inc(s_sq.h, 1)
                        sq_i = s_sq.bump()
                        k.emit("scalar", c_sqrt)

                        er_ = s_ln.bump()
                        def c_apply(v, i=i, sq_i=sq_i, er_=er_):
                            v.wait_ge(s_sq.h, sq_i)
                            if i >= 2:
                                v.wait_ge(s_tp.h, tp_ms[i - 2])
                            v.reciprocal(rstd[:, :], sjunk[:, 0:1]).then_inc(s_ln.h, 1)
                            v.wait_ge(s_ln.h, er_)
                            v.tensor_scalar(xnrm[:, i % 2, :], xc[:, i % 2, :],
                                            mv2[:, 0:1], rstd[:, :],
                                            op0=ALU.subtract, op1=ALU.mult
                                            ).then_inc(s_ln.h, 1)
                        ln_apply[i] = s_ln.bump()
                        k.emit("vector", c_apply)

                        def c_tr(t, i=i):
                            t.wait_ge(s_ln.h, ln_apply[i])
                            if i >= 2:
                                t.wait_ge(s_ev.h, evV[i - 2])
                            pT = pTrb if i % 2 == 0 else pTr2
                            for d in range(8):
                                mm = t.transpose(pT[:, d, :],
                                                 xnrm[:, i % 2, d * 128:(d + 1) * 128],
                                                 id128[:, :])
                            mm.then_inc(s_tp.h, 1)
                        tp_ms[i] = s_tp.bump()
                        k.emit("tensor", c_tr)

                        def c_evV(v, i=i):
                            v.wait_ge(s_tp.h, tp_ms[i])
                            pT = pTrb if i % 2 == 0 else pTr2
                            for d in range(8):
                                mm = v.tensor_copy(xnT[:, d, i * 128:(i + 1) * 128],
                                                   pT[:, d, :])
                            mm.then_inc(s_ev.h, 1)
                        evV[i] = s_ev.bump()
                        k.emit("vector", c_evV)
                        evS[i] = s_ev2.v

                    if b == 0:
                        def zfill(sync):
                            for cz in range(E * CAP // 128):
                                sync.dma_start(
                                    znbkt[128 * cz:128 * (cz + 1), :],
                                    zot[:, :]).then_inc(s_st.h, 16)

                        globals()
                        k.emit("sync", zfill)
                        ZFILL_AT = s_st.bump(16 * (E * CAP // 128))

                    lm_pieces = []
                    for d in range(8):
                        el_ = None
                        def c_lm(v, d=d, el_=None):
                            if d == 0:
                                v.wait_ge(s_ev.h, evV[15])
                            v.tensor_reduce(lred[:, :],
                                            xnT[:, d, :].rearrange("p (m s) -> p m s", m=MLM),
                                            axis=AX.X, op=ALU.add).then_inc(s_ln.h, 1)
                            v.wait_ge(s_ln.h, el_)
                            v.tensor_scalar(xnlT[:, d, :], lred[:, :], 1.0 / SEG, None,
                                            op0=ALU.mult).then_inc(s_ln.h, 1)
                        lm_pieces.append(c_lm)
                    lm_ms = []

                    qk_mm = {}
                    qk_ev = {}
                    wq_at = s_gq.v

                    for nh in range(4):
                        for cc in range(16):
                            gidx = nh * 16 + cc

                            def qk_t(t, nh=nh, cc=cc, gidx=gidx, wq_at=wq_at):
                                if gidx == 0:
                                    t.wait_ge(s_gq.h, wq_at)
                                if cc == 0:
                                    for j in range(4):
                                        t.wait_ge(s_ev.h, evV[4 * nh + j])
                                if gidx >= 2:
                                    t.wait_ge(s_ev.h, qk_ev[gidx - 2])
                                pQ = pA if gidx % 2 == 0 else pB
                                for d in range(8):
                                    mm = t.matmul(pQ[:, :],
                                                  wqkvb[:, d, cc * 128:(cc + 1) * 128],
                                                  xnT[:, d, nh * 512:(nh + 1) * 512],
                                                  start=(d == 0), stop=(d == 7))
                                mm.then_inc(s_mm.h, 1)
                            qk_mm[gidx] = s_mm.bump()
                            k.emit("tensor", qk_t)

                            def qk_e(v, nh=nh, cc=cc, gidx=gidx):
                                v.wait_ge(s_mm.h, qk_mm[gidx])
                                pQ = pA if gidx % 2 == 0 else pB
                                scale = 0.125 if cc < 8 else 1.0
                                v.tensor_scalar(QKT[:, cc, nh * 512:(nh + 1) * 512],
                                                pQ[:, :], scale, None,
                                                op0=ALU.mult).then_inc(s_ev.h, 1)
                            qk_ev[gidx] = s_ev.bump()
                            k.emit("vector", qk_e)
                            if 8 <= gidx < 16:
                                el_ = s_ln.bump()
                                fn = lm_pieces[gidx - 8]
                                k.emit("vector",
                                       lambda v, fn=fn, el_=el_: fn(v, el_=el_))
                                lm_ms.append(s_ln.bump())

                    v_mm = {}
                    v_ev = {}
                    for i in range(16):
                        for dh in range(2):
                            vc = i * 2 + dh

                            def v_t(t, i=i, dh=dh, vc=vc):
                                if vc >= 2:
                                    t.wait_ge(s_ev2.h, v_ev[vc - 2])
                                pV = pC if vc % 2 == 0 else pD
                                for d in range(8):
                                    mm = t.matmul(pV[:, :], xnT[:, d, i * 128:(i + 1) * 128],
                                                  wqkvb[:, d, 2048 + dh * 512:2048 + (dh + 1) * 512],
                                                  start=(d == 0), stop=(d == 7))
                                mm.then_inc(s_mm.h, 1)
                            v_mm[vc] = s_mm.bump()
                            k.emit("tensor", v_t)

                            def v_e(sc, i=i, dh=dh, vc=vc):
                                sc.wait_ge(s_mm.h, v_mm[vc])
                                pV = pC if vc % 2 == 0 else pD
                                sc.copy(Vrm[:, i, dh * 512:(dh + 1) * 512],
                                        pV[:, :]).then_inc(s_ev2.h, 1)
                            v_ev[vc] = s_ev2.bump()
                            k.emit("scalar", v_e)

                    ql_mm = {}
                    ql_ev = {}
                    for cc in range(16):
                        def ql_t(t, cc=cc):
                            if cc == 0:
                                t.wait_ge(s_ln.h, lm_ms[-1])
                            if cc >= 2:
                                t.wait_ge(s_ev.h, ql_ev[cc - 2])
                            pQ = pE_ if cc % 2 == 0 else pF_
                            for d in range(8):
                                mm = t.matmul(pQ[:, 0:MLM],
                                              wqkvb[:, d, cc * 128:(cc + 1) * 128],
                                              xnlT[:, d, :], start=(d == 0), stop=(d == 7))
                            mm.then_inc(s_mm.h, 1)
                        ql_mm[cc] = s_mm.bump()
                        k.emit("tensor", ql_t)

                        def ql_e(v, cc=cc):
                            v.wait_ge(s_mm.h, ql_mm[cc])
                            pQ = pE_ if cc % 2 == 0 else pF_
                            scale = 0.125 if cc < 8 else 1.0
                            v.tensor_scalar(QlKlT[:, cc, :], pQ[:, 0:MLM], scale, None,
                                            op0=ALU.mult).then_inc(s_ev.h, 1)
                        ql_ev[cc] = s_ev.bump()
                        k.emit("vector", ql_e)

                    nun = 4
                    k.emit("gpsimd", emit_wprep(wprep_units[wprep_pos:wprep_pos + nun]))
                    wprep_pos += nun

                    k.run_block()

                if debug_stage == "p2ab":
                    return nc
                if debug_stage == "p2abd":
                    k.new_block()
                    def dump(sync):
                        sync.dma_start(dbg_d[:, 0, :], xnT[:, 0, :]).then_inc(s_st.h, 16)
                        sync.dma_start(dbg_d[:, 1, :], QKT[:, 0, :]).then_inc(s_st.h, 16)
                        sync.dma_start(dbg_d[:, 2, :], QKT[:, 8, :]).then_inc(s_st.h, 16)
                        sync.dma_start(dbg_d[:, 3, 0:1024], Vrm[:, 0, :]).then_inc(s_st.h, 16)
                        sync.dma_start(dbg_d[:, 3, 1024:1088], QlKlT[:, 0, :]).then_inc(s_st.h, 16)
                        sync.dma_start(dbg_d[:, 3, 1088:1152], xnlT[:, 0, :]).then_inc(s_st.h, 16)
                        sync.wait_ge(s_st.h, s_st_target)
                    s_st_target = s_st.bump(16 * 6)
                    k.emit("sync", dump)
                    k.run_block()
                    return nc

                with ExitStack() as pstk:
                    psb = lambda n, s, d: pstk.enter_context(nc.sbuf_tensor(k.uname(n), s, d))
                    SVT = psb("SVT", [128, 8, N], BF16)
                    E1T = psb("E1T", [128, 2, N], BF16)
                    E3T = psb("E3T", [128, 2, 16, 128], BF16)
                    e1f = psb("e1f", [128, 2, 512], F32)
                    e1n = psb("e1n", [128, 2, 512], BF16)
                    k2e = psb("k2e", [128, 64], F32)
                    k2s = psb("k2s", [128, 1], F32)
                    k2n = psb("k2n", [128, 64], BF16)
                    k2T_st = psb("k2T_st", [128, 64], BF16)
                    k2T_bd = psb("k2T_bd", [128, 128], BF16)
                    Kl_bd = psb("Kl_bd", [128, 128], BF16)
                    Ql_bd = psb("Ql_bd", [128, 128], BF16)
                    V_st = psb("V_st", [128, 64], BF16)
                    V_bd = psb("V_bd", [128, 128], BF16)
                    VT_bd = psb("VT_bd", [128, 128], BF16)
                    KVT_bd = psb("KVT_bd", [128, 128], BF16)
                    Z_bd = psb("Z_bd", [128, 128], BF16)
                    Zst = psb("Zst", [128, 64], BF16)
                    T_st = psb("T_st", [128, 64], BF16)
                    AVn = psb("AVn", [128, 64], BF16)
                    rdr = psb("rdr", [1, 2], F32)
                    scol = psb("scol", [128, 1], F32)
                    rds = psb("rds", [128, 1], F32)
                    s1r = psb("s1r", [128, 8], F32)
                    rs3 = psb("rs3", [128, 1], F32)
                    nmax = psb("nmax", [128, 1], F32)

                    k.new_block()

                    def pz(g):
                        for tile in (k2T_bd, Kl_bd, Ql_bd, V_bd, VT_bd, KVT_bd, Z_bd):
                            g.memset(tile[:, :], 0.0)
                        g.nop().then_inc(s_gp.h, 1)
                    bd_zero = s_gp.bump()
                    k.emit("gpsimd", pz)

                    prev_pf_free = None
                    prev_sv = {}
                    prev_av = {}
                    prev_tr2_ok = None
                    prev_kg_last = None
                    prev_k3e = None
                    prev_k1T = None
                    prev_svev = None
                    avn_prev = {}

                    sub = debug_stage if isinstance(debug_stage, str) else ""
                    npairs = 1 if sub.startswith("p2c") and sub != "p2c" else PAIRS
                    for pr in range(npairs):
                        par = pr % 2
                        def c0(g, pr=pr, gate=prev_kg_last):
                            if pr == 0:
                                g.wait_ge(s_gp.h, bd_zero)
                            if gate is not None:
                                g.wait_ge(s_mm.h, gate)
                            g.tensor_copy(Kl_bd[0:64, 0:64], QlKlT[0:64, 8 + pr, :])
                            g.tensor_copy(Kl_bd[64:128, 64:128], QlKlT[64:128, 8 + pr, :])
                            g.tensor_copy(Ql_bd[0:64, 0:64], QlKlT[0:64, pr, :])
                            g.tensor_copy(Ql_bd[64:128, 64:128],
                                          QlKlT[64:128, pr, :]).then_inc(s_gp.h, 1)
                        klbd_ok = s_gp.bump()
                        k.emit("gpsimd", c0)
                        if sub == "p2c_0":
                            break

                        def c1(t, pr=pr, gate=prev_pf_free):
                            if gate is not None:
                                t.wait_ge(s_nv.h, gate)
                            for hh in range(2):
                                mm = t.matmul(pF_[hh * 64:(hh + 1) * 64, 0:64],
                                              QlKlT[hh * 64:(hh + 1) * 64, pr, :],
                                              QlKlT[hh * 64:(hh + 1) * 64, 8 + pr, :],
                                              start=True, stop=True)
                            mm.then_inc(s_mm.h, 1)
                        l2_ms = s_mm.bump()
                        k.emit("tensor", c1)
                        if sub == "p2c_1":
                            break

                        def c2(v, l2_ms=l2_ms):
                            v.wait_ge(s_mm.h, l2_ms)
                            v.reduce_max(nmax[:, :], pF_[:, 0:64], axis=AX.X,
                                         negate=True).then_inc(s_nv.h, 1)
                        nmax_ms = s_nv.bump()
                        k.emit("vector", c2)
                        if sub == "p2c_2":
                            break

                        def c3(sc, nmax_ms=nmax_ms):
                            sc.wait_ge(s_nv.h, nmax_ms)
                            sc.activation(k2e[:, :], pF_[:, 0:64], AF.Exp,
                                          bias=nmax[:, :], scale=1.0,
                                          accum_out=k2s[:, :]).then_inc(s_ev2.h, 1)
                        k2e_ms = s_ev2.bump()
                        k.emit("scalar", c3)
                        if sub == "p2c_3":
                            break

                        ec4 = s_nv.bump()
                        def c4(v, k2e_ms=k2e_ms, gate=prev_kg_last, ec4=ec4):
                            v.wait_ge(s_ev2.h, k2e_ms)
                            if gate is not None:
                                v.wait_ge(s_mm.h, gate)
                            v.reciprocal(rds[:, :], k2s[:, :]).then_inc(s_nv.h, 1)
                            v.wait_ge(s_nv.h, ec4)
                            v.tensor_scalar(k2n[:, :], k2e[:, :], rds[:, :], None,
                                            op0=ALU.mult).then_inc(s_nv.h, 1)
                        k2n_ms = s_nv.bump()
                        k.emit("vector", c4)
                        if sub == "p2c_4":
                            break

                        def c5(t, k2n_ms=k2n_ms, gate=prev_tr2_ok):
                            t.wait_ge(s_nv.h, k2n_ms)
                            if gate is not None:
                                t.wait_ge(s_nv.h, gate)
                            t.transpose(pTr2[0:64, 0, 0:64], k2n[0:64, :], id64s[0:64, :])
                            mm = t.transpose(pTr2[64:128, 0, 64:128], k2n[64:128, :],
                                             id64s[64:128, :])
                            mm.then_inc(s_nt.h, 1)
                        k2t_ps = s_nt.bump()
                        k.emit("tensor", c5)
                        if sub == "p2c_5":
                            break

                        def c6(v, k2t_ps=k2t_ps):
                            v.wait_ge(s_nt.h, k2t_ps)
                            v.tensor_copy(k2T_st[0:64, :], pTr2[0:64, 0, 0:64])
                            v.tensor_copy(k2T_st[64:128, :], pTr2[64:128, 0, 64:128])
                            v.tensor_copy(k2T_bd[0:64, 0:64], pTr2[0:64, 0, 0:64])
                            v.tensor_copy(k2T_bd[64:128, 64:128],
                                          pTr2[64:128, 0, 64:128]).then_inc(s_nv.h, 1)
                        k2t_ok = s_nv.bump()
                        k.emit("vector", c6)
                        if sub == "p2c_6":
                            break

                        def c7a(v, k2t_ok=k2t_ok):
                            v.wait_ge(s_nv.h, k2t_ok)
                            v.reduce_sum(scol[:, :], k2T_st[:, :], axis=AX.X
                                         ).then_inc(s_nv.h, 1)
                        scol_ms = s_nv.bump()
                        k.emit("vector", c7a)

                        def c7b(t, scol_ms=scol_ms):
                            t.wait_ge(s_nv.h, scol_ms)
                            t.transpose(pF_[0:1, 256:384], scol[:, :],
                                        id128f[:, :]).then_inc(s_mm.h, 1)
                        den_ms = s_mm.bump()
                        k.emit("tensor", c7b)
                        if sub == "p2c_7":
                            break

                        ec8 = s_nv.bump()
                        def c8(v, den_ms=den_ms, ec8=ec8):
                            v.wait_ge(s_mm.h, den_ms)
                            v.reduce_max(rdr[0:1, 0:1], pF_[0:1, 256:320], axis=AX.X)
                            v.reduce_max(rdr[0:1, 1:2], pF_[0:1, 320:384],
                                         axis=AX.X).then_inc(s_nv.h, 1)
                            v.wait_ge(s_nv.h, ec8)
                            v.reciprocal(rdr[0:1, :], rdr[0:1, :]).then_inc(s_nv.h, 1)
                        rd_ms = s_nv.bump()
                        k.emit("vector", c8)
                        if sub == "p2c_8":
                            break

                        def c9(t, rd_ms=rd_ms):
                            t.wait_ge(s_nv.h, rd_ms)
                            t.matmul(pF_[:, 192:193], sel0[0:1, :], rdr[0:1, 0:1],
                                     start=True, stop=False)
                            t.matmul(pF_[:, 192:193], sel1[0:1, :], rdr[0:1, 1:2],
                                     start=False, stop=True).then_inc(s_mm.h, 1)
                        rdb_ms = s_mm.bump()
                        k.emit("tensor", c9)
                        if sub == "p2c_9":
                            break

                        ec10 = s_nv.bump()
                        ec10b = s_nv.bump()
                        def c10(v, rdb_ms=rdb_ms, k2t_ok=k2t_ok, ec10=ec10, ec10b=ec10b):
                            v.wait_ge(s_mm.h, rdb_ms)
                            v.wait_ge(s_nv.h, k2t_ok)
                            v.tensor_copy(rds[:, :], pF_[:, 192:193]).then_inc(s_nv.h, 1)
                            v.wait_ge(s_nv.h, ec10)
                            v.tensor_scalar(V_st[:, :], k2T_st[:, :], rds[:, :], None,
                                            op0=ALU.mult).then_inc(s_nv.h, 1)
                            v.wait_ge(s_nv.h, ec10b)
                            v.tensor_copy(V_bd[0:64, 0:64], V_st[0:64, :])
                            v.tensor_copy(V_bd[64:128, 64:128],
                                          V_st[64:128, :]).then_inc(s_nv.h, 1)
                        v_ok = s_nv.bump()
                        k.emit("vector", c10)
                        if sub == "p2cS":
                            break

                        newton = []
                        vok = {0: v_ok}
                        for it in range(6):
                            slot = 1 + (it % 2)

                            def n_b(t, it=it, slot=slot, vok=vok):
                                t.wait_ge(s_nv.h, vok[it])
                                t.transpose(pTr2[0:64, slot, 0:64], V_st[0:64, :],
                                            id64s[0:64, :])
                                t.transpose(pTr2[64:128, slot, 64:128], V_st[64:128, :],
                                            id64s[64:128, :])
                                t.matmul(pF_[:, 256:320], k2T_bd[:, :], V_st[:, :],
                                         start=True, stop=True)
                                t.matmul(pF_[:, 320:384], V_bd[:, :], k2T_st[:, :],
                                         start=True, stop=True).then_inc(s_nt.h, 1)
                            kv_ms = s_nt.bump()

                            def n_c(v, it=it, slot=slot, kv_ms=kv_ms):
                                v.wait_ge(s_nt.h, kv_ms)
                                v.tensor_tensor(T_st[:, :], cIs[:, 0, :], pF_[:, 256:320],
                                                op=ALU.subtract)
                                v.tensor_copy(KVT_bd[0:64, 0:64], pF_[0:64, 320:384])
                                v.tensor_copy(KVT_bd[64:128, 64:128], pF_[64:128, 320:384])
                                v.tensor_copy(VT_bd[0:64, 0:64], pTr2[0:64, slot, 0:64])
                                v.tensor_copy(VT_bd[64:128, 64:128],
                                              pTr2[64:128, slot, 64:128]).then_inc(s_nv.h, 1)
                            t1_ms = s_nv.bump()

                            def n_d(t, t1_ms=t1_ms):
                                t.wait_ge(s_nv.h, t1_ms)
                                t.matmul(pF_[:, 384:448], KVT_bd[:, :], T_st[:, :],
                                         start=True, stop=True).then_inc(s_nt.h, 1)
                            u1_ms = s_nt.bump()

                            def n_e(v, u1_ms=u1_ms):
                                v.wait_ge(s_nt.h, u1_ms)
                                v.tensor_tensor(T_st[:, :], cIs[:, 1, :], pF_[:, 384:448],
                                                op=ALU.subtract).then_inc(s_nv.h, 1)
                            t2_ms = s_nv.bump()

                            def n_f(t, t2_ms=t2_ms):
                                t.wait_ge(s_nv.h, t2_ms)
                                t.matmul(pF_[:, 448:512], KVT_bd[:, :], T_st[:, :],
                                         start=True, stop=True).then_inc(s_nt.h, 1)
                            u2_ms = s_nt.bump()

                            def n_g(v, u2_ms=u2_ms):
                                v.wait_ge(s_nt.h, u2_ms)
                                v.tensor_tensor(T_st[:, :], cIs[:, 2, :], pF_[:, 448:512],
                                                op=ALU.subtract).then_inc(s_nv.h, 1)
                            t3_ms = s_nv.bump()

                            def n_h(t, t3_ms=t3_ms):
                                t.wait_ge(s_nv.h, t3_ms)
                                t.matmul(pF_[:, 256:320], VT_bd[:, :], T_st[:, :],
                                         start=True, stop=True).then_inc(s_nt.h, 1)
                            vn_ms = s_nt.bump()

                            eni = s_nv.bump() if it < 5 else None
                            def n_i(v, vn_ms=vn_ms, it=it, eni=eni):
                                v.wait_ge(s_nt.h, vn_ms)
                                if it < 5:
                                    v.tensor_scalar(V_st[:, :], pF_[:, 256:320], 0.25, None,
                                                    op0=ALU.mult).then_inc(s_nv.h, 1)
                                    v.wait_ge(s_nv.h, eni)
                                    v.tensor_copy(V_bd[0:64, 0:64], V_st[0:64, :])
                                    v.tensor_copy(V_bd[64:128, 64:128],
                                                  V_st[64:128, :]).then_inc(s_nv.h, 1)
                                else:
                                    v.tensor_scalar(V_st[:, :], pF_[:, 256:320], 0.25,
                                                    None, op0=ALU.mult).then_inc(s_nv.h, 1)
                            vok[it + 1] = s_nv.bump()

                            newton += [("tensor", n_b), ("vector", n_c),
                                       ("tensor", n_d), ("vector", n_e), ("tensor", n_f),
                                       ("vector", n_g), ("tensor", n_h), ("vector", n_i)]

                        kpath = []
                        k1g = {}
                        k1e_ms = {}
                        k1n_ms = {}
                        k1t_ms = {}
                        k1T_ms = {}
                        k3g = {}
                        k3e_ms = {}
                        for g8 in range(8):
                            isk1 = g8 < 4
                            gg = g8 % 4

                            def kg_t(t, pr=pr, g8=g8, isk1=isk1, gg=gg,
                                     klbd_ok=klbd_ok, pk3e=prev_k3e,
                                     k1e_ms=k1e_ms, k3e_ms=k3e_ms):
                                if g8 == 0:
                                    t.wait_ge(s_gp.h, klbd_ok)
                                if g8 >= 2:
                                    t.wait_ge(s_ev2.h,
                                              (k1e_ms if g8 - 2 < 4 else k3e_ms)[
                                                  (g8 - 2) % 4])
                                elif pk3e is not None:
                                    t.wait_ge(s_ev2.h, pk3e[2 + g8])
                                pX = pA if g8 % 2 == 0 else pB
                                cc = (pr if isk1 else 8 + pr)
                                bd = Kl_bd if isk1 else Ql_bd
                                for j in range(4):
                                    nch = 4 * gg + j
                                    mm = t.matmul(pX[:, j * 128:(j + 1) * 128],
                                                  QKT[:, cc, nch * 128:(nch + 1) * 128],
                                                  bd[:, :], start=True, stop=True)
                                mm.then_inc(s_mm.h, 1)
                            (k1g if isk1 else k3g)[gg] = s_mm.bump()
                            kpath.append(("tensor", kg_t))

                            if isk1:
                                def k1_s(sc, pr=pr, gg=gg, par=par, k1g=k1g, k1n_ms=k1n_ms):
                                    sc.wait_ge(s_mm.h, k1g[gg])
                                    if gg >= 2:
                                        sc.wait_ge(s_ev.h, k1n_ms[gg - 2])
                                    pX = pA if gg % 2 == 0 else pB
                                    sc.activation(e1f[:, gg % 2, :], pX[:, :],
                                                  AF.Exp).then_inc(s_ev2.h, 1)
                                k1e_ms[gg] = s_ev2.bump()
                                kpath.append(("scalar", k1_s))

                                ek1, ek2 = s_ev.bump(), s_ev.bump()
                                def k1_v(v, gg=gg, par=par, k1e_ms=k1e_ms,
                                         k1t_ms=k1t_ms, ek1=ek1, ek2=ek2):
                                    v.wait_ge(s_ev2.h, k1e_ms[gg])
                                    if gg >= 2:
                                        v.wait_ge(s_tp.h, k1t_ms[gg - 2])
                                    v.tensor_reduce(s1r[:, :],
                                                    e1f[:, gg % 2, :].rearrange(
                                                        "p (a m) -> p a m", a=8),
                                                    axis=AX.X, op=ALU.add).then_inc(s_ev.h, 1)
                                    v.wait_ge(s_ev.h, ek1)
                                    v.reciprocal(s1r[:, :], s1r[:, :]).then_inc(s_ev.h, 1)
                                    v.wait_ge(s_ev.h, ek2)
                                    for j in range(8):
                                        op = v.tensor_scalar(
                                            e1n[:, gg % 2, j * 64:(j + 1) * 64],
                                            e1f[:, gg % 2, j * 64:(j + 1) * 64],
                                            s1r[:, j:j + 1], None, op0=ALU.mult)
                                    op.then_inc(s_ev.h, 1)
                                k1n_ms[gg] = s_ev.bump()
                                kpath.append(("vector", k1_v))

                                def k1_tr(t, gg=gg, par=par, pk1T=prev_k1T, k1n_ms=k1n_ms, k1T_ms=k1T_ms):
                                    t.wait_ge(s_ev.h, k1n_ms[gg])
                                    if gg >= 1:
                                        t.wait_ge(s_ev2.h, k1T_ms[gg - 1])
                                    elif pk1T is not None:
                                        t.wait_ge(s_ev2.h, pk1T[3])
                                    for j in range(4):
                                        mm = t.transpose(pTrb[:, 4 * (gg % 2) + j, :],
                                                         e1n[:, gg % 2, j * 128:(j + 1) * 128],
                                                         id128[:, :])
                                    mm.then_inc(s_tp.h, 1)
                                k1t_ms[gg] = s_tp.bump()
                                kpath.append(("tensor", k1_tr))

                                def k1_e(sc, gg=gg, par=par, pr=pr, k1t_ms=k1t_ms):
                                    sc.wait_ge(s_tp.h, k1t_ms[gg])
                                    if gg == 0 and pr >= 2 and prev_sv.get(pr - 2):
                                        sc.wait_ge(s_mm.h, prev_sv[pr - 2])
                                    for j in range(4):
                                        nch = 4 * gg + j
                                        mm = sc.copy(E1T[:, par, nch * 128:(nch + 1) * 128],
                                                     pTrb[:, 4 * (gg % 2) + j, :])
                                    mm.then_inc(s_ev2.h, 1)
                                k1T_ms[gg] = s_ev2.bump()
                                kpath.append(("scalar", k1_e))
                            else:
                                def k3_s(sc, gg=gg, par=par, pr=pr, k3g=k3g):
                                    sc.wait_ge(s_mm.h, k3g[gg])
                                    if gg == 0 and pr >= 2 and prev_av.get(pr - 2):
                                        sc.wait_ge(s_mm.h, prev_av[pr - 2])
                                    pX = pA if gg % 2 == 0 else pB
                                    sc.activation(
                                        E3T[:, par, 4 * gg:4 * gg + 4, :].rearrange(
                                            "p a m -> p (a m)"),
                                        pX[:, :], AF.Exp).then_inc(s_ev2.h, 1)
                                k3e_ms[gg] = s_ev2.bump()
                                kpath.append(("scalar", k3_s))

                        def s3av_t(t, pr=pr, par=par, psvev=prev_svev, k3e_ms=k3e_ms):
                            t.wait_ge(s_ev2.h, k3e_ms[3])
                            if pr >= 1:
                                t.wait_ge(s_nv.h, avn_prev[pr - 1])
                            if psvev is not None:
                                t.wait_ge(s_ev.h, psvev[3])
                            for nch in range(16):
                                t.matmul(pC[:, 0:1], E3T[:, par, nch, :], ones128[:, :],
                                         start=(nch == 0), stop=(nch == 15))
                                mm = t.matmul(pD[:, 0:128], E3T[:, par, nch, :],
                                              Vrm[:, nch, pr * 128:(pr + 1) * 128],
                                              start=(nch == 0), stop=(nch == 15))
                            mm.then_inc(s_mm.h, 1)
                        av_ms = s_mm.bump()
                        prev_av[pr] = av_ms
                        kpath.append(("tensor", s3av_t))

                        eav = s_nv.bump()
                        def av_v(v, av_ms=av_ms, eav=eav):
                            v.wait_ge(s_mm.h, av_ms)
                            v.reciprocal(rs3[:, :], pC[:, 0:1]).then_inc(s_nv.h, 1)
                            v.wait_ge(s_nv.h, eav)
                            v.tensor_scalar(AVn[0:64, :], pD[0:64, 0:64],
                                            rs3[0:64, :], None, op0=ALU.mult)
                            v.tensor_scalar(AVn[64:128, :], pD[64:128, 64:128],
                                            rs3[64:128, :], None,
                                            op0=ALU.mult).then_inc(s_nv.h, 1)
                        avn_ms = s_nv.bump()
                        avn_prev[pr] = avn_ms
                        kpath.append(("vector", av_v))

                        if sub == "p2cN":
                            for eng, fn in newton:
                                k.emit(eng, fn)
                            break
                        if sub == "p2cK":
                            for eng, fn in newton + kpath:
                                k.emit(eng, fn)
                            break
                        merged = []
                        ni, ki = 0, 0
                        while ni < len(newton) or ki < len(kpath):
                            for _ in range(3):
                                if ni < len(newton):
                                    merged.append(newton[ni]); ni += 1
                            if ki < len(kpath):
                                merged.append(kpath[ki]); ki += 1
                        for eng, fn in merged:
                            k.emit(eng, fn)

                        def z_t(t, vok6=vok[6], avn_ms=avn_ms):
                            t.wait_ge(s_nv.h, vok6)
                            t.transpose(pTr2[0:64, 3, 0:64], V_st[0:64, :], id64s[0:64, :])
                            mm = t.transpose(pTr2[64:128, 3, 64:128], V_st[64:128, :],
                                             id64s[64:128, :])
                            mm.then_inc(s_nt.h, 1)
                        v6t_ps = s_nt.bump()
                        k.emit("tensor", z_t)

                        def z_v(v, v6t_ps=v6t_ps):
                            v.wait_ge(s_nt.h, v6t_ps)
                            v.tensor_copy(VT_bd[0:64, 0:64], pTr2[0:64, 3, 0:64])
                            v.tensor_copy(VT_bd[64:128, 64:128],
                                          pTr2[64:128, 3, 64:128]).then_inc(s_nv.h, 1)
                        v6t_ok = s_nv.bump()
                        k.emit("vector", z_v)

                        def z_mm(t, v6t_ok=v6t_ok, avn_ms=avn_ms):
                            t.wait_ge(s_nv.h, v6t_ok)
                            t.wait_ge(s_nv.h, avn_ms)
                            t.matmul(pF_[:, 384:448], VT_bd[:, :], AVn[:, :],
                                     start=True, stop=True).then_inc(s_nt.h, 1)
                        zmm_ms = s_nt.bump()
                        k.emit("tensor", z_mm)

                        def z_ev(v, zmm_ms=zmm_ms):
                            v.wait_ge(s_nt.h, zmm_ms)
                            v.tensor_copy(Zst[:, :], pF_[:, 384:448]).then_inc(s_nv.h, 1)
                        zst_ms = s_nv.bump()
                        k.emit("vector", z_ev)

                        def z_bd(g, zst_ms=zst_ms):
                            g.wait_ge(s_nv.h, zst_ms)
                            g.tensor_copy(Z_bd[0:64, 0:64], Zst[0:64, :])
                            g.tensor_copy(Z_bd[64:128, 64:128],
                                          Zst[64:128, :]).then_inc(s_gp.h, 1)
                        zbd_ms = s_gp.bump()
                        k.emit("gpsimd", z_bd)
                        prev_pf_free = zst_ms
                        prev_tr2_ok = vok[6]

                        sv_ev_ms = {}
                        for nh2 in range(4):
                            def sv_t(t, pr=pr, par=par, nh2=nh2, zbd_ms=zbd_ms, k1T_ms=k1T_ms, sv_ev_ms=sv_ev_ms):
                                if nh2 == 0:
                                    t.wait_ge(s_gp.h, zbd_ms)
                                    t.wait_ge(s_ev2.h, k1T_ms[3])
                                if nh2 >= 2:
                                    t.wait_ge(s_ev.h, sv_ev_ms[nh2 - 2])
                                pX = pE_ if nh2 % 2 == 0 else pC
                                mm = t.matmul(pX[:, :], Z_bd[:, :],
                                              E1T[:, par, nh2 * 512:(nh2 + 1) * 512],
                                              start=True, stop=True)
                                mm.then_inc(s_mm.h, 1)
                            svmm = s_mm.bump()
                            k.emit("tensor", sv_t)

                            def sv_e(v, pr=pr, nh2=nh2, svmm=svmm):
                                v.wait_ge(s_mm.h, svmm)
                                pX = pE_ if nh2 % 2 == 0 else pC
                                v.tensor_copy(SVT[:, pr, nh2 * 512:(nh2 + 1) * 512],
                                              pX[:, :]).then_inc(s_ev.h, 1)
                            sv_ev_ms[nh2] = s_ev.bump()
                            k.emit("vector", sv_e)
                        prev_sv[pr] = svmm
                        prev_kg_last = k3g[3]
                        prev_k3e = dict(k3e_ms)
                        prev_k1T = dict(k1T_ms)
                        prev_svev = dict(sv_ev_ms)

                    sv_done_v = s_ev.v
                    k.run_block()

                    if sub.startswith("p2c"):
                        return nc

                    with ExitStack() as estk:
                        esb = lambda n, s, d: estk.enter_context(nc.sbuf_tensor(k.uname(n), s, d))
                        wprojb = esb("wprojb", [128, 8, DM], BF16)
                        xc2 = esb("xc2", [128, 2, DM], F32)
                        x1t = esb("x1t", [128, 2, DM], F32)
                        stat2 = esb("stat2", [128, 16], F32)
                        mv3 = esb("mv3", [128, 2], F32)
                        rstd2 = esb("rstd2", [128, 1], F32)
                        sjunk2 = esb("sjunk2", [128, 4], F32)

                        k.new_block()
                        gq0 = s_gq.v

                        def e_gq(g):
                            for d in range(8):
                                g.dma_start(wprojb[:, d, :],
                                            wproj_d[d * 128:(d + 1) * 128, :]).then_inc(s_gq.h, 16)
                        s_gq.bump(16 * 8)
                        k.emit("gpsimd", e_gq)

                        pj_mm = {}
                        pj_ap = {}
                        ST_X1 = {}
                        for i in range(16):
                            def e_sync(sync, i=i, b=b):
                                if i >= 2:
                                    sync.wait_ge(s_ln.h, pj_ap[i - 2])
                                sync.dma_start(
                                    xc2[:, i % 2, :],
                                    x_d[b * N + 128 * i:b * N + 128 * (i + 1), :]
                                ).then_inc(s_x.h, 16)
                                if i >= 1:
                                    sync.wait_ge(s_ln.h, pj_ap[i - 1])
                                    sync.dma_start(
                                        x1s[b * N + 128 * (i - 1):b * N + 128 * i, :],
                                        x1t[:, (i - 1) % 2, :]).then_inc(s_st.h, 16)
                            x_at = s_x.bump(16)
                            if i >= 1:
                                ST_X1[i - 1] = s_st.bump(16)
                            k.emit("sync", e_sync)

                            def e_t(t, i=i, gq0=gq0):
                                if i == 0:
                                    t.wait_ge(s_gq.h, gq0 + 128)
                                for dh in range(2):
                                    gI = 2 * i + dh
                                    if gI >= 2:
                                        t.wait_ge(s_ln.h, pj_ap.get(i - 1, 0))
                                    pX = pA if gI % 2 == 0 else pB
                                    for pr2 in range(8):
                                        mm = t.matmul(pX[:, :],
                                                      SVT[:, pr2, i * 128:(i + 1) * 128],
                                                      wprojb[:, pr2, dh * 512:(dh + 1) * 512],
                                                      start=(pr2 == 0), stop=(pr2 == 7))
                                    mm.then_inc(s_mm.h, 1)
                            pj_mm[i] = (s_mm.bump(), s_mm.bump())
                            k.emit("tensor", e_t)

                            ee1, ee2 = s_ln.bump(), s_ln.bump()
                            def e_v1(v, i=i, x_at=x_at, ee1=ee1, ee2=ee2):
                                v.wait_ge(s_mm.h, pj_mm[i][1])
                                v.bn_stats(stat2[:, 0:6], pA[:, :]).then_inc(s_ln.h, 1)
                                v.wait_ge(s_ln.h, ee1)
                                v.bn_stats(stat2[:, 6:12], pB[:, :]).then_inc(s_ln.h, 1)
                                v.wait_ge(s_ln.h, ee2)
                                v.bn_aggr(mv3[:, :], stat2[:, 0:12]).then_inc(s_ln.h, 1)
                            pst = s_ln.bump()
                            k.emit("vector", e_v1)

                            def e_sq(sc, i=i, pst=pst):
                                sc.wait_ge(s_ln.h, pst)
                                sc.activation(sjunk2[:, 0:1], mv3[:, 1:2], AF.Sqrt,
                                              bias=epsc[:, 0:1], scale=1.0).then_inc(s_sq.h, 1)
                            sqm = s_sq.bump()
                            k.emit("scalar", e_sq)

                            ee3 = s_ln.bump()
                            def e_v2(v, i=i, sqm=sqm, x_at=x_at, ee3=ee3):
                                v.wait_ge(s_sq.h, sqm)
                                v.wait_ge(s_x.h, x_at)
                                if i >= 2:
                                    v.wait_ge(s_st.h, ST_X1[i - 2])
                                v.reciprocal(rstd2[:, :], sjunk2[:, 0:1]).then_inc(s_ln.h, 1)
                                v.wait_ge(s_ln.h, ee3)
                                for dh in range(2):
                                    pX = pA if dh == 0 else pB
                                    v.tensor_scalar(x1t[:, i % 2, dh * 512:(dh + 1) * 512],
                                                    pX[:, :], mv3[:, 0:1], rstd2[:, :],
                                                    op0=ALU.subtract, op1=ALU.mult)
                                op = v.tensor_tensor(x1t[:, i % 2, :], x1t[:, i % 2, :],
                                                     xc2[:, i % 2, :], op=ALU.add)
                                op.then_inc(s_ln.h, 1)
                            pj_ap[i] = s_ln.bump()
                            k.emit("vector", e_v2)

                        def e_fin(sync, b=b):
                            sync.wait_ge(s_ln.h, pj_ap[15])
                            sync.dma_start(x1s[b * N + 128 * 15:b * N + 128 * 16, :],
                                           x1t[:, 15 % 2, :]).then_inc(s_st.h, 16)
                        s_st.bump(16)
                        k.emit("sync", e_fin)
                        k.run_block()

        if debug_stage in ("attn", "attn1"):
            k.new_block()
            st_target = s_st.v
            def fin(sync):
                sync.wait_ge(s_st.h, st_target)
            k.emit("sync", fin)
            k.run_block()
            return nc

        with ExitStack() as rstk:
            rsb = lambda n, s, d: rstk.enter_context(nc.sbuf_tensor(k.uname(n), s, d))
            xc1 = rsb("xc1", [128, 2, DM], F32)
            zf = rsb("zf", [128, 2, DM], F32)
            zbb = rsb("zbb", [128, 2, DM], BF16)
            znbt = rsb("znbt", [128, 2, DM], BF16)
            zT32 = rsb("zT32", [128, 2, 8, 128], F32)
            stat3 = rsb("stat3", [128, 16], F32)
            mvA = rsb("mvA", [128, 2], F32)
            mvB = rsb("mvB", [128, 2], F32)
            rstdA = rsb("rstdA", [128, 1], F32)
            rstdB = rsb("rstdB", [128, 1], F32)
            sjk = rsb("sjk", [128, 4], F32)
            affs = rsb("affs", [128, 8], F32)
            max8 = rsb("max8", [128, 8], F32)
            idx8 = rsb("idx8", [128, 8], U32)
            esf = rsb("esf", [128, 1], F32)
            ohf = rsb("ohf", [128, 8], F32)
            ohsel = rsb("ohsel", [128, 8], F32)
            rkred = rsb("rkred", [128, 1], F32)
            of32 = rsb("of32", [128, 1], F32)
            ovr = rsb("ovr", [128, 1], F32)
            ogf = rsb("ogf", [128, 1], F32)
            cnt8 = rsb("cnt8", [1, 8], F32)

            DBG = debug_stage == "dump"
            k.new_block()
            _breg = {}

            def r_breg(g):
                _breg["r"] = g.alloc_register(name="sc_bound")
                g.reg_mov(_breg["r"], E * CAP - 1)
            k.emit("gpsimd", r_breg)
            if wprep_pos < len(wprep_units):
                k.emit("gpsimd", emit_wprep(wprep_units[wprep_pos:]))
                wprep_pos = len(wprep_units)

            r_ap1 = {}
            r_ap2 = {}
            r_tp = {}
            r_ev = {}
            r_sc = {}
            r_ch = {}
            for c in range(NCH):
                def r_sync(sync, c=c):
                    if c >= 2:
                        sync.wait_ge(s_ln.h, r_ap2[c - 2])
                    sync.dma_start(xc1[:, c % 2, :],
                                   x1s[128 * c:128 * (c + 1), :]).then_inc(s_x.h, 16)
                x_at = s_x.bump(16)
                k.emit("sync", r_sync)

                er1, er2 = s_ln.bump(), s_ln.bump()
                def r_v1(v, c=c, x_at=x_at, er1=er1, er2=er2):
                    v.wait_ge(s_x.h, x_at)
                    v.bn_stats(stat3[:, 0:6], xc1[:, c % 2, 0:512]).then_inc(s_ln.h, 1)
                    v.wait_ge(s_ln.h, er1)
                    v.bn_stats(stat3[:, 6:12], xc1[:, c % 2, 512:1024]).then_inc(s_ln.h, 1)
                    v.wait_ge(s_ln.h, er2)
                    v.bn_aggr(mvA[:, :], stat3[:, 0:12]).then_inc(s_ln.h, 1)
                st1 = s_ln.bump()
                k.emit("vector", r_v1)

                def r_s1(sc, c=c, st1=st1):
                    sc.wait_ge(s_ln.h, st1)
                    sc.activation(sjk[:, 0:1], mvA[:, 1:2], AF.Sqrt,
                                  bias=epsc[:, 0:1], scale=1.0).then_inc(s_sq.h, 1)
                sq1 = s_sq.bump()
                k.emit("scalar", r_s1)

                er3, er4, er5 = s_ln.bump(), s_ln.bump(), s_ln.bump()
                def r_v2(v, c=c, sq1=sq1, er3=er3, er4=er4, er5=er5):
                    v.wait_ge(s_sq.h, sq1)
                    if c >= 2:
                        v.wait_ge(s_tp.h, r_tp[c - 2])
                        v.wait_ge(s_st.h, ST_ZB[c - 2])
                    v.reciprocal(rstdA[:, :], sjk[:, 0:1]).then_inc(s_ln.h, 1)
                    v.wait_ge(s_ln.h, er3)
                    v.tensor_scalar(zf[:, c % 2, :], xc1[:, c % 2, :], mvA[:, 0:1],
                                    rstdA[:, :], op0=ALU.subtract, op1=ALU.mult)
                    v.tensor_copy(zbb[:, c % 2, :], zf[:, c % 2, :])
                    v.bn_stats(stat3[:, 0:6], zf[:, c % 2, 0:512]).then_inc(s_ln.h, 1)
                    v.wait_ge(s_ln.h, er4)
                    v.bn_stats(stat3[:, 6:12], zf[:, c % 2, 512:1024]).then_inc(s_ln.h, 1)
                    v.wait_ge(s_ln.h, er5)
                    v.bn_aggr(mvB[:, :], stat3[:, 0:12]).then_inc(s_ln.h, 1)
                r_ap1[c] = s_ln.bump()
                k.emit("vector", r_v2)

                def r_st(sync, c=c):
                    sync.wait_ge(s_ln.h, r_ap1[c])
                    sync.dma_start(zs[128 * c:128 * (c + 1), :],
                                   zbb[:, c % 2, :]).then_inc(s_st.h, 16)
                if c == 0:
                    ST_ZB = {}
                ST_ZB[c] = s_st.bump(16)
                k.emit("sync", r_st)

                def r_s2(sc, c=c):
                    sc.wait_ge(s_ln.h, r_ap1[c])
                    sc.activation(sjk[:, 1:2], mvB[:, 1:2], AF.Sqrt,
                                  bias=epsc[:, 0:1], scale=1.0).then_inc(s_sq.h, 1)
                sq2 = s_sq.bump()
                k.emit("scalar", r_s2)

                er6 = s_ln.bump()
                def r_v3(v, c=c, sq2=sq2, er6=er6):
                    v.wait_ge(s_sq.h, sq2)
                    if c >= 2:
                        v.wait_ge(s_sc.h, r_sc[c - 2])
                    v.reciprocal(rstdB[:, :], sjk[:, 1:2]).then_inc(s_ln.h, 1)
                    v.wait_ge(s_ln.h, er6)
                    v.tensor_scalar(znbt[:, c % 2, :], zf[:, c % 2, :], mvB[:, 0:1],
                                    rstdB[:, :], op0=ALU.subtract, op1=ALU.mult
                                    ).then_inc(s_ln.h, 1)
                r_ap2[c] = s_ln.bump()
                k.emit("vector", r_v3)

                def r_tr(t, c=c):
                    t.wait_ge(s_ln.h, r_ap1[c])
                    if c >= 2:
                        t.wait_ge(s_ev.h, r_ev[c - 2])
                    bks = (pA, pB) if c % 2 == 0 else (pC, pD)
                    for d in range(8):
                        pX = bks[d // 4]
                        mm = t.transpose(pX[:, (d % 4) * 128:(d % 4 + 1) * 128],
                                         zf[:, c % 2, d * 128:(d + 1) * 128],
                                         id128f[:, :])
                    mm.then_inc(s_tp.h, 1)
                r_tp[c] = s_tp.bump()
                k.emit("tensor", r_tr)

                def r_tev(v, c=c):
                    v.wait_ge(s_tp.h, r_tp[c])
                    bks = (pA, pB) if c % 2 == 0 else (pC, pD)
                    for d in range(8):
                        pX = bks[d // 4]
                        mm = v.tensor_copy(zT32[:, c % 2, d, :],
                                           pX[:, (d % 4) * 128:(d % 4 + 1) * 128])
                    mm.then_inc(s_ev.h, 1)
                r_ev[c] = s_ev.bump()
                k.emit("vector", r_tev)

                def r_aff(t, c=c):
                    t.wait_ge(s_ev.h, r_ev[c])
                    if c >= 1:
                        t.wait_ge(s_ln.h, AFF_CONS[c - 1])
                    for d in range(8):
                        mm = t.matmul(pE_[:, 0:8], zT32[:, c % 2, d, :], CT[:, d, :],
                                      start=(d == 0), stop=(d == 7))
                    mm.then_inc(s_mm.h, 1)
                aff_ms = s_mm.bump()
                k.emit("tensor", r_aff)

                ea1, ea2, ea3, ea4 = (s_ln.bump(), s_ln.bump(), s_ln.bump(),
                                      s_ln.bump())
                def r_v4(v, c=c, aff_ms=aff_ms, ea1=ea1, ea2=ea2, ea3=ea3, ea4=ea4):
                    v.wait_ge(s_mm.h, aff_ms)

                    v.tensor_copy(affs[:, :], pE_[:, 0:8]).then_inc(s_ln.h, 1)
                    v.wait_ge(s_ln.h, ea1)
                    v.max(max8[:, :], affs[:, :]).then_inc(s_ln.h, 1)
                    v.wait_ge(s_ln.h, ea2)
                    v.max_index(idx8[:, :], max8[:, :], affs[:, :]).then_inc(s_ln.h, 1)
                    v.wait_ge(s_ln.h, ea3)
                    v.tensor_copy(esf[:, :], idx8[:, 0:1]).then_inc(s_ln.h, 1)
                    v.wait_ge(s_ln.h, ea4)
                    v.tensor_scalar(ohf[:, :], iota8[:, :], esf[:, :], None,
                                    op0=ALU.is_equal).then_inc(s_ln.h, 1)
                if c == 0:
                    AFF_CONS = {}
                AFF_CONS[c] = s_ln.bump()
                k.emit("vector", r_v4)

                def r_sg(v, c=c):
                    v.tensor_copy(mmax_all[:, c:c + 1], max8[:, 0:1]).then_inc(s_ln.h, 1)
                if c == 0:
                    SG_MS = {}
                SG_MS[c] = s_ln.bump()
                k.emit("vector", r_sg)

                def r_rk(t, c=c):
                    t.wait_ge(s_ln.h, AFF_CONS[c])
                    if c >= 1:
                        t.wait_ge(s_rt.h, r_ch[c - 1])
                    t.matmul(pF_[:, 0:8], ltu[:, :], ohf[:, :], start=True, stop=False)
                    t.matmul(pF_[:, 0:8], ones1f[0:1, :], Rrow[0:1, :],
                             start=False, stop=True)
                    t.matmul(pF_[0:1, 8:16], ones128f[:, :], ohf[:, :],
                             start=True, stop=True).then_inc(s_mm.h, 1)
                rk_ms = s_mm.bump()
                k.emit("tensor", r_rk)

                eb = [s_ln.bump() for _ in range(2)]
                eb2a = s_ln.bump()
                eb.append(s_ln.bump())
                eb6 = [s_ln.bump() for _ in range(2)]
                eb += [s_ln.bump() for _ in range(3)]
                def r_v5(v, c=c, rk_ms=rk_ms, eb=eb, eb6=eb6, eb2a=eb2a):
                    v.wait_ge(s_mm.h, rk_ms)
                    v.tensor_tensor(ohsel[:, :], pF_[:, 0:8], ohf[:, :],
                                    op=ALU.mult).then_inc(s_ln.h, 1)
                    v.wait_ge(s_ln.h, eb[0])
                    v.tensor_reduce(rkred[:, :], ohsel[:, :], axis=AX.X,
                                    op=ALU.add).then_inc(s_ln.h, 1)
                    v.wait_ge(s_ln.h, eb[1])
                    v.tensor_scalar(of32[:, :], esf[:, :], float(CAP), rkred[:, :],
                                    op0=ALU.mult, op1=ALU.add)
                    v.tensor_scalar(ovr[:, :], rkred[:, :], -(float(CAP) - 1.0), 0.0,
                                    op0=ALU.add, op1=ALU.max).then_inc(s_ln.h, 1)
                    v.wait_ge(s_ln.h, eb2a)
                    v.tensor_scalar(ovr[:, :], ovr[:, :], 1.0, None,
                                    op0=ALU.min).then_inc(s_ln.h, 1)
                    v.wait_ge(s_ln.h, eb[2])
                    v.tensor_scalar(ogf[:, :], of32[:, :], -1.0, float(E * CAP),
                                    op0=ALU.mult, op1=ALU.add).then_inc(s_ln.h, 1)
                    v.wait_ge(s_ln.h, eb6[0])
                    v.tensor_tensor(ogf[:, :], ogf[:, :], ovr[:, :],
                                    op=ALU.mult).then_inc(s_ln.h, 1)
                    v.wait_ge(s_ln.h, eb6[1])
                    v.tensor_tensor(ogf[:, :], ogf[:, :], of32[:, :],
                                    op=ALU.add).then_inc(s_ln.h, 1)
                    v.wait_ge(s_ln.h, eb[3])
                    v.tensor_copy(og_all[:, c:c + 1], ogf[:, :])
                    v.scalar_tensor_tensor(of32[:, :], ovr[:, :], 1.0e9, of32[:, :],
                                           op0=ALU.mult, op1=ALU.add).then_inc(s_ln.h, 1)
                    v.wait_ge(s_ln.h, eb[4])
                    v.tensor_copy(o_all[:, c:c + 1], of32[:, :])
                    v.tensor_copy(cnt8[0:1, :], pF_[0:1, 8:16]).then_inc(s_ln.h, 1)
                    v.wait_ge(s_ln.h, eb[5])
                    if DBG:
                        v.tensor_copy(dbg_rk[:, 0, c:c + 1], rkred[:, :])
                        v.tensor_copy(dbg_rk[:, 1, c:c + 1], ovr[:, :])
                        v.tensor_copy(dbg_rk[:, 2, c:c + 1], ogf[:, :])
                        v.tensor_copy(dbg_rk[:, 3, c:c + 1], esf[:, :])
                    v.tensor_tensor(Rrow[0:1, :], Rrow[0:1, :], cnt8[0:1, :],
                                    op=ALU.add).then_inc(s_rt.h, 1)
                r_ch[c] = s_rt.bump()
                k.emit("vector", r_v5)

                def r_scat(g, c=c):
                    if c == 0:
                        g.wait_ge(s_st.h, ZFILL_AT)
                    g.wait_ge(s_ln.h, r_ap2[c])
                    g.wait_ge(s_rt.h, r_ch[c])
                    g.indirect_dma_start(
                        out=znbkt, out_offset=IndirectOffsetOnAxis(o_all[:, c:c + 1], 0),
                        in_=znbt[:, c % 2, :], in_offset=None,
                        bounds_check=_breg["r"],
                        oob_is_err=False).then_inc(s_sc.h, 16)
                r_sc[c] = s_sc.bump(16)
                k.emit("gpsimd", r_scat)

            def r_tzero(sync):
                for cz in range((E * CAP + 128) // 128):
                    sync.dma_start(tbkt[128 * cz:128 * (cz + 1), :],
                                   zot[:, :]).then_inc(s_st.h, 16)
            s_st.bump(16 * ((E * CAP + 128) // 128))
            k.emit("sync", r_tzero)

            def r_sig_all(sc):
                sc.wait_ge(s_ln.h, SG_MS[NCH - 1])
                sc.activation(alpha_all[:, :], mmax_all[:, :],
                              AF.Sigmoid).then_inc(s_sq.h, 1)
            s_sq.bump()
            k.emit("scalar", r_sig_all)

            en1, en2 = s_ln.bump(), s_ln.bump()
            def r_nse(v, en1=en1, en2=en2):
                v.wait_ge(s_rt.h, r_ch[NCH - 1])
                v.tensor_scalar(nse_f[0:1, :], Rrow[0:1, :], 255.0, 1.0 / 256.0,
                                op0=ALU.add, op1=ALU.mult).then_inc(s_ln.h, 1)
                v.wait_ge(s_ln.h, en1)
                v.tensor_scalar(nse_f[0:1, :], nse_f[0:1, :], 0.25, None,
                                op0=ALU.add).then_inc(s_ln.h, 1)
                v.wait_ge(s_ln.h, en2)
                v.tensor_copy(nse_i[0:1, :], nse_f[0:1, :]).then_inc(s_rt.h, 1)
            NSE_MS = s_rt.bump()
            k.emit("vector", r_nse)
            k.run_block()

        with ExitStack() as fstk:
            fsb = lambda n, s, d: fstk.enter_context(nc.sbuf_tensor(k.uname(n), s, d))
            W1T = fsb("W1T", [128, 8, FF], BF16)
            W2T = fsb("W2T", [128, NFC, DM], BF16)
            znl = fsb("znl", [128, 2, 2, DM], BF16)
            znTt = fsb("znTt", [128, 2, 8, 256], BF16)
            hTt = fsb("hTt", [128, 2, 256], BF16)
            t_sb = fsb("t_sb", [128, 2, 2, DM], BF16)
            b2row = fsb("b2row", [1, 2, DM], BF16)
            PT4 = [(pA, pB), (pC, pD)]

            k.new_block()
            REGS = {}

            def mk_setup(eng_name):
                def setup(e):
                    REGS[eng_name] = (e.alloc_register(name="rns" + eng_name),
                                      e.alloc_register(name="rs" + eng_name))
                return setup
            for en_ in ("sync", "tensor", "vector", "scalar"):
                k.emit(en_, mk_setup(en_))

            def cond(e_obj, eng_name, ex, s):
                rns, rs = REGS[eng_name]
                if s == 0:
                    e_obj.reg_load(rns, nse_i[0:1, ex:ex + 1])
                e_obj.reg_mov(rs, s)
                return e_obj.If_lt(rs, rns)

            T_ST = {}
            B2_AT = {}
            W_AT = {}
            ZN_AT = {}

            for ex in range(E):
                def f_b2(g, ex=ex):
                    if ex >= 2:
                        g.wait_ge(s_w2.h, NFC * NSC * (ex - 1))
                    g.dma_start(b2row[:, ex % 2, :],
                                b2_d[ex:ex + 1, :]).then_inc(s_gq.h, 16)
                B2_AT[ex] = s_gq.bump(16)
                k.emit("gpsimd", f_b2)

                def f_wload(sync, ex=ex):
                    if ex >= 1:
                        sync.wait_ge(s_w1.h, NFC * NSC * ex)
                        sync.wait_ge(s_w2.h, NFC * NSC * ex)
                    for s in range(2):
                        sg = NSC * ex + s
                        if sg >= 2:
                            sync.wait_ge(s_trg.h, 2 * (sg - 1))
                        for gi in range(2):
                            sync.dma_start(
                                znl[:, sg % 2, gi, :],
                                znbkt[ex * CAP + s * 256 + gi * 128:
                                      ex * CAP + s * 256 + gi * 128 + 128, :]
                            ).then_inc(s_zn.h, 16)
                    for d in range(8):
                        sync.dma_start(W1T[:, d, :], w1b[ex, :, d * 128:(d + 1) * 128],
                                       transpose=True).then_inc(s_w.h, 16)
                    for f in range(NFC):
                        sync.dma_start(W2T[:, f, :], w2b[ex, :, f * 128:(f + 1) * 128],
                                       transpose=True).then_inc(s_w.h, 16)
                s_w.bump(16 * 40)
                ZN_AT[NSC * ex] = s_zn.bump(32)
                ZN_AT[NSC * ex + 1] = s_zn.bump(32)
                W_AT[ex] = s_w.v
                W_BASE = W_AT[ex] - 16 * 40
                if ex == 0:
                    W_BASES = {}
                W_BASES[ex] = W_BASE
                k.emit("sync", f_wload)

                def f_tpro(t, ex=ex):
                    t.wait_ge(s_w.h, W_BASES[ex] + 16 * 8)
                    for gi in range(2):
                        sg = NSC * ex
                        grp = 2 * sg + gi
                        pT = pTrb if grp % 2 == 0 else pTr2
                        t.wait_ge(s_zn.h, ZN_AT[sg])
                        if grp >= 2:
                            t.wait_ge(s_znT.h, grp - 1)
                        for d in range(8):
                            mm = t.transpose(pT[:, d, :],
                                             znl[:, sg % 2, gi, d * 128:(d + 1) * 128],
                                             id128[:, :])
                        mm.then_inc(s_trg.h, 1)
                s_trg.bump(2)
                k.emit("tensor", f_tpro)

                for s in range(NSC):
                    sg = NSC * ex + s

                    has_pre = s + 2 < NSC

                    def f_sync(sync, ex=ex, s=s, sg=sg, has_pre=has_pre):
                        with cond(sync, "sync", ex, s):
                            if has_pre:
                                sync.wait_ge(s_trg.h, 2 * (sg + 1))
                                for gi in range(2):
                                    sync.dma_start(
                                        znl[:, sg % 2, gi, :],
                                        znbkt[ex * CAP + (s + 2) * 256 + gi * 128:
                                              ex * CAP + (s + 2) * 256 + gi * 128 + 128, :]
                                    ).then_inc(s_zn.h, 16)
                            sync.wait_ge(s_te.h, sg + 1)
                            for gi in range(2):
                                sync.dma_start(
                                    tbkt[ex * CAP + s * 256 + gi * 128:
                                         ex * CAP + s * 256 + gi * 128 + 128, :],
                                    t_sb[:, sg % 2, gi, :]).then_inc(s_st.h, 16)
                        with sync.Else():
                            if has_pre:
                                sync.sem_inc(s_zn.h, 32)
                            sync.sem_inc(s_st.h, 32)
                    if has_pre:
                        ZN_AT[sg + 2] = s_zn.bump(32)
                    T_ST[sg] = s_st.bump(32)
                    k.emit("sync", f_sync)

                    has_tr = s + 1 < NSC

                    def f_tensor(t, ex=ex, s=s, sg=sg, has_tr=has_tr):
                        def w2_group(kk):
                            ff = kk % NFC
                            t.wait_ge(s_he.h, kk + 1)
                            if s == 0:
                                t.wait_ge(s_w.h, W_BASES[ex] + 16 * (8 + ff + 1))
                            if ff == 0:
                                if sg >= 1:
                                    t.wait_ge(s_te.h, sg)
                                t.wait_ge(s_gq.h, B2_AT[ex])
                                for sg2 in range(2):
                                    for dh in range(2):
                                        t.matmul(PT4[sg2][dh][:, :], ones1r[:, :],
                                                 b2row[0:1, ex % 2, dh * 512:(dh + 1) * 512],
                                                 start=True, stop=False)
                            for sg2 in range(2):
                                for dh in range(2):
                                    mm = t.matmul(PT4[sg2][dh][:, :],
                                                  hTt[:, kk % 2, sg2 * 128:(sg2 + 1) * 128],
                                                  W2T[:, ff, dh * 512:(dh + 1) * 512],
                                                  start=False, stop=(ff == NFC - 1))
                            mm.then_inc(s_w2.h, 1)

                        def tr_group(sgp, gi):
                            grp = 2 * sgp + gi
                            pT = pTrb if grp % 2 == 0 else pTr2
                            t.wait_ge(s_zn.h, ZN_AT[sgp])
                            t.wait_ge(s_znT.h, grp - 1)
                            for d in range(8):
                                mm = t.transpose(pT[:, d, :],
                                                 znl[:, sgp % 2, gi, d * 128:(d + 1) * 128],
                                                 id128[:, :])
                            mm.then_inc(s_trg.h, 1)

                        with cond(t, "tensor", ex, s):
                            t.wait_ge(s_znT.h, 2 * (sg + 1))
                            for f in range(NFC):
                                kk = NFC * sg + f
                                pw = pE_ if kk % 2 == 0 else pF_
                                if kk >= 2:
                                    t.wait_ge(s_he.h, kk - 1)
                                for d in range(8):
                                    mm = t.matmul(pw[:, 0:256],
                                                  W1T[:, d, f * 128:(f + 1) * 128],
                                                  znTt[:, sg % 2, d, :],
                                                  start=(d == 0), stop=(d == 7))
                                mm.then_inc(s_w1.h, 1)
                                if has_tr and f == 26:
                                    tr_group(sg + 1, 0)
                                if has_tr and f == 28:
                                    tr_group(sg + 1, 1)
                                if f >= 1:
                                    w2_group(kk - 1)
                            w2_group(NFC * sg + NFC - 1)
                        with t.Else():
                            t.sem_inc(s_w1.h, NFC)
                            t.sem_inc(s_w2.h, NFC)
                            if has_tr:
                                t.sem_inc(s_trg.h, 2)
                    s_w1.bump(NFC)
                    s_w2.bump(NFC)
                    if has_tr:
                        s_trg.bump(2)
                    k.emit("tensor", f_tensor)

                    def f_vector(v, ex=ex, s=s, sg=sg):
                        with cond(v, "vector", ex, s):
                            if sg >= 2:
                                v.wait_ge(s_w1.h, NFC * (sg - 1))
                            for gi in range(2):
                                grp = 2 * sg + gi
                                pT = pTrb if grp % 2 == 0 else pTr2
                                v.wait_ge(s_trg.h, grp + 1)
                                for d in range(8):
                                    op = v.tensor_scalar(
                                        znTt[:, sg % 2, d, gi * 128:(gi + 1) * 128],
                                        pT[:, d, :], GT[:, d, ex:ex + 1],
                                        BT[:, d, ex:ex + 1],
                                        op0=ALU.mult, op1=ALU.add)
                                op.then_inc(s_znT.h, 1)
                        with v.Else():
                            v.sem_inc(s_znT.h, 2)
                    s_znT.bump(2)
                    k.emit("vector", f_vector)

                    def f_scalar(sc, ex=ex, s=s, sg=sg):
                        with cond(sc, "scalar", ex, s):
                            for f in range(NFC):
                                kk = NFC * sg + f
                                pw = pE_ if kk % 2 == 0 else pF_
                                sc.wait_ge(s_w1.h, kk + 1)
                                if kk >= 2:
                                    sc.wait_ge(s_w2.h, kk - 1)
                                sc.activation(hTt[:, kk % 2, :], pw[:, 0:256], AF.Relu,
                                              bias=B1T[:, f, ex:ex + 1],
                                              scale=1.0).then_inc(s_he.h, 1)
                            sc.wait_ge(s_w2.h, NFC * (sg + 1))
                            if sg >= 2:
                                sc.wait_ge(s_st.h, T_ST[sg - 2])
                            for sg2 in range(2):
                                for dh in range(2):
                                    op = sc.copy(t_sb[:, sg % 2, sg2, dh * 512:(dh + 1) * 512],
                                                 PT4[sg2][dh][:, :])
                            op.then_inc(s_te.h, 1)
                        with sc.Else():
                            sc.sem_inc(s_he.h, NFC)
                            sc.sem_inc(s_te.h, 1)
                    s_he.bump(NFC)
                    s_te.bump(1)
                    k.emit("scalar", f_scalar)

            k.run_block()

        with ExitStack() as estk2:
            qsb = lambda n, s, d: estk2.enter_context(nc.sbuf_tensor(k.uname(n), s, d))
            xq = qsb("xq", [128, 2, DM], F32)
            zq = qsb("zq", [128, 2, DM], BF16)
            tq = qsb("tq", [128, 2, DM], BF16)
            fq = qsb("fq", [128, 2, DM], F32)
            oq = qsb("oq", [128, 2, DM], F32)
            stat4 = qsb("stat4", [128, 16], F32)
            mv4 = qsb("mv4", [128, 2], F32)
            rstd4 = qsb("rstd4", [128, 1], F32)
            sjk4 = qsb("sjk4", [128, 4], F32)

            k.new_block()
            q_ap = {}
            q_st = {}
            for c in range(NCH):
                def q_sync(sync, c=c):
                    if c >= 2:
                        sync.wait_ge(s_ln.h, q_ap[c - 2])
                    sync.dma_start(xq[:, c % 2, :],
                                   x1s[128 * c:128 * (c + 1), :]).then_inc(s_x.h, 16)
                    sync.dma_start(zq[:, c % 2, :],
                                   zs[128 * c:128 * (c + 1), :]).then_inc(s_x.h, 16)
                x_at = s_x.bump(32)
                k.emit("sync", q_sync)

                def q_g(g, c=c):
                    if c >= 2:
                        g.wait_ge(s_ln.h, q_ap[c - 2])
                    g.indirect_dma_start(
                        out=tq[:, c % 2, :], out_offset=None, in_=tbkt,
                        in_offset=IndirectOffsetOnAxis(og_all[:, c:c + 1], 0)
                    ).then_inc(s_sc.h, 16)
                t_at = s_sc.bump(16)
                k.emit("gpsimd", q_g)

                eq0, eq1, eq2 = s_ln.bump(), s_ln.bump(), s_ln.bump()
                def q_v1(v, c=c, x_at=x_at, t_at=t_at, eq0=eq0, eq1=eq1, eq2=eq2):
                    v.wait_ge(s_x.h, x_at)
                    v.wait_ge(s_sc.h, t_at)
                    v.scalar_tensor_tensor(fq[:, c % 2, :], tq[:, c % 2, :],
                                           alpha_all[:, c:c + 1], zq[:, c % 2, :],
                                           op0=ALU.mult, op1=ALU.add).then_inc(s_ln.h, 1)
                    v.wait_ge(s_ln.h, eq0)
                    v.bn_stats(stat4[:, 0:6], fq[:, c % 2, 0:512]).then_inc(s_ln.h, 1)
                    v.wait_ge(s_ln.h, eq1)
                    v.bn_stats(stat4[:, 6:12], fq[:, c % 2, 512:1024]).then_inc(s_ln.h, 1)
                    v.wait_ge(s_ln.h, eq2)
                    v.bn_aggr(mv4[:, :], stat4[:, 0:12]).then_inc(s_ln.h, 1)
                stq = s_ln.bump()
                k.emit("vector", q_v1)

                def q_s(sc, c=c, stq=stq):
                    sc.wait_ge(s_ln.h, stq)
                    sc.activation(sjk4[:, 0:1], mv4[:, 1:2], AF.Sqrt,
                                  bias=epsc[:, 0:1], scale=1.0).then_inc(s_sq.h, 1)
                sqq = s_sq.bump()
                k.emit("scalar", q_s)

                eq3 = s_ln.bump()
                def q_v2(v, c=c, sqq=sqq, eq3=eq3):
                    v.wait_ge(s_sq.h, sqq)
                    if c >= 2:
                        v.wait_ge(s_st.h, q_st[c - 2])
                    v.reciprocal(rstd4[:, :], sjk4[:, 0:1]).then_inc(s_ln.h, 1)
                    v.wait_ge(s_ln.h, eq3)
                    v.tensor_scalar(oq[:, c % 2, :], fq[:, c % 2, :], mv4[:, 0:1],
                                    rstd4[:, :], op0=ALU.subtract, op1=ALU.mult)
                    v.tensor_tensor(oq[:, c % 2, :], oq[:, c % 2, :], xq[:, c % 2, :],
                                    op=ALU.add).then_inc(s_ln.h, 1)
                q_ap[c] = s_ln.bump()
                k.emit("vector", q_v2)

                def q_store(sync, c=c):
                    sync.wait_ge(s_ln.h, q_ap[c])
                    sync.dma_start(out_d[128 * c:128 * (c + 1), :],
                                   oq[:, c % 2, :]).then_inc(s_st.h, 16)
                q_st[c] = s_st.bump(16)
                k.emit("sync", q_store)

            fin_target = s_st.v
            def q_fin(sync):
                sync.wait_ge(s_st.h, fin_target)
            k.emit("sync", q_fin)
            k.run_block()

            if debug_stage == "dump":
                k.new_block()
                base = s_x.v

                def dmp(sync, base=base):
                    n = 0
                    pairs = ([(d_zn, znbkt, E * CAP // 128)] +
                             [(d_t, tbkt, E * CAP // 128 + 1)] +
                             [(d_z, zs, ROWS // 128)])
                    for dst, srcb, nch2 in pairs:
                        for cc2 in range(nch2):
                            if n >= 2:
                                sync.wait_ge(s_x.h, base + 16 * (2 * (n - 1)))
                            sync.dma_start(tq[:, n % 2, :],
                                           srcb[128 * cc2:128 * (cc2 + 1), :]
                                           ).then_inc(s_x.h, 16)
                            sync.wait_ge(s_x.h, base + 32 * n + 16)
                            sync.dma_start(dst[128 * cc2:128 * (cc2 + 1), :],
                                           tq[:, n % 2, :]).then_inc(s_x.h, 16)
                            n += 1
                    sync.dma_start(d_o[:, 0, :], o_all[:, :]).then_inc(s_x.h, 16)
                    sync.dma_start(d_o[:, 1, :], og_all[:, :]).then_inc(s_x.h, 16)
                    sync.dma_start(d_al[:, :], alpha_all[:, :]).then_inc(s_x.h, 16)
                    sync.dma_start(d_rk[:, :, :], dbg_rk[:, :, :]).then_inc(s_x.h, 16)
                    sync.wait_ge(s_x.h, base + 32 * n + 64)
                ntotal = E * CAP // 128 + E * CAP // 128 + 1 + ROWS // 128
                s_x.bump(32 * ntotal + 64)
                k.emit("sync", dmp)
                k.run_block()

    return nc


def kernel(**inputs):
    x = np.ascontiguousarray(inputs["L0_attn_x"], np.float32)
    nc = build_program()
    in_maps = []
    for c in range(8):
        shard = {
            "x": np.ascontiguousarray(x[4 * c:4 * (c + 1)].reshape(ROWS, DM)),
            "wqkv": np.ascontiguousarray(inputs["L0_attn_w_qkv"], np.float32),
            "wproj": np.ascontiguousarray(inputs["L0_attn_w_proj"], np.float32),
            "cent": np.ascontiguousarray(inputs["L0_ffn_centroids"], np.float32),
            "lng": np.ascontiguousarray(inputs["L0_ffn_ln_g"], np.float32),
            "lnb": np.ascontiguousarray(inputs["L0_ffn_ln_b"], np.float32),
            "w1": np.ascontiguousarray(inputs["L0_ffn_w1"], np.float32),
            "b1": np.ascontiguousarray(inputs["L0_ffn_b1"], np.float32),
            "w2": np.ascontiguousarray(inputs["L0_ffn_w2"], np.float32),
            "b2": np.ascontiguousarray(inputs["L0_ffn_b2"], np.float32),
        }
        in_maps.append(shard)
    res = run_bass_kernel_spmd(nc, in_maps, core_ids=list(range(8)))
    out = np.concatenate([r["out"].reshape(4, N, DM) for r in res.results], axis=0)
    return out


# version 13
# speedup vs baseline: 1.2620x; 1.1177x over previous
import numpy as np
import ml_dtypes
from contextlib import ExitStack

import concourse.bass as bass
import concourse.mybir as mybir
from concourse.bass_utils import run_bass_kernel_spmd
from bass_rust import IndirectOffsetOnAxis

F32 = mybir.dt.float32
BF16 = mybir.dt.bfloat16
U32 = mybir.dt.uint32
AF = mybir.ActivationFunctionType
ALU = mybir.AluOpType
AX = mybir.AxisListType

B = 4
N = 2048
DM = 1024
H = 16
PAIRS = 8
HD = 64
MLM = 64
SEG = 32
ROWS = B * N
NCH = ROWS // 128
E = 8
FF = 4096
CAP = 4096
NSC = CAP // 256
NFC = FF // 128
EPS = 1e-5

ENGS = ("sync", "tensor", "vector", "scalar", "gpsimd")


class Sem:
    def __init__(self, stack, nc, name):
        self.h = stack.enter_context(nc.semaphore(name))
        self.v = 0

    def bump(self, k=1):
        self.v += k
        return self.v


class MK:
    def __init__(self, nc, stack, debug_stage=None):
        self.nc = nc
        self.stack = stack
        self.debug_stage = debug_stage
        self.q = None

    def new_block(self):
        self.q = {k: [] for k in ENGS}

    def emit(self, eng, fn):
        self.q[eng].append(fn)

    def run_block(self):
        q = self.q
        with self.nc.Block() as blk:
            for eng in ENGS:
                fns = q[eng]
                if not fns:
                    continue
                def body(e, fns=fns):
                    for fn in fns:
                        fn(e)
                getattr(blk, eng)(body)
        self.q = None

    _uid = 0

    def uname(self, name):
        MK._uid += 1
        return f"{name}_u{MK._uid}"

    def sb(self, name, shape, dt):
        return self.stack.enter_context(self.nc.sbuf_tensor(name, shape, dt))

    def ps(self, name, shape, dt):
        return self.stack.enter_context(self.nc.psum_tensor(name, shape, dt))

    def sem(self, name):
        return Sem(self.stack, self.nc, name)


def build_program(debug_stage=None):
    nc = bass.Bass("TRN2", target_bir_lowering=False, detect_race_conditions=False)

    x_d = nc.dram_tensor("x", [ROWS, DM], F32, kind="ExternalInput").ap()
    wqkv_d = nc.dram_tensor("wqkv", [DM, 3 * DM], F32, kind="ExternalInput").ap()
    wproj_d = nc.dram_tensor("wproj", [DM, DM], F32, kind="ExternalInput").ap()
    cent_d = nc.dram_tensor("cent", [E, DM], F32, kind="ExternalInput").ap()
    lng_d = nc.dram_tensor("lng", [E, DM], F32, kind="ExternalInput").ap()
    lnb_d = nc.dram_tensor("lnb", [E, DM], F32, kind="ExternalInput").ap()
    w1_d = nc.dram_tensor("w1", [E, FF, DM], F32, kind="ExternalInput").ap()
    b1_d = nc.dram_tensor("b1", [E, FF], F32, kind="ExternalInput").ap()
    w2_d = nc.dram_tensor("w2", [E, DM, FF], F32, kind="ExternalInput").ap()
    b2_d = nc.dram_tensor("b2", [E, DM], F32, kind="ExternalInput").ap()
    out_d = nc.dram_tensor("out", [ROWS, DM], F32, kind="ExternalOutput").ap()

    dbg_d = None
    if debug_stage in ("p2abd",):
        dbg_d = nc.dram_tensor("dbg", [128, 4, 2048], BF16, kind="ExternalOutput").ap()
    x1_kind = "ExternalOutput" if debug_stage == "attn" else "Internal"
    x1s = nc.dram_tensor("x1s", [ROWS, DM], F32, kind=x1_kind).ap()
    znbkt = nc.dram_tensor("znbkt", [E * CAP, DM], BF16, kind="Internal").ap()
    rmtab = nc.dram_tensor("rmtab", [E * CAP + 128, 4], F32, kind="Internal").ap()
    if debug_stage == "dump":
        d_zn = nc.dram_tensor("d_zn", [E * CAP, DM], BF16, kind="ExternalOutput").ap()
        d_o = nc.dram_tensor("d_o", [128, 2, NCH], U32, kind="ExternalOutput").ap()
        d_rk = nc.dram_tensor("d_rk", [128, 4, NCH], F32, kind="ExternalOutput").ap()
    if debug_stage == "epi":
        d_oq = nc.dram_tensor("d_oq", [128, 2, DM], F32, kind="ExternalOutput").ap()
        d_fq = nc.dram_tensor("d_fq", [128, 2, DM], F32, kind="ExternalOutput").ap()
        d_xg = nc.dram_tensor("d_xg", [128, 2, DM], F32, kind="ExternalOutput").ap()
        d_ts = nc.dram_tensor("d_ts", [128, 2, 2, DM], BF16, kind="ExternalOutput").ap()
        d_rm2 = nc.dram_tensor("d_rm2", [128, 2, 2, 4], F32, kind="ExternalOutput").ap()
        d_al2 = nc.dram_tensor("d_al2", [128, 4], F32, kind="ExternalOutput").ap()
        d_ix = nc.dram_tensor("d_ix", [128, 4], U32, kind="ExternalOutput").ap()
        d_mv = nc.dram_tensor("d_mv", [128, 2, 2, 2], F32, kind="ExternalOutput").ap()
        d_zl = nc.dram_tensor("d_zl", [128, 3, 2, DM], BF16, kind="ExternalOutput").ap()
    w1b = nc.dram_tensor("w1b", [E, FF, DM], BF16, kind="Internal").ap()
    w2b = nc.dram_tensor("w2b", [E, DM, FF], BF16, kind="Internal").ap()

    with ExitStack() as stack:
        k = MK(nc, stack, debug_stage)
        sb, ps, sem = k.sb, k.ps, k.sem

        id128 = sb("id128", [128, 128], BF16)
        id64s = sb("id64s", [128, 64], BF16)
        id128f = sb("id128f", [128, 128], F32)
        ltu = sb("ltu", [128, 128], F32)
        ones128f = sb("ones128f", [128, 1], F32)
        iota8 = sb("iota8", [128, 8], F32)
        ones128 = sb("ones128", [128, 1], BF16)
        ones1r = sb("ones1r", [1, 128], BF16)
        ones1f = sb("ones1f", [1, 128], F32)
        sel0 = sb("sel0", [1, 128], F32)
        sel1 = sb("sel1", [1, 128], F32)
        cIs = sb("cIs", [128, 3, 64], F32)
        CT = sb("CT", [128, 8, E], F32)
        GT = sb("GT", [128, 8, E], F32)
        BT = sb("BT", [128, 8, E], F32)
        B1T = sb("B1T", [128, NFC, E], F32)
        dbg_rk = sb("dbg_rk", [128, 4, NCH], F32)
        o_all = sb("o_all", [128, NCH], U32)
        rmsent = sb("rmsent", [128, 4], F32)
        riota = sb("riota", [128, NCH], F32)
        Rrow = sb("Rrow", [1, E], F32)
        nse_f = sb("nse_f", [1, E], F32)
        nse_i = sb("nse_i", [1, E], mybir.dt.int32)
        epsc = sb("epsc", [128, 1], F32)
        zot = sb("zot", [128, DM], BF16)

        pA = ps("pA", [128, 512], F32)
        pB = ps("pB", [128, 512], F32)
        pC = ps("pC", [128, 512], F32)
        pD = ps("pD", [128, 512], F32)
        pE_ = ps("pE", [128, 512], F32)
        pF_ = ps("pF", [128, 512], F32)
        pTrb = ps("pTrb", [128, 8, 128], BF16)
        pTr2 = ps("pTr2", [128, 8, 128], BF16)

        s_x = sem("s_x")
        s_st = sem("s_st")
        s_gq = sem("s_gq")
        s_wp = sem("s_wp")
        s_ln = sem("s_ln")
        s_sq = sem("s_sq")
        s_tp = sem("s_tp")
        s_ev = sem("s_ev")
        s_ev2 = sem("s_ev2")
        s_mm = sem("s_mm")
        s_nt = sem("s_nt")
        s_nv = sem("s_nv")
        s_gp = sem("s_gp")
        s_rt = sem("s_rt")
        s_w = sem("s_w")
        s_zn = sem("s_zn")
        s_trg = sem("s_trg")
        s_znT = sem("s_znT")
        s_w1 = sem("s_w1")
        s_he = sem("s_he")
        s_w2 = sem("s_w2")
        s_te = sem("s_te")
        s_sc = sem("s_sc")

        wprep_units = [("w1", e) for e in range(E)] + [("w2", e) for e in range(E)]

        def emit_wprep(units):
            def body(g, units=units):
                for kind, e in units:
                    if kind == "w1":
                        g.dma_start(w1b[e, :, :], w1_d[e, :, :]).then_inc(s_wp.h, 16)
                    else:
                        g.dma_start(w2b[e, :, :], w2_d[e, :, :]).then_inc(s_wp.h, 16)
            s_wp.bump(16 * len(units))
            return body

        p0stk = ExitStack()
        rowbuf = p0stk.enter_context(nc.sbuf_tensor("rowbuf", [1, 4 * DM], F32))
        k.new_block()

        def p0_g(g):
            def ident_into(t, val=1.0, halfsize=None):
                g.memset(t, 0.0)
                if halfsize is None:
                    sq = t.shape[-1]
                    g.affine_select(out=t, in_=t, compare_op=ALU.not_equal, fill=val,
                                    base=0, pattern=[[-1, sq]], channel_multiplier=1)
            ident_into(id128[:, :])
            ident_into(id128f[:, :])
            g.memset(id64s[:, :], 0.0)
            for hh in range(2):
                g.affine_select(out=id64s[hh * 64:(hh + 1) * 64, :],
                                in_=id64s[hh * 64:(hh + 1) * 64, :],
                                compare_op=ALU.not_equal, fill=1.0, base=0,
                                pattern=[[-1, 64]], channel_multiplier=1)
            g.memset(ltu[:, :], 1.0)
            g.affine_select(out=ltu[:, :], in_=ltu[:, :], compare_op=ALU.is_ge,
                            fill=0.0, base=-1, pattern=[[1, 128]], channel_multiplier=-1)
            g.iota(iota8[:, :], [[1, 8]], base=0, channel_multiplier=0,
                   allow_small_or_imprecise_dtypes=True)
            g.memset(rmsent[:, :], 1.0e9)
            g.iota(riota[:, :], [[128, NCH]], base=0, channel_multiplier=1,
                   allow_small_or_imprecise_dtypes=True)
            g.memset(ones128[:, :], 1.0)
            g.memset(ones128f[:, :], 1.0)
            g.memset(ones1r[:, :], 1.0)
            g.memset(ones1f[:, :], 1.0)
            g.memset(sel0[:, :], 1.0)
            g.affine_select(out=sel0[:, :], in_=sel0[:, :], compare_op=ALU.is_ge,
                            fill=0.0, base=63, pattern=[[-1, 128]], channel_multiplier=0)
            g.memset(sel1[:, :], 1.0)
            g.affine_select(out=sel1[:, :], in_=sel1[:, :], compare_op=ALU.is_ge,
                            fill=0.0, base=-64, pattern=[[1, 128]], channel_multiplier=0)
            g.memset(cIs[:, :, :], 0.0)
            for j, c in enumerate([7.0, 15.0, 13.0]):
                for hh in range(2):
                    g.affine_select(out=cIs[hh * 64:(hh + 1) * 64, j, :],
                                    in_=cIs[hh * 64:(hh + 1) * 64, j, :],
                                    compare_op=ALU.not_equal, fill=c, base=0,
                                    pattern=[[-1, 64]], channel_multiplier=1)
            g.memset(Rrow[:, :], 0.0)
            g.memset(epsc[:, :], EPS)
            g.memset(zot[:, :], 0.0)
            g.nop().then_inc(s_gp.h, 1)
        P0G = s_gp.bump()
        k.emit("gpsimd", p0_g)

        p0_plan = []
        ev_ms = []
        for e in range(E):
            p0_plan.append(("4rows", e))
        for seg in range(1, 4):
            for e in range(E):
                p0_plan.append(("b1seg", e, seg))

        def p0_sync(sync):
            for it, item in enumerate(p0_plan):
                if it >= 1:
                    sync.wait_ge(s_ev.h, it)
                if item[0] == "4rows":
                    e = item[1]
                    sync.dma_start(rowbuf[0:1, 0:DM], cent_d[e:e + 1, :]).then_inc(s_x.h, 16)
                    sync.dma_start(rowbuf[0:1, DM:2 * DM], lng_d[e:e + 1, :]).then_inc(s_x.h, 16)
                    sync.dma_start(rowbuf[0:1, 2 * DM:3 * DM], lnb_d[e:e + 1, :]).then_inc(s_x.h, 16)
                    sync.dma_start(rowbuf[0:1, 3 * DM:4 * DM], b1_d[e:e + 1, 0:1024]).then_inc(s_x.h, 16)
                else:
                    e, seg = item[1], item[2]
                    sync.dma_start(rowbuf[0:1, 0:DM],
                                   b1_d[e:e + 1, seg * 1024:(seg + 1) * 1024]).then_inc(s_x.h, 16)
        nload = 0
        p0_loadcnt = []
        for item in p0_plan:
            nload += 4 if item[0] == "4rows" else 1
            p0_loadcnt.append(nload)
        s_x.bump(16 * nload)

        def p0_t(t):
            for it, item in enumerate(p0_plan):
                t.wait_ge(s_x.h, 16 * p0_loadcnt[it])
                if it >= 2:
                    t.wait_ge(s_ev.h, it - 1)
                pX = pA if it % 2 == 0 else pB
                nr = 4 if item[0] == "4rows" else 1
                for c in range(8):
                    for r in range(nr):
                        mm = t.matmul(pX[:, 4 * c + r:4 * c + r + 1],
                                      rowbuf[0:1, r * DM + c * 128:r * DM + (c + 1) * 128],
                                      ones1f[0:1, 0:1], start=True, stop=True)
                mm.then_inc(s_mm.h, 1)
        p0_mm0 = s_mm.v
        s_mm.bump(len(p0_plan))

        def p0_v(v):
            for it, item in enumerate(p0_plan):
                v.wait_ge(s_mm.h, p0_mm0 + it + 1)
                pX = pA if it % 2 == 0 else pB
                if item[0] == "4rows":
                    e = item[1]
                    for c in range(8):
                        v.tensor_copy(CT[:, c, e:e + 1], pX[:, 4 * c:4 * c + 1])
                        v.tensor_copy(GT[:, c, e:e + 1], pX[:, 4 * c + 1:4 * c + 2])
                        v.tensor_copy(BT[:, c, e:e + 1], pX[:, 4 * c + 2:4 * c + 3])
                        op = v.tensor_copy(B1T[:, c, e:e + 1], pX[:, 4 * c + 3:4 * c + 4])
                    op.then_inc(s_ev.h, 1)
                else:
                    e, seg = item[1], item[2]
                    for c in range(8):
                        op = v.tensor_copy(B1T[:, seg * 8 + c, e:e + 1],
                                           pX[:, 4 * c:4 * c + 1])
                    op.then_inc(s_ev.h, 1)
        s_ev.bump(len(p0_plan))

        k.emit("sync", p0_sync)
        k.emit("tensor", p0_t)
        k.emit("vector", p0_v)
        k.run_block()
        p0stk.close()

        if debug_stage == "p0":
            return nc

        wprep_pos = 0
        CTbf = None

        nbatch = 1 if debug_stage in ("p2ab", "p2abd", "p2c", "attn1") else B
        for b in range(nbatch):
            with ExitStack() as bstk:
                bsb = lambda n, s, d: bstk.enter_context(nc.sbuf_tensor(k.uname(n), s, d))
                QKT = bsb("QKT", [128, 16, N], BF16)
                Vrm = bsb("Vrm", [128, 16, DM], BF16)
                xnlT = bsb("xnlT", [128, 8, MLM], BF16)
                QlKlT = bsb("QlKlT", [128, 16, MLM], BF16)

                with ExitStack() as astk:
                    asb = lambda n, s, d: astk.enter_context(nc.sbuf_tensor(k.uname(n), s, d))
                    xnT = asb("xnT", [128, 8, N], BF16)
                    wqkvb = asb("wqkvb", [128, 8, 3 * DM], BF16)
                    xc = asb("xc", [128, 2, DM], F32)
                    xnrm = asb("xnrm", [128, 2, DM], BF16)
                    stat = asb("stat", [128, 16], F32)
                    mv2 = asb("mv2", [128, 2], F32)
                    rstd = asb("rstd", [128, 1], F32)
                    sjunk = asb("sjunk", [128, 4], F32)
                    lred = asb("lred", [128, MLM], F32)

                    k.new_block()
                    xrow0 = b * N

                    gq0 = s_gq.v
                    def a_gq(g, gq0=gq0):
                        for d in range(8):
                            g.dma_start(wqkvb[:, d, :],
                                        wqkv_d[d * 128:(d + 1) * 128, :]).then_inc(s_gq.h, 16)
                    s_gq.bump(16 * 8)
                    k.emit("gpsimd", a_gq)

                    ln_stats = {}
                    ln_apply = {}
                    tp_ms = {}
                    evV = {}
                    evS = {}
                    for i in range(16):
                        x_at = s_x.v + 16

                        def c_sync(sync, i=i, xrow0=xrow0):
                            if i >= 2:
                                sync.wait_ge(s_ln.h, ln_apply[i - 2])
                            sync.dma_start(
                                xc[:, i % 2, :],
                                x_d[xrow0 + 128 * i:xrow0 + 128 * (i + 1), :]).then_inc(s_x.h, 16)
                        s_x.bump(16)
                        k.emit("sync", c_sync)

                        e1_, e2_ = s_ln.bump(), s_ln.bump()
                        def c_stats(v, i=i, x_at=x_at, e1_=e1_, e2_=e2_):
                            v.wait_ge(s_x.h, x_at)
                            v.bn_stats(stat[:, 0:6], xc[:, i % 2, 0:512]).then_inc(s_ln.h, 1)
                            v.wait_ge(s_ln.h, e1_)
                            v.bn_stats(stat[:, 6:12], xc[:, i % 2, 512:1024]).then_inc(s_ln.h, 1)
                            v.wait_ge(s_ln.h, e2_)
                            v.bn_aggr(mv2[:, :], stat[:, 0:12]).then_inc(s_ln.h, 1)
                        ln_stats[i] = s_ln.bump()
                        k.emit("vector", c_stats)

                        def c_sqrt(sc, i=i):
                            sc.wait_ge(s_ln.h, ln_stats[i])
                            sc.activation(sjunk[:, 0:1], mv2[:, 1:2], AF.Sqrt,
                                          bias=epsc[:, 0:1], scale=1.0).then_inc(s_sq.h, 1)
                        sq_i = s_sq.bump()
                        k.emit("scalar", c_sqrt)

                        er_ = s_ln.bump()
                        def c_apply(v, i=i, sq_i=sq_i, er_=er_):
                            v.wait_ge(s_sq.h, sq_i)
                            if i >= 2:
                                v.wait_ge(s_tp.h, tp_ms[i - 2])
                            v.reciprocal(rstd[:, :], sjunk[:, 0:1]).then_inc(s_ln.h, 1)
                            v.wait_ge(s_ln.h, er_)
                            v.tensor_scalar(xnrm[:, i % 2, :], xc[:, i % 2, :],
                                            mv2[:, 0:1], rstd[:, :],
                                            op0=ALU.subtract, op1=ALU.mult
                                            ).then_inc(s_ln.h, 1)
                        ln_apply[i] = s_ln.bump()
                        k.emit("vector", c_apply)

                        def c_tr(t, i=i):
                            t.wait_ge(s_ln.h, ln_apply[i])
                            if i >= 2:
                                t.wait_ge(s_ev.h, evV[i - 2])
                            pT = pTrb if i % 2 == 0 else pTr2
                            for d in range(8):
                                mm = t.transpose(pT[:, d, :],
                                                 xnrm[:, i % 2, d * 128:(d + 1) * 128],
                                                 id128[:, :])
                            mm.then_inc(s_tp.h, 1)
                        tp_ms[i] = s_tp.bump()
                        k.emit("tensor", c_tr)

                        def c_evV(v, i=i):
                            v.wait_ge(s_tp.h, tp_ms[i])
                            pT = pTrb if i % 2 == 0 else pTr2
                            for d in range(8):
                                mm = v.tensor_copy(xnT[:, d, i * 128:(i + 1) * 128],
                                                   pT[:, d, :])
                            mm.then_inc(s_ev.h, 1)
                        evV[i] = s_ev.bump()
                        k.emit("vector", c_evV)
                        evS[i] = s_ev2.v

                    if b == 0:
                        def zfill(sync):
                            for cz in range(E * CAP // 128):
                                sync.dma_start(
                                    znbkt[128 * cz:128 * (cz + 1), :],
                                    zot[:, :]).then_inc(s_st.h, 16)

                        globals()
                        k.emit("sync", zfill)
                        ZFILL_AT = s_st.bump(16 * (E * CAP // 128))
                    else:
                        nrmf = (E * CAP + 128) // 128
                        lo_f = ((b - 1) * nrmf + 2) // 3
                        hi_f = (b * nrmf + 2) // 3
                        def rmfill(sync, lo_f=lo_f, hi_f=hi_f):
                            for cz in range(lo_f, hi_f):
                                sync.dma_start(
                                    rmtab[128 * cz:128 * (cz + 1), :],
                                    rmsent[:, :]).then_inc(s_st.h, 16)
                        k.emit("sync", rmfill)
                        s_st.bump(16 * (hi_f - lo_f))

                    lm_pieces = []
                    for d in range(8):
                        el_ = None
                        def c_lm(v, d=d, el_=None):
                            if d == 0:
                                v.wait_ge(s_ev.h, evV[15])
                            v.tensor_reduce(lred[:, :],
                                            xnT[:, d, :].rearrange("p (m s) -> p m s", m=MLM),
                                            axis=AX.X, op=ALU.add).then_inc(s_ln.h, 1)
                            v.wait_ge(s_ln.h, el_)
                            v.tensor_scalar(xnlT[:, d, :], lred[:, :], 1.0 / SEG, None,
                                            op0=ALU.mult).then_inc(s_ln.h, 1)
                        lm_pieces.append(c_lm)
                    lm_ms = []

                    qk_mm = {}
                    qk_ev = {}
                    wq_at = s_gq.v

                    for nh in range(4):
                        for cc in range(16):
                            gidx = nh * 16 + cc

                            def qk_t(t, nh=nh, cc=cc, gidx=gidx, wq_at=wq_at):
                                if gidx == 0:
                                    t.wait_ge(s_gq.h, wq_at)
                                if cc == 0:
                                    for j in range(4):
                                        t.wait_ge(s_ev.h, evV[4 * nh + j])
                                if gidx >= 2:
                                    t.wait_ge(s_ev.h, qk_ev[gidx - 2])
                                pQ = pA if gidx % 2 == 0 else pB
                                for d in range(8):
                                    mm = t.matmul(pQ[:, :],
                                                  wqkvb[:, d, cc * 128:(cc + 1) * 128],
                                                  xnT[:, d, nh * 512:(nh + 1) * 512],
                                                  start=(d == 0), stop=(d == 7))
                                mm.then_inc(s_mm.h, 1)
                            qk_mm[gidx] = s_mm.bump()
                            k.emit("tensor", qk_t)

                            def qk_e(v, nh=nh, cc=cc, gidx=gidx):
                                v.wait_ge(s_mm.h, qk_mm[gidx])
                                pQ = pA if gidx % 2 == 0 else pB
                                scale = 0.125 if cc < 8 else 1.0
                                v.tensor_scalar(QKT[:, cc, nh * 512:(nh + 1) * 512],
                                                pQ[:, :], scale, None,
                                                op0=ALU.mult).then_inc(s_ev.h, 1)
                            qk_ev[gidx] = s_ev.bump()
                            k.emit("vector", qk_e)
                            if 8 <= gidx < 16:
                                el_ = s_ln.bump()
                                fn = lm_pieces[gidx - 8]
                                k.emit("vector",
                                       lambda v, fn=fn, el_=el_: fn(v, el_=el_))
                                lm_ms.append(s_ln.bump())

                    v_mm = {}
                    v_ev = {}
                    for i in range(16):
                        for dh in range(2):
                            vc = i * 2 + dh

                            def v_t(t, i=i, dh=dh, vc=vc):
                                if vc >= 2:
                                    t.wait_ge(s_ev2.h, v_ev[vc - 2])
                                pV = pC if vc % 2 == 0 else pD
                                for d in range(8):
                                    mm = t.matmul(pV[:, :], xnT[:, d, i * 128:(i + 1) * 128],
                                                  wqkvb[:, d, 2048 + dh * 512:2048 + (dh + 1) * 512],
                                                  start=(d == 0), stop=(d == 7))
                                mm.then_inc(s_mm.h, 1)
                            v_mm[vc] = s_mm.bump()
                            k.emit("tensor", v_t)

                            def v_e(sc, i=i, dh=dh, vc=vc):
                                sc.wait_ge(s_mm.h, v_mm[vc])
                                pV = pC if vc % 2 == 0 else pD
                                sc.copy(Vrm[:, i, dh * 512:(dh + 1) * 512],
                                        pV[:, :]).then_inc(s_ev2.h, 1)
                            v_ev[vc] = s_ev2.bump()
                            k.emit("scalar", v_e)

                    ql_mm = {}
                    ql_ev = {}
                    for cc in range(16):
                        def ql_t(t, cc=cc):
                            if cc == 0:
                                t.wait_ge(s_ln.h, lm_ms[-1])
                            if cc >= 2:
                                t.wait_ge(s_ev.h, ql_ev[cc - 2])
                            pQ = pE_ if cc % 2 == 0 else pF_
                            for d in range(8):
                                mm = t.matmul(pQ[:, 0:MLM],
                                              wqkvb[:, d, cc * 128:(cc + 1) * 128],
                                              xnlT[:, d, :], start=(d == 0), stop=(d == 7))
                            mm.then_inc(s_mm.h, 1)
                        ql_mm[cc] = s_mm.bump()
                        k.emit("tensor", ql_t)

                        def ql_e(v, cc=cc):
                            v.wait_ge(s_mm.h, ql_mm[cc])
                            pQ = pE_ if cc % 2 == 0 else pF_
                            scale = 0.125 if cc < 8 else 1.0
                            v.tensor_scalar(QlKlT[:, cc, :], pQ[:, 0:MLM], scale, None,
                                            op0=ALU.mult).then_inc(s_ev.h, 1)
                        ql_ev[cc] = s_ev.bump()
                        k.emit("vector", ql_e)

                    nun = 4
                    k.emit("gpsimd", emit_wprep(wprep_units[wprep_pos:wprep_pos + nun]))
                    wprep_pos += nun

                    k.run_block()

                if debug_stage == "p2ab":
                    return nc
                if debug_stage == "p2abd":
                    k.new_block()
                    def dump(sync):
                        sync.dma_start(dbg_d[:, 0, :], xnT[:, 0, :]).then_inc(s_st.h, 16)
                        sync.dma_start(dbg_d[:, 1, :], QKT[:, 0, :]).then_inc(s_st.h, 16)
                        sync.dma_start(dbg_d[:, 2, :], QKT[:, 8, :]).then_inc(s_st.h, 16)
                        sync.dma_start(dbg_d[:, 3, 0:1024], Vrm[:, 0, :]).then_inc(s_st.h, 16)
                        sync.dma_start(dbg_d[:, 3, 1024:1088], QlKlT[:, 0, :]).then_inc(s_st.h, 16)
                        sync.dma_start(dbg_d[:, 3, 1088:1152], xnlT[:, 0, :]).then_inc(s_st.h, 16)
                        sync.wait_ge(s_st.h, s_st_target)
                    s_st_target = s_st.bump(16 * 6)
                    k.emit("sync", dump)
                    k.run_block()
                    return nc

                with ExitStack() as pstk:
                    psb = lambda n, s, d: pstk.enter_context(nc.sbuf_tensor(k.uname(n), s, d))
                    SVT = psb("SVT", [128, 8, N], BF16)
                    E1T = psb("E1T", [128, 2, N], BF16)
                    E3T = psb("E3T", [128, 2, 16, 128], BF16)
                    e1f = psb("e1f", [128, 2, 512], F32)
                    e1n = psb("e1n", [128, 2, 512], BF16)
                    k2e = psb("k2e", [128, 64], F32)
                    k2s = psb("k2s", [128, 1], F32)
                    k2n = psb("k2n", [128, 64], BF16)
                    k2T_st = psb("k2T_st", [128, 64], BF16)
                    k2T_bd = psb("k2T_bd", [128, 128], BF16)
                    Kl_bd = psb("Kl_bd", [128, 128], BF16)
                    Ql_bd = psb("Ql_bd", [128, 128], BF16)
                    V_st = psb("V_st", [128, 64], BF16)
                    V_bd = psb("V_bd", [128, 128], BF16)
                    VT_bd = psb("VT_bd", [128, 128], BF16)
                    KVT_bd = psb("KVT_bd", [128, 128], BF16)
                    Z_bd = psb("Z_bd", [128, 128], BF16)
                    Zst = psb("Zst", [128, 64], BF16)
                    T_st = psb("T_st", [128, 64], BF16)
                    AVn = psb("AVn", [128, 64], BF16)
                    rdr = psb("rdr", [1, 2], F32)
                    scol = psb("scol", [128, 1], F32)
                    rds = psb("rds", [128, 1], F32)
                    s1r = psb("s1r", [128, 8], F32)
                    rs3 = psb("rs3", [128, 1], F32)
                    nmax = psb("nmax", [128, 1], F32)

                    k.new_block()

                    def pz(g):
                        for tile in (k2T_bd, Kl_bd, Ql_bd, V_bd, VT_bd, KVT_bd, Z_bd):
                            g.memset(tile[:, :], 0.0)
                        g.nop().then_inc(s_gp.h, 1)
                    bd_zero = s_gp.bump()
                    k.emit("gpsimd", pz)

                    prev_pf_free = None
                    prev_sv = {}
                    prev_av = {}
                    prev_tr2_ok = None
                    prev_kg_last = None
                    prev_k3e = None
                    prev_k1T = None
                    prev_svev = None
                    avn_prev = {}

                    sub = debug_stage if isinstance(debug_stage, str) else ""
                    npairs = 1 if sub.startswith("p2c") and sub != "p2c" else PAIRS
                    for pr in range(npairs):
                        par = pr % 2
                        def c0(g, pr=pr, gate=prev_kg_last):
                            if pr == 0:
                                g.wait_ge(s_gp.h, bd_zero)
                            if gate is not None:
                                g.wait_ge(s_mm.h, gate)
                            g.tensor_copy(Kl_bd[0:64, 0:64], QlKlT[0:64, 8 + pr, :])
                            g.tensor_copy(Kl_bd[64:128, 64:128], QlKlT[64:128, 8 + pr, :])
                            g.tensor_copy(Ql_bd[0:64, 0:64], QlKlT[0:64, pr, :])
                            g.tensor_copy(Ql_bd[64:128, 64:128],
                                          QlKlT[64:128, pr, :]).then_inc(s_gp.h, 1)
                        klbd_ok = s_gp.bump()
                        k.emit("gpsimd", c0)
                        if sub == "p2c_0":
                            break

                        def c1(t, pr=pr, gate=prev_pf_free):
                            if gate is not None:
                                t.wait_ge(s_nv.h, gate)
                            for hh in range(2):
                                mm = t.matmul(pF_[hh * 64:(hh + 1) * 64, 0:64],
                                              QlKlT[hh * 64:(hh + 1) * 64, pr, :],
                                              QlKlT[hh * 64:(hh + 1) * 64, 8 + pr, :],
                                              start=True, stop=True)
                            mm.then_inc(s_mm.h, 1)
                        l2_ms = s_mm.bump()
                        k.emit("tensor", c1)
                        if sub == "p2c_1":
                            break

                        def c2(v, l2_ms=l2_ms):
                            v.wait_ge(s_mm.h, l2_ms)
                            v.reduce_max(nmax[:, :], pF_[:, 0:64], axis=AX.X,
                                         negate=True).then_inc(s_nv.h, 1)
                        nmax_ms = s_nv.bump()
                        k.emit("vector", c2)
                        if sub == "p2c_2":
                            break

                        def c3(sc, nmax_ms=nmax_ms):
                            sc.wait_ge(s_nv.h, nmax_ms)
                            sc.activation(k2e[:, :], pF_[:, 0:64], AF.Exp,
                                          bias=nmax[:, :], scale=1.0,
                                          accum_out=k2s[:, :]).then_inc(s_ev2.h, 1)
                        k2e_ms = s_ev2.bump()
                        k.emit("scalar", c3)
                        if sub == "p2c_3":
                            break

                        ec4 = s_nv.bump()
                        def c4(v, k2e_ms=k2e_ms, gate=prev_kg_last, ec4=ec4):
                            v.wait_ge(s_ev2.h, k2e_ms)
                            if gate is not None:
                                v.wait_ge(s_mm.h, gate)
                            v.reciprocal(rds[:, :], k2s[:, :]).then_inc(s_nv.h, 1)
                            v.wait_ge(s_nv.h, ec4)
                            v.tensor_scalar(k2n[:, :], k2e[:, :], rds[:, :], None,
                                            op0=ALU.mult).then_inc(s_nv.h, 1)
                        k2n_ms = s_nv.bump()
                        k.emit("vector", c4)
                        if sub == "p2c_4":
                            break

                        def c5(t, k2n_ms=k2n_ms, gate=prev_tr2_ok):
                            t.wait_ge(s_nv.h, k2n_ms)
                            if gate is not None:
                                t.wait_ge(s_nv.h, gate)
                            t.transpose(pTr2[0:64, 0, 0:64], k2n[0:64, :], id64s[0:64, :])
                            mm = t.transpose(pTr2[64:128, 0, 64:128], k2n[64:128, :],
                                             id64s[64:128, :])
                            mm.then_inc(s_nt.h, 1)
                        k2t_ps = s_nt.bump()
                        k.emit("tensor", c5)
                        if sub == "p2c_5":
                            break

                        def c6(v, k2t_ps=k2t_ps):
                            v.wait_ge(s_nt.h, k2t_ps)
                            v.tensor_copy(k2T_st[0:64, :], pTr2[0:64, 0, 0:64])
                            v.tensor_copy(k2T_st[64:128, :], pTr2[64:128, 0, 64:128])
                            v.tensor_copy(k2T_bd[0:64, 0:64], pTr2[0:64, 0, 0:64])
                            v.tensor_copy(k2T_bd[64:128, 64:128],
                                          pTr2[64:128, 0, 64:128]).then_inc(s_nv.h, 1)
                        k2t_ok = s_nv.bump()
                        k.emit("vector", c6)
                        if sub == "p2c_6":
                            break

                        def c7a(v, k2t_ok=k2t_ok):
                            v.wait_ge(s_nv.h, k2t_ok)
                            v.reduce_sum(scol[:, :], k2T_st[:, :], axis=AX.X
                                         ).then_inc(s_nv.h, 1)
                        scol_ms = s_nv.bump()
                        k.emit("vector", c7a)

                        def c7b(t, scol_ms=scol_ms):
                            t.wait_ge(s_nv.h, scol_ms)
                            t.transpose(pF_[0:1, 256:384], scol[:, :],
                                        id128f[:, :]).then_inc(s_mm.h, 1)
                        den_ms = s_mm.bump()
                        k.emit("tensor", c7b)
                        if sub == "p2c_7":
                            break

                        ec8 = s_nv.bump()
                        def c8(v, den_ms=den_ms, ec8=ec8):
                            v.wait_ge(s_mm.h, den_ms)
                            v.reduce_max(rdr[0:1, 0:1], pF_[0:1, 256:320], axis=AX.X)
                            v.reduce_max(rdr[0:1, 1:2], pF_[0:1, 320:384],
                                         axis=AX.X).then_inc(s_nv.h, 1)
                            v.wait_ge(s_nv.h, ec8)
                            v.reciprocal(rdr[0:1, :], rdr[0:1, :]).then_inc(s_nv.h, 1)
                        rd_ms = s_nv.bump()
                        k.emit("vector", c8)
                        if sub == "p2c_8":
                            break

                        def c9(t, rd_ms=rd_ms):
                            t.wait_ge(s_nv.h, rd_ms)
                            t.matmul(pF_[:, 192:193], sel0[0:1, :], rdr[0:1, 0:1],
                                     start=True, stop=False)
                            t.matmul(pF_[:, 192:193], sel1[0:1, :], rdr[0:1, 1:2],
                                     start=False, stop=True).then_inc(s_mm.h, 1)
                        rdb_ms = s_mm.bump()
                        k.emit("tensor", c9)
                        if sub == "p2c_9":
                            break

                        ec10 = s_nv.bump()
                        ec10b = s_nv.bump()
                        def c10(v, rdb_ms=rdb_ms, k2t_ok=k2t_ok, ec10=ec10, ec10b=ec10b):
                            v.wait_ge(s_mm.h, rdb_ms)
                            v.wait_ge(s_nv.h, k2t_ok)
                            v.tensor_copy(rds[:, :], pF_[:, 192:193]).then_inc(s_nv.h, 1)
                            v.wait_ge(s_nv.h, ec10)
                            v.tensor_scalar(V_st[:, :], k2T_st[:, :], rds[:, :], None,
                                            op0=ALU.mult).then_inc(s_nv.h, 1)
                            v.wait_ge(s_nv.h, ec10b)
                            v.tensor_copy(V_bd[0:64, 0:64], V_st[0:64, :])
                            v.tensor_copy(V_bd[64:128, 64:128],
                                          V_st[64:128, :]).then_inc(s_nv.h, 1)
                        v_ok = s_nv.bump()
                        k.emit("vector", c10)
                        if sub == "p2cS":
                            break

                        newton = []
                        vok = {0: v_ok}
                        for it in range(6):
                            slot = 1 + (it % 2)

                            def n_b(t, it=it, slot=slot, vok=vok):
                                t.wait_ge(s_nv.h, vok[it])
                                t.transpose(pTr2[0:64, slot, 0:64], V_st[0:64, :],
                                            id64s[0:64, :])
                                t.transpose(pTr2[64:128, slot, 64:128], V_st[64:128, :],
                                            id64s[64:128, :])
                                t.matmul(pF_[:, 256:320], k2T_bd[:, :], V_st[:, :],
                                         start=True, stop=True)
                                t.matmul(pF_[:, 320:384], V_bd[:, :], k2T_st[:, :],
                                         start=True, stop=True).then_inc(s_nt.h, 1)
                            kv_ms = s_nt.bump()

                            def n_c(v, it=it, slot=slot, kv_ms=kv_ms):
                                v.wait_ge(s_nt.h, kv_ms)
                                v.tensor_tensor(T_st[:, :], cIs[:, 0, :], pF_[:, 256:320],
                                                op=ALU.subtract)
                                v.tensor_copy(KVT_bd[0:64, 0:64], pF_[0:64, 320:384])
                                v.tensor_copy(KVT_bd[64:128, 64:128], pF_[64:128, 320:384])
                                v.tensor_copy(VT_bd[0:64, 0:64], pTr2[0:64, slot, 0:64])
                                v.tensor_copy(VT_bd[64:128, 64:128],
                                              pTr2[64:128, slot, 64:128]).then_inc(s_nv.h, 1)
                            t1_ms = s_nv.bump()

                            def n_d(t, t1_ms=t1_ms):
                                t.wait_ge(s_nv.h, t1_ms)
                                t.matmul(pF_[:, 384:448], KVT_bd[:, :], T_st[:, :],
                                         start=True, stop=True).then_inc(s_nt.h, 1)
                            u1_ms = s_nt.bump()

                            def n_e(v, u1_ms=u1_ms):
                                v.wait_ge(s_nt.h, u1_ms)
                                v.tensor_tensor(T_st[:, :], cIs[:, 1, :], pF_[:, 384:448],
                                                op=ALU.subtract).then_inc(s_nv.h, 1)
                            t2_ms = s_nv.bump()

                            def n_f(t, t2_ms=t2_ms):
                                t.wait_ge(s_nv.h, t2_ms)
                                t.matmul(pF_[:, 448:512], KVT_bd[:, :], T_st[:, :],
                                         start=True, stop=True).then_inc(s_nt.h, 1)
                            u2_ms = s_nt.bump()

                            def n_g(v, u2_ms=u2_ms):
                                v.wait_ge(s_nt.h, u2_ms)
                                v.tensor_tensor(T_st[:, :], cIs[:, 2, :], pF_[:, 448:512],
                                                op=ALU.subtract).then_inc(s_nv.h, 1)
                            t3_ms = s_nv.bump()

                            def n_h(t, t3_ms=t3_ms):
                                t.wait_ge(s_nv.h, t3_ms)
                                t.matmul(pF_[:, 256:320], VT_bd[:, :], T_st[:, :],
                                         start=True, stop=True).then_inc(s_nt.h, 1)
                            vn_ms = s_nt.bump()

                            eni = s_nv.bump() if it < 5 else None
                            def n_i(v, vn_ms=vn_ms, it=it, eni=eni):
                                v.wait_ge(s_nt.h, vn_ms)
                                if it < 5:
                                    v.tensor_scalar(V_st[:, :], pF_[:, 256:320], 0.25, None,
                                                    op0=ALU.mult).then_inc(s_nv.h, 1)
                                    v.wait_ge(s_nv.h, eni)
                                    v.tensor_copy(V_bd[0:64, 0:64], V_st[0:64, :])
                                    v.tensor_copy(V_bd[64:128, 64:128],
                                                  V_st[64:128, :]).then_inc(s_nv.h, 1)
                                else:
                                    v.tensor_scalar(V_st[:, :], pF_[:, 256:320], 0.25,
                                                    None, op0=ALU.mult).then_inc(s_nv.h, 1)
                            vok[it + 1] = s_nv.bump()

                            newton += [("tensor", n_b), ("vector", n_c),
                                       ("tensor", n_d), ("vector", n_e), ("tensor", n_f),
                                       ("vector", n_g), ("tensor", n_h), ("vector", n_i)]

                        kpath = []
                        k1g = {}
                        k1e_ms = {}
                        k1n_ms = {}
                        k1t_ms = {}
                        k1T_ms = {}
                        k3g = {}
                        k3e_ms = {}
                        for g8 in range(8):
                            isk1 = g8 < 4
                            gg = g8 % 4

                            def kg_t(t, pr=pr, g8=g8, isk1=isk1, gg=gg,
                                     klbd_ok=klbd_ok, pk3e=prev_k3e,
                                     k1e_ms=k1e_ms, k3e_ms=k3e_ms):
                                if g8 == 0:
                                    t.wait_ge(s_gp.h, klbd_ok)
                                if g8 >= 2:
                                    t.wait_ge(s_ev2.h,
                                              (k1e_ms if g8 - 2 < 4 else k3e_ms)[
                                                  (g8 - 2) % 4])
                                elif pk3e is not None:
                                    t.wait_ge(s_ev2.h, pk3e[2 + g8])
                                pX = pA if g8 % 2 == 0 else pB
                                cc = (pr if isk1 else 8 + pr)
                                bd = Kl_bd if isk1 else Ql_bd
                                for j in range(4):
                                    nch = 4 * gg + j
                                    mm = t.matmul(pX[:, j * 128:(j + 1) * 128],
                                                  QKT[:, cc, nch * 128:(nch + 1) * 128],
                                                  bd[:, :], start=True, stop=True)
                                mm.then_inc(s_mm.h, 1)
                            (k1g if isk1 else k3g)[gg] = s_mm.bump()
                            kpath.append(("tensor", kg_t))

                            if isk1:
                                def k1_s(sc, pr=pr, gg=gg, par=par, k1g=k1g, k1n_ms=k1n_ms):
                                    sc.wait_ge(s_mm.h, k1g[gg])
                                    if gg >= 2:
                                        sc.wait_ge(s_ev.h, k1n_ms[gg - 2])
                                    pX = pA if gg % 2 == 0 else pB
                                    sc.activation(e1f[:, gg % 2, :], pX[:, :],
                                                  AF.Exp).then_inc(s_ev2.h, 1)
                                k1e_ms[gg] = s_ev2.bump()
                                kpath.append(("scalar", k1_s))

                                ek1, ek2 = s_ev.bump(), s_ev.bump()
                                def k1_v(v, gg=gg, par=par, k1e_ms=k1e_ms,
                                         k1t_ms=k1t_ms, ek1=ek1, ek2=ek2):
                                    v.wait_ge(s_ev2.h, k1e_ms[gg])
                                    if gg >= 2:
                                        v.wait_ge(s_tp.h, k1t_ms[gg - 2])
                                    v.tensor_reduce(s1r[:, :],
                                                    e1f[:, gg % 2, :].rearrange(
                                                        "p (a m) -> p a m", a=8),
                                                    axis=AX.X, op=ALU.add).then_inc(s_ev.h, 1)
                                    v.wait_ge(s_ev.h, ek1)
                                    v.reciprocal(s1r[:, :], s1r[:, :]).then_inc(s_ev.h, 1)
                                    v.wait_ge(s_ev.h, ek2)
                                    for j in range(8):
                                        op = v.tensor_scalar(
                                            e1n[:, gg % 2, j * 64:(j + 1) * 64],
                                            e1f[:, gg % 2, j * 64:(j + 1) * 64],
                                            s1r[:, j:j + 1], None, op0=ALU.mult)
                                    op.then_inc(s_ev.h, 1)
                                k1n_ms[gg] = s_ev.bump()
                                kpath.append(("vector", k1_v))

                                def k1_tr(t, gg=gg, par=par, pk1T=prev_k1T, k1n_ms=k1n_ms, k1T_ms=k1T_ms):
                                    t.wait_ge(s_ev.h, k1n_ms[gg])
                                    if gg >= 1:
                                        t.wait_ge(s_ev2.h, k1T_ms[gg - 1])
                                    elif pk1T is not None:
                                        t.wait_ge(s_ev2.h, pk1T[3])
                                    for j in range(4):
                                        mm = t.transpose(pTrb[:, 4 * (gg % 2) + j, :],
                                                         e1n[:, gg % 2, j * 128:(j + 1) * 128],
                                                         id128[:, :])
                                    mm.then_inc(s_tp.h, 1)
                                k1t_ms[gg] = s_tp.bump()
                                kpath.append(("tensor", k1_tr))

                                def k1_e(sc, gg=gg, par=par, pr=pr, k1t_ms=k1t_ms):
                                    sc.wait_ge(s_tp.h, k1t_ms[gg])
                                    if gg == 0 and pr >= 2 and prev_sv.get(pr - 2):
                                        sc.wait_ge(s_mm.h, prev_sv[pr - 2])
                                    for j in range(4):
                                        nch = 4 * gg + j
                                        mm = sc.copy(E1T[:, par, nch * 128:(nch + 1) * 128],
                                                     pTrb[:, 4 * (gg % 2) + j, :])
                                    mm.then_inc(s_ev2.h, 1)
                                k1T_ms[gg] = s_ev2.bump()
                                kpath.append(("scalar", k1_e))
                            else:
                                def k3_s(sc, gg=gg, par=par, pr=pr, k3g=k3g):
                                    sc.wait_ge(s_mm.h, k3g[gg])
                                    if gg == 0 and pr >= 2 and prev_av.get(pr - 2):
                                        sc.wait_ge(s_mm.h, prev_av[pr - 2])
                                    pX = pA if gg % 2 == 0 else pB
                                    sc.activation(
                                        E3T[:, par, 4 * gg:4 * gg + 4, :].rearrange(
                                            "p a m -> p (a m)"),
                                        pX[:, :], AF.Exp).then_inc(s_ev2.h, 1)
                                k3e_ms[gg] = s_ev2.bump()
                                kpath.append(("scalar", k3_s))

                        def s3av_t(t, pr=pr, par=par, psvev=prev_svev, k3e_ms=k3e_ms):
                            t.wait_ge(s_ev2.h, k3e_ms[3])
                            if pr >= 1:
                                t.wait_ge(s_nv.h, avn_prev[pr - 1])
                            if psvev is not None:
                                t.wait_ge(s_ev.h, psvev[3])
                            for nch in range(16):
                                t.matmul(pC[:, 0:1], E3T[:, par, nch, :], ones128[:, :],
                                         start=(nch == 0), stop=(nch == 15))
                                mm = t.matmul(pD[:, 0:128], E3T[:, par, nch, :],
                                              Vrm[:, nch, pr * 128:(pr + 1) * 128],
                                              start=(nch == 0), stop=(nch == 15))
                            mm.then_inc(s_mm.h, 1)
                        av_ms = s_mm.bump()
                        prev_av[pr] = av_ms
                        kpath.append(("tensor", s3av_t))

                        eav = s_nv.bump()
                        def av_v(v, av_ms=av_ms, eav=eav):
                            v.wait_ge(s_mm.h, av_ms)
                            v.reciprocal(rs3[:, :], pC[:, 0:1]).then_inc(s_nv.h, 1)
                            v.wait_ge(s_nv.h, eav)
                            v.tensor_scalar(AVn[0:64, :], pD[0:64, 0:64],
                                            rs3[0:64, :], None, op0=ALU.mult)
                            v.tensor_scalar(AVn[64:128, :], pD[64:128, 64:128],
                                            rs3[64:128, :], None,
                                            op0=ALU.mult).then_inc(s_nv.h, 1)
                        avn_ms = s_nv.bump()
                        avn_prev[pr] = avn_ms
                        kpath.append(("vector", av_v))

                        if sub == "p2cN":
                            for eng, fn in newton:
                                k.emit(eng, fn)
                            break
                        if sub == "p2cK":
                            for eng, fn in newton + kpath:
                                k.emit(eng, fn)
                            break
                        merged = []
                        ni, ki = 0, 0
                        while ni < len(newton) or ki < len(kpath):
                            for _ in range(3):
                                if ni < len(newton):
                                    merged.append(newton[ni]); ni += 1
                            if ki < len(kpath):
                                merged.append(kpath[ki]); ki += 1
                        for eng, fn in merged:
                            k.emit(eng, fn)

                        def z_t(t, vok6=vok[6], avn_ms=avn_ms):
                            t.wait_ge(s_nv.h, vok6)
                            t.transpose(pTr2[0:64, 3, 0:64], V_st[0:64, :], id64s[0:64, :])
                            mm = t.transpose(pTr2[64:128, 3, 64:128], V_st[64:128, :],
                                             id64s[64:128, :])
                            mm.then_inc(s_nt.h, 1)
                        v6t_ps = s_nt.bump()
                        k.emit("tensor", z_t)

                        def z_v(v, v6t_ps=v6t_ps):
                            v.wait_ge(s_nt.h, v6t_ps)
                            v.tensor_copy(VT_bd[0:64, 0:64], pTr2[0:64, 3, 0:64])
                            v.tensor_copy(VT_bd[64:128, 64:128],
                                          pTr2[64:128, 3, 64:128]).then_inc(s_nv.h, 1)
                        v6t_ok = s_nv.bump()
                        k.emit("vector", z_v)

                        def z_mm(t, v6t_ok=v6t_ok, avn_ms=avn_ms):
                            t.wait_ge(s_nv.h, v6t_ok)
                            t.wait_ge(s_nv.h, avn_ms)
                            t.matmul(pF_[:, 384:448], VT_bd[:, :], AVn[:, :],
                                     start=True, stop=True).then_inc(s_nt.h, 1)
                        zmm_ms = s_nt.bump()
                        k.emit("tensor", z_mm)

                        def z_ev(v, zmm_ms=zmm_ms):
                            v.wait_ge(s_nt.h, zmm_ms)
                            v.tensor_copy(Zst[:, :], pF_[:, 384:448]).then_inc(s_nv.h, 1)
                        zst_ms = s_nv.bump()
                        k.emit("vector", z_ev)

                        def z_bd(g, zst_ms=zst_ms):
                            g.wait_ge(s_nv.h, zst_ms)
                            g.tensor_copy(Z_bd[0:64, 0:64], Zst[0:64, :])
                            g.tensor_copy(Z_bd[64:128, 64:128],
                                          Zst[64:128, :]).then_inc(s_gp.h, 1)
                        zbd_ms = s_gp.bump()
                        k.emit("gpsimd", z_bd)
                        prev_pf_free = zst_ms
                        prev_tr2_ok = vok[6]

                        sv_ev_ms = {}
                        for nh2 in range(4):
                            def sv_t(t, pr=pr, par=par, nh2=nh2, zbd_ms=zbd_ms, k1T_ms=k1T_ms, sv_ev_ms=sv_ev_ms):
                                if nh2 == 0:
                                    t.wait_ge(s_gp.h, zbd_ms)
                                    t.wait_ge(s_ev2.h, k1T_ms[3])
                                if nh2 >= 2:
                                    t.wait_ge(s_ev.h, sv_ev_ms[nh2 - 2])
                                pX = pE_ if nh2 % 2 == 0 else pC
                                mm = t.matmul(pX[:, :], Z_bd[:, :],
                                              E1T[:, par, nh2 * 512:(nh2 + 1) * 512],
                                              start=True, stop=True)
                                mm.then_inc(s_mm.h, 1)
                            svmm = s_mm.bump()
                            k.emit("tensor", sv_t)

                            def sv_e(v, pr=pr, nh2=nh2, svmm=svmm):
                                v.wait_ge(s_mm.h, svmm)
                                pX = pE_ if nh2 % 2 == 0 else pC
                                v.tensor_copy(SVT[:, pr, nh2 * 512:(nh2 + 1) * 512],
                                              pX[:, :]).then_inc(s_ev.h, 1)
                            sv_ev_ms[nh2] = s_ev.bump()
                            k.emit("vector", sv_e)
                        prev_sv[pr] = svmm
                        prev_kg_last = k3g[3]
                        prev_k3e = dict(k3e_ms)
                        prev_k1T = dict(k1T_ms)
                        prev_svev = dict(sv_ev_ms)

                    sv_done_v = s_ev.v
                    k.run_block()

                    if sub.startswith("p2c"):
                        return nc

                    with ExitStack() as estk:
                        esb = lambda n, s, d: estk.enter_context(nc.sbuf_tensor(k.uname(n), s, d))
                        wprojb = esb("wprojb", [128, 8, DM], BF16)
                        xc2 = esb("xc2", [128, 2, DM], F32)
                        x1t = esb("x1t", [128, 2, DM], F32)
                        stat2 = esb("stat2", [128, 16], F32)
                        mv3 = esb("mv3", [128, 2], F32)
                        rstd2 = esb("rstd2", [128, 1], F32)
                        sjunk2 = esb("sjunk2", [128, 4], F32)

                        k.new_block()
                        gq0 = s_gq.v

                        def e_gq(g):
                            for d in range(8):
                                g.dma_start(wprojb[:, d, :],
                                            wproj_d[d * 128:(d + 1) * 128, :]).then_inc(s_gq.h, 16)
                        s_gq.bump(16 * 8)
                        k.emit("gpsimd", e_gq)

                        pj_mm = {}
                        pj_ap = {}
                        ST_X1 = {}
                        for i in range(16):
                            def e_sync(sync, i=i, b=b):
                                if i >= 2:
                                    sync.wait_ge(s_ln.h, pj_ap[i - 2])
                                sync.dma_start(
                                    xc2[:, i % 2, :],
                                    x_d[b * N + 128 * i:b * N + 128 * (i + 1), :]
                                ).then_inc(s_x.h, 16)
                                if i >= 1:
                                    sync.wait_ge(s_ln.h, pj_ap[i - 1])
                                    sync.dma_start(
                                        x1s[b * N + 128 * (i - 1):b * N + 128 * i, :],
                                        x1t[:, (i - 1) % 2, :]).then_inc(s_st.h, 16)
                            x_at = s_x.bump(16)
                            if i >= 1:
                                ST_X1[i - 1] = s_st.bump(16)
                            k.emit("sync", e_sync)

                            def e_t(t, i=i, gq0=gq0):
                                if i == 0:
                                    t.wait_ge(s_gq.h, gq0 + 128)
                                for dh in range(2):
                                    gI = 2 * i + dh
                                    if gI >= 2:
                                        t.wait_ge(s_ln.h, pj_ap.get(i - 1, 0))
                                    pX = pA if gI % 2 == 0 else pB
                                    for pr2 in range(8):
                                        mm = t.matmul(pX[:, :],
                                                      SVT[:, pr2, i * 128:(i + 1) * 128],
                                                      wprojb[:, pr2, dh * 512:(dh + 1) * 512],
                                                      start=(pr2 == 0), stop=(pr2 == 7))
                                    mm.then_inc(s_mm.h, 1)
                            pj_mm[i] = (s_mm.bump(), s_mm.bump())
                            k.emit("tensor", e_t)

                            ee1, ee2 = s_ln.bump(), s_ln.bump()
                            def e_v1(v, i=i, x_at=x_at, ee1=ee1, ee2=ee2):
                                v.wait_ge(s_mm.h, pj_mm[i][1])
                                v.bn_stats(stat2[:, 0:6], pA[:, :]).then_inc(s_ln.h, 1)
                                v.wait_ge(s_ln.h, ee1)
                                v.bn_stats(stat2[:, 6:12], pB[:, :]).then_inc(s_ln.h, 1)
                                v.wait_ge(s_ln.h, ee2)
                                v.bn_aggr(mv3[:, :], stat2[:, 0:12]).then_inc(s_ln.h, 1)
                            pst = s_ln.bump()
                            k.emit("vector", e_v1)

                            def e_sq(sc, i=i, pst=pst):
                                sc.wait_ge(s_ln.h, pst)
                                sc.activation(sjunk2[:, 0:1], mv3[:, 1:2], AF.Sqrt,
                                              bias=epsc[:, 0:1], scale=1.0).then_inc(s_sq.h, 1)
                            sqm = s_sq.bump()
                            k.emit("scalar", e_sq)

                            ee3 = s_ln.bump()
                            def e_v2(v, i=i, sqm=sqm, x_at=x_at, ee3=ee3):
                                v.wait_ge(s_sq.h, sqm)
                                v.wait_ge(s_x.h, x_at)
                                if i >= 2:
                                    v.wait_ge(s_st.h, ST_X1[i - 2])
                                v.reciprocal(rstd2[:, :], sjunk2[:, 0:1]).then_inc(s_ln.h, 1)
                                v.wait_ge(s_ln.h, ee3)
                                for dh in range(2):
                                    pX = pA if dh == 0 else pB
                                    v.tensor_scalar(x1t[:, i % 2, dh * 512:(dh + 1) * 512],
                                                    pX[:, :], mv3[:, 0:1], rstd2[:, :],
                                                    op0=ALU.subtract, op1=ALU.mult)
                                op = v.tensor_tensor(x1t[:, i % 2, :], x1t[:, i % 2, :],
                                                     xc2[:, i % 2, :], op=ALU.add)
                                op.then_inc(s_ln.h, 1)
                            pj_ap[i] = s_ln.bump()
                            k.emit("vector", e_v2)

                        def e_fin(sync, b=b):
                            sync.wait_ge(s_ln.h, pj_ap[15])
                            sync.dma_start(x1s[b * N + 128 * 15:b * N + 128 * 16, :],
                                           x1t[:, 15 % 2, :]).then_inc(s_st.h, 16)
                        s_st.bump(16)
                        k.emit("sync", e_fin)
                        k.run_block()

        if debug_stage in ("attn", "attn1"):
            k.new_block()
            st_target = s_st.v
            def fin(sync):
                sync.wait_ge(s_st.h, st_target)
            k.emit("sync", fin)
            k.run_block()
            return nc

        with ExitStack() as rstk:
            rsb = lambda n, s, d: rstk.enter_context(nc.sbuf_tensor(k.uname(n), s, d))
            xc1 = rsb("xc1", [128, 2, DM], F32)
            zf = rsb("zf", [128, 2, DM], F32)
            rmch = rsb("rmch", [128, 2, 4], F32)
            znbt = rsb("znbt", [128, 2, DM], BF16)
            zT32 = rsb("zT32", [128, 2, 8, 128], F32)
            stat3 = rsb("stat3", [128, 16], F32)
            mvA = rsb("mvA", [128, 2], F32)
            mvB = rsb("mvB", [128, 2], F32)
            rstdA = rsb("rstdA", [128, 1], F32)
            rstdB = rsb("rstdB", [128, 1], F32)
            sjk = rsb("sjk", [128, 4], F32)
            affs = rsb("affs", [128, 8], F32)
            max8 = rsb("max8", [128, 8], F32)
            idx8 = rsb("idx8", [128, 8], U32)
            esf = rsb("esf", [128, 1], F32)
            ohf = rsb("ohf", [128, 8], F32)
            ohsel = rsb("ohsel", [128, 8], F32)
            rkred = rsb("rkred", [128, 1], F32)
            of32 = rsb("of32", [128, 1], F32)
            ovr = rsb("ovr", [128, 1], F32)
            ogf = rsb("ogf", [128, 1], F32)
            cnt8 = rsb("cnt8", [1, 8], F32)

            DBG = debug_stage == "dump"
            k.new_block()
            P2_ST_DONE = s_st.v
            _breg = {}

            def r_breg(g):
                _breg["r"] = g.alloc_register(name="sc_bound")
                g.reg_mov(_breg["r"], E * CAP - 1)
                _breg["rows"] = g.alloc_register(name="sc_bound2")
                g.reg_mov(_breg["rows"], ROWS - 1)
            k.emit("gpsimd", r_breg)
            if wprep_pos < len(wprep_units):
                k.emit("gpsimd", emit_wprep(wprep_units[wprep_pos:]))
                wprep_pos = len(wprep_units)

            r_ap1 = {}
            r_ap2 = {}
            r_tp = {}
            r_ev = {}
            r_sc = {}
            r_ch = {}
            rm_sq = {}
            for c in range(NCH):
                def r_sync(sync, c=c):
                    if c >= 2:
                        sync.wait_ge(s_ln.h, r_ap2[c - 2])
                    sync.dma_start(xc1[:, c % 2, :],
                                   x1s[128 * c:128 * (c + 1), :]).then_inc(s_x.h, 16)
                x_at = s_x.bump(16)
                k.emit("sync", r_sync)

                er1, er2 = s_ln.bump(), s_ln.bump()
                def r_v1(v, c=c, x_at=x_at, er1=er1, er2=er2):
                    v.wait_ge(s_x.h, x_at)
                    v.bn_stats(stat3[:, 0:6], xc1[:, c % 2, 0:512]).then_inc(s_ln.h, 1)
                    v.wait_ge(s_ln.h, er1)
                    v.bn_stats(stat3[:, 6:12], xc1[:, c % 2, 512:1024]).then_inc(s_ln.h, 1)
                    v.wait_ge(s_ln.h, er2)
                    v.bn_aggr(mvA[:, :], stat3[:, 0:12]).then_inc(s_ln.h, 1)
                st1 = s_ln.bump()
                k.emit("vector", r_v1)

                def r_s1(sc, c=c, st1=st1):
                    sc.wait_ge(s_ln.h, st1)
                    sc.activation(sjk[:, 0:1], mvA[:, 1:2], AF.Sqrt,
                                  bias=epsc[:, 0:1], scale=1.0).then_inc(s_sq.h, 1)
                sq1 = s_sq.bump()
                k.emit("scalar", r_s1)

                er3, er4, er5 = s_ln.bump(), s_ln.bump(), s_ln.bump()
                def r_v2(v, c=c, sq1=sq1, er3=er3, er4=er4, er5=er5):
                    v.wait_ge(s_sq.h, sq1)
                    if c >= 2:
                        v.wait_ge(s_tp.h, r_tp[c - 2])
                    v.reciprocal(rstdA[:, :], sjk[:, 0:1]).then_inc(s_ln.h, 1)
                    v.wait_ge(s_ln.h, er3)
                    v.tensor_scalar(zf[:, c % 2, :], xc1[:, c % 2, :], mvA[:, 0:1],
                                    rstdA[:, :], op0=ALU.subtract, op1=ALU.mult)
                    v.bn_stats(stat3[:, 0:6], zf[:, c % 2, 0:512]).then_inc(s_ln.h, 1)
                    v.wait_ge(s_ln.h, er4)
                    v.bn_stats(stat3[:, 6:12], zf[:, c % 2, 512:1024]).then_inc(s_ln.h, 1)
                    v.wait_ge(s_ln.h, er5)
                    v.bn_aggr(mvB[:, :], stat3[:, 0:12]).then_inc(s_ln.h, 1)
                r_ap1[c] = s_ln.bump()
                k.emit("vector", r_v2)

                def r_s2(sc, c=c):
                    sc.wait_ge(s_ln.h, r_ap1[c])
                    sc.activation(sjk[:, 1:2], mvB[:, 1:2], AF.Sqrt,
                                  bias=epsc[:, 0:1], scale=1.0).then_inc(s_sq.h, 1)
                sq2 = s_sq.bump()
                k.emit("scalar", r_s2)

                er6 = s_ln.bump()
                def r_v3(v, c=c, sq2=sq2, er6=er6):
                    v.wait_ge(s_sq.h, sq2)
                    if c >= 2:
                        v.wait_ge(s_sc.h, r_sc[c - 2])
                    v.reciprocal(rstdB[:, :], sjk[:, 1:2]).then_inc(s_ln.h, 1)
                    v.wait_ge(s_ln.h, er6)
                    v.tensor_scalar(znbt[:, c % 2, :], zf[:, c % 2, :], mvB[:, 0:1],
                                    rstdB[:, :], op0=ALU.subtract, op1=ALU.mult
                                    ).then_inc(s_ln.h, 1)
                r_ap2[c] = s_ln.bump()
                k.emit("vector", r_v3)

                def r_tr(t, c=c):
                    t.wait_ge(s_ln.h, r_ap1[c])
                    if c >= 2:
                        t.wait_ge(s_ev.h, r_ev[c - 2])
                    bks = (pA, pB) if c % 2 == 0 else (pC, pD)
                    for d in range(8):
                        pX = bks[d // 4]
                        mm = t.transpose(pX[:, (d % 4) * 128:(d % 4 + 1) * 128],
                                         zf[:, c % 2, d * 128:(d + 1) * 128],
                                         id128f[:, :])
                    mm.then_inc(s_tp.h, 1)
                r_tp[c] = s_tp.bump()
                k.emit("tensor", r_tr)

                def r_tev(v, c=c):
                    v.wait_ge(s_tp.h, r_tp[c])
                    bks = (pA, pB) if c % 2 == 0 else (pC, pD)
                    for d in range(8):
                        pX = bks[d // 4]
                        mm = v.tensor_copy(zT32[:, c % 2, d, :],
                                           pX[:, (d % 4) * 128:(d % 4 + 1) * 128])
                    mm.then_inc(s_ev.h, 1)
                r_ev[c] = s_ev.bump()
                k.emit("vector", r_tev)

                def r_aff(t, c=c):
                    t.wait_ge(s_ev.h, r_ev[c])
                    if c >= 1:
                        t.wait_ge(s_ln.h, AFF_CONS[c - 1])
                    for d in range(8):
                        mm = t.matmul(pE_[:, 0:8], zT32[:, c % 2, d, :], CT[:, d, :],
                                      start=(d == 0), stop=(d == 7))
                    mm.then_inc(s_mm.h, 1)
                aff_ms = s_mm.bump()
                k.emit("tensor", r_aff)

                ea1, ea2, ea3, ea4 = (s_ln.bump(), s_ln.bump(), s_ln.bump(),
                                      s_ln.bump())
                def r_v4(v, c=c, aff_ms=aff_ms, ea1=ea1, ea2=ea2, ea3=ea3, ea4=ea4):
                    v.wait_ge(s_mm.h, aff_ms)
                    if c >= 1:
                        v.wait_ge(s_sq.h, rm_sq[c - 1])

                    v.tensor_copy(affs[:, :], pE_[:, 0:8]).then_inc(s_ln.h, 1)
                    v.wait_ge(s_ln.h, ea1)
                    v.max(max8[:, :], affs[:, :]).then_inc(s_ln.h, 1)
                    v.wait_ge(s_ln.h, ea2)
                    v.max_index(idx8[:, :], max8[:, :], affs[:, :]).then_inc(s_ln.h, 1)
                    v.wait_ge(s_ln.h, ea3)
                    v.tensor_copy(esf[:, :], idx8[:, 0:1]).then_inc(s_ln.h, 1)
                    v.wait_ge(s_ln.h, ea4)
                    v.tensor_scalar(ohf[:, :], iota8[:, :], esf[:, :], None,
                                    op0=ALU.is_equal).then_inc(s_ln.h, 1)
                if c == 0:
                    AFF_CONS = {}
                AFF_CONS[c] = s_ln.bump()
                k.emit("vector", r_v4)

                def r_rk(t, c=c):
                    t.wait_ge(s_ln.h, AFF_CONS[c])
                    if c >= 1:
                        t.wait_ge(s_rt.h, r_ch[c - 1])
                    t.matmul(pF_[:, 0:8], ltu[:, :], ohf[:, :], start=True, stop=False)
                    t.matmul(pF_[:, 0:8], ones1f[0:1, :], Rrow[0:1, :],
                             start=False, stop=True)
                    t.matmul(pF_[0:1, 8:16], ones128f[:, :], ohf[:, :],
                             start=True, stop=True).then_inc(s_mm.h, 1)
                rk_ms = s_mm.bump()
                k.emit("tensor", r_rk)

                eb = [s_ln.bump() for _ in range(6)]
                def r_v5(v, c=c, rk_ms=rk_ms, eb=eb):
                    v.wait_ge(s_mm.h, rk_ms)
                    v.tensor_tensor(ohsel[:, :], pF_[:, 0:8], ohf[:, :],
                                    op=ALU.mult).then_inc(s_ln.h, 1)
                    v.wait_ge(s_ln.h, eb[0])
                    v.tensor_reduce(rkred[:, :], ohsel[:, :], axis=AX.X,
                                    op=ALU.add).then_inc(s_ln.h, 1)
                    v.wait_ge(s_ln.h, eb[1])
                    v.tensor_scalar(of32[:, :], esf[:, :], float(CAP), rkred[:, :],
                                    op0=ALU.mult, op1=ALU.add)
                    v.tensor_scalar(ovr[:, :], rkred[:, :], -(float(CAP) - 1.0), 0.0,
                                    op0=ALU.add, op1=ALU.max).then_inc(s_ln.h, 1)
                    v.wait_ge(s_ln.h, eb[2])
                    v.tensor_scalar(ovr[:, :], ovr[:, :], 1.0, None,
                                    op0=ALU.min).then_inc(s_ln.h, 1)
                    v.wait_ge(s_ln.h, eb[3])
                    v.scalar_tensor_tensor(of32[:, :], ovr[:, :], 1.0e9, of32[:, :],
                                           op0=ALU.mult, op1=ALU.add).then_inc(s_ln.h, 1)
                    v.wait_ge(s_ln.h, eb[4])
                    v.tensor_copy(o_all[:, c:c + 1], of32[:, :])
                    v.tensor_copy(cnt8[0:1, :], pF_[0:1, 8:16]).then_inc(s_ln.h, 1)
                    v.wait_ge(s_ln.h, eb[5])
                    if DBG:
                        v.tensor_copy(dbg_rk[:, 0, c:c + 1], rkred[:, :])
                        v.tensor_copy(dbg_rk[:, 1, c:c + 1], ovr[:, :])
                        v.tensor_copy(dbg_rk[:, 2, c:c + 1], of32[:, :])
                        v.tensor_copy(dbg_rk[:, 3, c:c + 1], esf[:, :])
                    v.tensor_tensor(Rrow[0:1, :], Rrow[0:1, :], cnt8[0:1, :],
                                    op=ALU.add).then_inc(s_rt.h, 1)
                r_ch[c] = s_rt.bump()
                k.emit("vector", r_v5)

                def r_rm(sc, c=c, sq2=sq2):
                    sc.wait_ge(s_ln.h, AFF_CONS[c])
                    sc.wait_ge(s_sq.h, sq2)
                    if c >= 2:
                        sc.wait_ge(s_sc.h, r_sc[c - 2])
                    sc.copy(rmch[:, c % 2, 0:1], riota[:, c:c + 1])
                    sc.copy(rmch[:, c % 2, 1:2], max8[:, 0:1])
                    sc.copy(rmch[:, c % 2, 2:3], mvB[:, 0:1])
                    sc.copy(rmch[:, c % 2, 3:4], sjk[:, 1:2]).then_inc(s_sq.h, 1)
                rm_sq[c] = s_sq.bump()
                k.emit("scalar", r_rm)

                def r_scat(g, c=c):
                    if c == 0:
                        g.wait_ge(s_st.h, P2_ST_DONE)
                    g.wait_ge(s_ln.h, r_ap2[c])
                    g.wait_ge(s_rt.h, r_ch[c])
                    g.indirect_dma_start(
                        out=znbkt, out_offset=IndirectOffsetOnAxis(o_all[:, c:c + 1], 0),
                        in_=znbt[:, c % 2, :], in_offset=None,
                        bounds_check=_breg["r"],
                        oob_is_err=False).then_inc(s_sc.h, 16)
                    g.wait_ge(s_sq.h, rm_sq[c])
                    g.indirect_dma_start(
                        out=rmtab, out_offset=IndirectOffsetOnAxis(o_all[:, c:c + 1], 0),
                        in_=rmch[:, c % 2, :], in_offset=None,
                        bounds_check=_breg["r"],
                        oob_is_err=False).then_inc(s_sc.h, 16)
                r_sc[c] = s_sc.bump(32)
                k.emit("gpsimd", r_scat)

            en1, en2 = s_ln.bump(), s_ln.bump()
            def r_nse(v, en1=en1, en2=en2):
                v.wait_ge(s_rt.h, r_ch[NCH - 1])
                v.tensor_scalar(nse_f[0:1, :], Rrow[0:1, :], 255.0, 1.0 / 256.0,
                                op0=ALU.add, op1=ALU.mult).then_inc(s_ln.h, 1)
                v.wait_ge(s_ln.h, en1)
                v.tensor_scalar(nse_f[0:1, :], nse_f[0:1, :], 0.25, None,
                                op0=ALU.add).then_inc(s_ln.h, 1)
                v.wait_ge(s_ln.h, en2)
                v.tensor_copy(nse_i[0:1, :], nse_f[0:1, :]).then_inc(s_rt.h, 1)
            NSE_MS = s_rt.bump()
            k.emit("vector", r_nse)
            k.run_block()

        with ExitStack() as fstk:
            fsb = lambda n, s, d: fstk.enter_context(nc.sbuf_tensor(k.uname(n), s, d))
            W1T = fsb("W1T", [128, 8, FF], BF16)
            W2T = fsb("W2T", [128, NFC, DM], BF16)
            znl = fsb("znl", [128, 3, 2, DM], BF16)
            znTt = fsb("znTt", [128, 2, 8, 256], BF16)
            hTt = fsb("hTt", [128, 2, 256], BF16)
            t_sb = fsb("t_sb", [128, 2, 2, DM], BF16)
            b2row = fsb("b2row", [1, 2, DM], BF16)
            rmb = fsb("rmb", [128, 2, 2, 4], F32)
            idxu = fsb("idxu", [128, 4], U32)
            alph = fsb("alph", [128, 4], F32)
            xg = fsb("xg", [128, 2, DM], F32)
            fq4 = fsb("fq4", [128, 2, DM], F32)
            oq4 = fsb("oq4", [128, 2, DM], F32)
            stat5 = fsb("stat5", [128, 24], F32)
            mv5 = fsb("mv5", [128, 2, 2, 2], F32)
            sj5 = fsb("sj5", [128, 2, 2], F32)
            rstd5 = fsb("rstd5", [128, 2], F32)
            oobu = fsb("oobu", [128, 1], U32)
            PT4 = [(pA, pB), (pC, pD)]

            k.new_block()
            REGS = {}

            def mk_setup(eng_name):
                def setup(e):
                    REGS[eng_name] = (e.alloc_register(name="rns" + eng_name),
                                      e.alloc_register(name="rs" + eng_name),
                                      e.alloc_register(name="rnsE" + eng_name),
                                      e.alloc_register(name="rsE" + eng_name))
                return setup
            for en_ in ("sync", "tensor", "vector", "scalar", "gpsimd"):
                k.emit(en_, mk_setup(en_))

            def p4_oob(v):
                v.tensor_copy(oobu[:, :], rmsent[:, 0:1]).then_inc(s_ln.h, 1)
            OOB_MS = s_ln.bump()
            k.emit("vector", p4_oob)

            def p4_pro(g):
                g.wait_ge(s_st.h, P2_ST_DONE)
                g.wait_ge(s_ln.h, OOB_MS)
            k.emit("gpsimd", p4_pro)

            def cond(e_obj, eng_name, ex, s):
                rns, rs = REGS[eng_name][0], REGS[eng_name][1]
                if s == 0:
                    e_obj.reg_load(rns, nse_i[0:1, ex:ex + 1])
                e_obj.reg_mov(rs, s)
                return e_obj.If_lt(rs, rns)

            def condE(e_obj, eng_name, sgp):
                exP, sP = sgp // NSC, sgp % NSC
                rns, rs = REGS[eng_name][2], REGS[eng_name][3]
                e_obj.reg_load(rns, nse_i[0:1, exP:exP + 1])
                e_obj.reg_mov(rs, sP)
                return e_obj.If_lt(rs, rns)

            B2_AT = {}
            W_AT = {}
            ZN_AT = {}
            RM_AT = {}
            IDX_MS = {}
            ZREC_MS = {}
            FQ_MS = {}
            BN_MS = {}
            SQ5_MS = {}
            OQ_MS = {}
            ALPH_MS = {}
            GX_AT = {}
            SC_AT = {}
            KRELU = 11

            def emit_epi_a(sgp):
                def f_epi_a(v, sgp=sgp):
                    with condE(v, "vector", sgp):
                        v.wait_ge(s_x.h, RM_AT[sgp])
                        if sgp >= 1:
                            v.wait_ge(s_sc.h, SC_AT[sgp - 1])
                        for gi in range(2):
                            op = v.tensor_scalar(oq4[:, gi, :], znl[:, sgp % 3, gi, :],
                                                 rmb[:, sgp % 2, gi, 3:4],
                                                 rmb[:, sgp % 2, gi, 2:3],
                                                 op0=ALU.mult, op1=ALU.add)
                        op.then_inc(s_ln.h, 1)
                    with v.Else():
                        v.sem_inc(s_ln.h, 1)
                ZREC_MS[sgp] = s_ln.bump()
                k.emit("vector", f_epi_a)

            def emit_epi_b(sgp):
                def f_epi(v, sgp=sgp):
                    with condE(v, "vector", sgp):
                        v.wait_ge(s_te.h, sgp + 1)
                        v.wait_ge(s_sq.h, ALPH_MS[sgp])
                        for gi in range(2):
                            op = v.scalar_tensor_tensor(
                                fq4[:, gi, :], t_sb[:, sgp % 2, gi, :],
                                alph[:, 2 * (sgp % 2) + gi:2 * (sgp % 2) + gi + 1],
                                oq4[:, gi, :], op0=ALU.mult, op1=ALU.add)
                        op.then_inc(s_ln.h, 1)
                        v.wait_ge(s_ln.h, FQ_MS[sgp])
                        for gi in range(2):
                            v.bn_stats(stat5[:, 12 * gi:12 * gi + 6],
                                       fq4[:, gi, 0:512])
                            op = v.bn_stats(stat5[:, 12 * gi + 6:12 * gi + 12],
                                            fq4[:, gi, 512:1024])
                        op.then_inc(s_ln.h, 1)
                        v.wait_ge(s_ln.h, STAT_E[sgp])
                        v.bn_aggr(mv5[:, sgp % 2, 0, :], stat5[:, 0:12])
                        v.bn_aggr(mv5[:, sgp % 2, 1, :],
                                  stat5[:, 12:24]).then_inc(s_ln.h, 1)
                        v.wait_ge(s_sq.h, SQ5_MS[sgp])
                        v.wait_ge(s_ln.h, BN_MS[sgp])
                        v.reciprocal(rstd5[:, :],
                                     sj5[:, sgp % 2, :]).then_inc(s_ln.h, 1)
                        v.wait_ge(s_ln.h, RC_E[sgp])
                        v.wait_ge(s_sc.h, GX_AT[sgp])
                        for gi in range(2):
                            v.tensor_scalar(oq4[:, gi, :], fq4[:, gi, :],
                                            mv5[:, sgp % 2, gi, 0:1],
                                            rstd5[:, gi:gi + 1],
                                            op0=ALU.subtract, op1=ALU.mult)
                            op = v.tensor_tensor(oq4[:, gi, :], oq4[:, gi, :],
                                                 xg[:, gi, :], op=ALU.add)
                        op.then_inc(s_ln.h, 1)
                    with v.Else():
                        v.sem_inc(s_ln.h, 5)
                FQ_MS[sgp] = s_ln.bump()
                STAT_E[sgp] = s_ln.bump()
                BN_MS[sgp] = s_ln.bump()
                RC_E[sgp] = s_ln.bump()
                OQ_MS[sgp] = s_ln.bump()
                k.emit("vector", f_epi)

            def emit_sq5(sgp):
                def f_sq5(sc, sgp=sgp):
                    with condE(sc, "scalar", sgp):
                        sc.wait_ge(s_ln.h, BN_MS[sgp])
                        sc.activation(sj5[:, sgp % 2, :], mv5[:, sgp % 2, :, 1:2],
                                      AF.Sqrt, bias=epsc[:, 0:1],
                                      scale=1.0).then_inc(s_sq.h, 1)
                    with sc.Else():
                        sc.sem_inc(s_sq.h, 1)
                SQ5_MS[sgp] = s_sq.bump()
                k.emit("scalar", f_sq5)

            def emit_scat(sgp):
                def f_scat(g, sgp=sgp):
                    with condE(g, "gpsimd", sgp):
                        g.wait_ge(s_ln.h, OQ_MS[sgp])
                        for gi in range(2):
                            g.indirect_dma_start(
                                out=out_d, out_offset=IndirectOffsetOnAxis(
                                    idxu[:, 2 * (sgp % 2) + gi:
                                         2 * (sgp % 2) + gi + 1], 0),
                                in_=oq4[:, gi, :], in_offset=None,
                                bounds_check=_breg["rows"],
                                oob_is_err=False).then_inc(s_sc.h, 16)
                    with g.Else():
                        for gi in range(2):
                            g.indirect_dma_start(
                                out=out_d, out_offset=IndirectOffsetOnAxis(
                                    oobu[:, 0:1], 0),
                                in_=oq4[:, gi, :], in_offset=None,
                                bounds_check=_breg["rows"],
                                oob_is_err=False).then_inc(s_sc.h, 16)
                SC_AT[sgp] = s_sc.bump(32)
                k.emit("gpsimd", f_scat)

            STAT_E = {}
            RC_E = {}

            for ex in range(E):
                def f_b2(g, ex=ex):
                    if ex >= 2:
                        g.wait_ge(s_w2.h, NFC * NSC * (ex - 1))
                    g.dma_start(b2row[:, ex % 2, :],
                                b2_d[ex:ex + 1, :]).then_inc(s_gq.h, 16)
                B2_AT[ex] = s_gq.bump(16)
                k.emit("gpsimd", f_b2)

                def f_wload(sync, ex=ex):
                    if ex >= 1:
                        sync.wait_ge(s_w1.h, NFC * NSC * ex)
                        sync.wait_ge(s_w2.h, NFC * NSC * ex)
                    if ex >= 1:
                        sync.wait_ge(s_ln.h, ZREC_MS[NSC * ex - 1])
                    for s in range(2):
                        sg = NSC * ex + s
                        if sg >= 2:
                            sync.wait_ge(s_trg.h, 2 * (sg - 1))
                        for gi in range(2):
                            sync.dma_start(
                                znl[:, sg % 3, gi, :],
                                znbkt[ex * CAP + s * 256 + gi * 128:
                                      ex * CAP + s * 256 + gi * 128 + 128, :]
                            ).then_inc(s_zn.h, 16)
                    for d in range(8):
                        sync.dma_start(W1T[:, d, :], w1b[ex, :, d * 128:(d + 1) * 128],
                                       transpose=True).then_inc(s_w.h, 16)
                    for f in range(NFC):
                        sync.dma_start(W2T[:, f, :], w2b[ex, :, f * 128:(f + 1) * 128],
                                       transpose=True).then_inc(s_w.h, 16)
                s_w.bump(16 * 40)
                ZN_AT[NSC * ex] = s_zn.bump(32)
                ZN_AT[NSC * ex + 1] = s_zn.bump(32)
                W_AT[ex] = s_w.v
                W_BASE = W_AT[ex] - 16 * 40
                if ex == 0:
                    W_BASES = {}
                W_BASES[ex] = W_BASE
                k.emit("sync", f_wload)

                def f_tpro(t, ex=ex):
                    t.wait_ge(s_w.h, W_BASES[ex] + 16 * 8)
                    for gi in range(2):
                        sg = NSC * ex
                        grp = 2 * sg + gi
                        pT = pTrb if grp % 2 == 0 else pTr2
                        t.wait_ge(s_zn.h, ZN_AT[sg])
                        if grp >= 2:
                            t.wait_ge(s_znT.h, grp - 1)
                        for d in range(8):
                            mm = t.transpose(pT[:, d, :],
                                             znl[:, sg % 3, gi, d * 128:(d + 1) * 128],
                                             id128[:, :])
                        mm.then_inc(s_trg.h, 1)
                s_trg.bump(2)
                k.emit("tensor", f_tpro)

                for s in range(NSC):
                    sg = NSC * ex + s

                    has_pre = s + 2 < NSC

                    def f_sync(sync, ex=ex, s=s, sg=sg, has_pre=has_pre):
                        with cond(sync, "sync", ex, s):
                            if sg >= 2:
                                sync.wait_ge(s_ln.h, ZREC_MS[sg - 2])
                            for gi in range(2):
                                sync.dma_start(
                                    rmb[:, sg % 2, gi, :],
                                    rmtab[ex * CAP + s * 256 + gi * 128:
                                          ex * CAP + s * 256 + gi * 128 + 128, :]
                                ).then_inc(s_x.h, 16)
                            if has_pre:
                                if sg >= 1:
                                    sync.wait_ge(s_trg.h, 2 * sg)
                                    sync.wait_ge(s_ln.h, ZREC_MS[sg - 1])
                                for gi in range(2):
                                    sync.dma_start(
                                        znl[:, (sg + 2) % 3, gi, :],
                                        znbkt[ex * CAP + (s + 2) * 256 + gi * 128:
                                              ex * CAP + (s + 2) * 256 + gi * 128 + 128, :]
                                    ).then_inc(s_zn.h, 16)
                        with sync.Else():
                            sync.sem_inc(s_x.h, 32)
                            if has_pre:
                                sync.sem_inc(s_zn.h, 32)
                    RM_AT[sg] = s_x.bump(32)
                    if has_pre:
                        ZN_AT[sg + 2] = s_zn.bump(32)
                    k.emit("sync", f_sync)

                    has_tr = s + 1 < NSC

                    def f_tensor(t, ex=ex, s=s, sg=sg, has_tr=has_tr):
                        def w2_group(kk):
                            ff = kk % NFC
                            t.wait_ge(s_he.h, kk + 1)
                            if s == 0:
                                t.wait_ge(s_w.h, W_BASES[ex] + 16 * (8 + ff + 1))
                            if ff == 0:
                                if sg >= 1:
                                    t.wait_ge(s_te.h, sg)
                                t.wait_ge(s_gq.h, B2_AT[ex])
                                for sg2 in range(2):
                                    for dh in range(2):
                                        t.matmul(PT4[sg2][dh][:, :], ones1r[:, :],
                                                 b2row[0:1, ex % 2, dh * 512:(dh + 1) * 512],
                                                 start=True, stop=False)
                            for sg2 in range(2):
                                for dh in range(2):
                                    mm = t.matmul(PT4[sg2][dh][:, :],
                                                  hTt[:, kk % 2, sg2 * 128:(sg2 + 1) * 128],
                                                  W2T[:, ff, dh * 512:(dh + 1) * 512],
                                                  start=False, stop=(ff == NFC - 1))
                            mm.then_inc(s_w2.h, 1)

                        def tr_group(sgp, gi):
                            grp = 2 * sgp + gi
                            pT = pTrb if grp % 2 == 0 else pTr2
                            t.wait_ge(s_zn.h, ZN_AT[sgp])
                            t.wait_ge(s_znT.h, grp - 1)
                            for d in range(8):
                                mm = t.transpose(pT[:, d, :],
                                                 znl[:, sgp % 3, gi, d * 128:(d + 1) * 128],
                                                 id128[:, :])
                            mm.then_inc(s_trg.h, 1)

                        with cond(t, "tensor", ex, s):
                            t.wait_ge(s_znT.h, 2 * (sg + 1))
                            for f in range(NFC):
                                kk = NFC * sg + f
                                pw = pE_ if kk % 2 == 0 else pF_
                                if kk >= 2:
                                    t.wait_ge(s_he.h, kk - 1)
                                for d in range(8):
                                    mm = t.matmul(pw[:, 0:256],
                                                  W1T[:, d, f * 128:(f + 1) * 128],
                                                  znTt[:, sg % 2, d, :],
                                                  start=(d == 0), stop=(d == 7))
                                mm.then_inc(s_w1.h, 1)
                                if has_tr and f == 26:
                                    tr_group(sg + 1, 0)
                                if has_tr and f == 28:
                                    tr_group(sg + 1, 1)
                                if f >= 1:
                                    w2_group(kk - 1)
                            w2_group(NFC * sg + NFC - 1)
                        with t.Else():
                            t.sem_inc(s_w1.h, NFC)
                            t.sem_inc(s_w2.h, NFC)
                            if has_tr:
                                t.sem_inc(s_trg.h, 2)
                    s_w1.bump(NFC)
                    s_w2.bump(NFC)
                    if has_tr:
                        s_trg.bump(2)
                    k.emit("tensor", f_tensor)

                    def f_vector(v, ex=ex, s=s, sg=sg):
                        with cond(v, "vector", ex, s):
                            if sg >= 2:
                                v.wait_ge(s_w1.h, NFC * (sg - 1))
                            for gi in range(2):
                                grp = 2 * sg + gi
                                pT = pTrb if grp % 2 == 0 else pTr2
                                v.wait_ge(s_trg.h, grp + 1)
                                for d in range(8):
                                    op = v.tensor_scalar(
                                        znTt[:, sg % 2, d, gi * 128:(gi + 1) * 128],
                                        pT[:, d, :], GT[:, d, ex:ex + 1],
                                        BT[:, d, ex:ex + 1],
                                        op0=ALU.mult, op1=ALU.add)
                                op.then_inc(s_znT.h, 1)
                            v.wait_ge(s_x.h, RM_AT[sg])
                            if sg >= 2:
                                v.wait_ge(s_sc.h, SC_AT[sg - 2])
                            v.tensor_copy(idxu[:, 2 * (sg % 2):2 * (sg % 2) + 1],
                                          rmb[:, sg % 2, 0, 0:1])
                            v.tensor_copy(idxu[:, 2 * (sg % 2) + 1:2 * (sg % 2) + 2],
                                          rmb[:, sg % 2, 1, 0:1]).then_inc(s_ln.h, 1)
                        with v.Else():
                            v.sem_inc(s_znT.h, 2)
                            v.sem_inc(s_ln.h, 1)
                    if sg >= 1:
                        emit_epi_a(sg - 1)
                    s_znT.bump(2)
                    IDX_MS[sg] = s_ln.bump()
                    k.emit("vector", f_vector)

                    if sg >= 1:
                        emit_epi_b(sg - 1)

                    def f_scalar_a(sc, ex=ex, s=s, sg=sg):
                        with cond(sc, "scalar", ex, s):
                            sc.wait_ge(s_x.h, RM_AT[sg])
                            sc.activation(alph[:, 2 * (sg % 2):2 * (sg % 2) + 2],
                                          rmb[:, sg % 2, :, 1:2],
                                          AF.Sigmoid).then_inc(s_sq.h, 1)
                            for f in range(KRELU):
                                kk = NFC * sg + f
                                pw = pE_ if kk % 2 == 0 else pF_
                                sc.wait_ge(s_w1.h, kk + 1)
                                if kk >= 2:
                                    sc.wait_ge(s_w2.h, kk - 1)
                                sc.activation(hTt[:, kk % 2, :], pw[:, 0:256], AF.Relu,
                                              bias=B1T[:, f, ex:ex + 1],
                                              scale=1.0).then_inc(s_he.h, 1)
                        with sc.Else():
                            sc.sem_inc(s_sq.h, 1)
                            sc.sem_inc(s_he.h, KRELU)
                    ALPH_MS[sg] = s_sq.bump()
                    s_he.bump(KRELU)
                    k.emit("scalar", f_scalar_a)

                    if sg >= 1:
                        emit_sq5(sg - 1)

                    def f_scalar_b(sc, ex=ex, s=s, sg=sg):
                        with cond(sc, "scalar", ex, s):
                            for f in range(KRELU, NFC):
                                kk = NFC * sg + f
                                pw = pE_ if kk % 2 == 0 else pF_
                                sc.wait_ge(s_w1.h, kk + 1)
                                sc.wait_ge(s_w2.h, kk - 1)
                                sc.activation(hTt[:, kk % 2, :], pw[:, 0:256], AF.Relu,
                                              bias=B1T[:, f, ex:ex + 1],
                                              scale=1.0).then_inc(s_he.h, 1)
                            sc.wait_ge(s_w2.h, NFC * (sg + 1))
                            if sg >= 2:
                                sc.wait_ge(s_ln.h, FQ_MS[sg - 2])
                            for sg2 in range(2):
                                for dh in range(2):
                                    op = sc.copy(t_sb[:, sg % 2, sg2, dh * 512:(dh + 1) * 512],
                                                 PT4[sg2][dh][:, :])
                            op.then_inc(s_te.h, 1)
                        with sc.Else():
                            sc.sem_inc(s_he.h, NFC - KRELU)
                            sc.sem_inc(s_te.h, 1)
                    s_he.bump(NFC - KRELU)
                    s_te.bump(1)
                    k.emit("scalar", f_scalar_b)

                    def f_gp(g, ex=ex, s=s, sg=sg):
                        with cond(g, "gpsimd", ex, s):
                            g.wait_ge(s_ln.h, IDX_MS[sg])
                            if sg >= 1:
                                g.wait_ge(s_ln.h, OQ_MS[sg - 1])
                            for gi in range(2):
                                g.indirect_dma_start(
                                    out=xg[:, gi, :], out_offset=None, in_=x1s,
                                    in_offset=IndirectOffsetOnAxis(
                                        idxu[:, 2 * (sg % 2) + gi:
                                             2 * (sg % 2) + gi + 1], 0),
                                    bounds_check=_breg["rows"],
                                    oob_is_err=False).then_inc(s_sc.h, 16)
                        with g.Else():
                            if sg >= 1:
                                g.wait_ge(s_ln.h, OQ_MS[sg - 1])
                            for gi in range(2):
                                g.indirect_dma_start(
                                    out=xg[:, gi, :], out_offset=None, in_=x1s,
                                    in_offset=IndirectOffsetOnAxis(oobu[:, 0:1], 0),
                                    bounds_check=_breg["rows"],
                                    oob_is_err=False).then_inc(s_sc.h, 16)
                    GX_AT[sg] = s_sc.bump(32)
                    k.emit("gpsimd", f_gp)

                    if sg >= 1:
                        emit_scat(sg - 1)

            SGL = NSC * E - 1
            emit_epi_a(SGL)
            emit_epi_b(SGL)
            emit_sq5(SGL)
            emit_scat(SGL)
            if debug_stage == "epi":
                fin_sc2 = s_sc.v
                def dmp_epi(sync, fin_sc2=fin_sc2):
                    sync.wait_ge(s_sc.h, fin_sc2)
                    sync.dma_start(d_oq, oq4[:, :, :]).then_inc(s_x.h, 16)
                    sync.dma_start(d_fq, fq4[:, :, :]).then_inc(s_x.h, 16)
                    sync.dma_start(d_xg, xg[:, :, :]).then_inc(s_x.h, 16)
                    sync.dma_start(d_ts, t_sb[:, :, :, :]).then_inc(s_x.h, 16)
                    sync.dma_start(d_rm2, rmb[:, :, :, :]).then_inc(s_x.h, 16)
                    sync.dma_start(d_al2, alph[:, :]).then_inc(s_x.h, 16)
                    sync.dma_start(d_ix, idxu[:, :]).then_inc(s_x.h, 16)
                    sync.dma_start(d_mv, mv5[:, :, :, :]).then_inc(s_x.h, 16)
                    sync.dma_start(d_zl, znl[:, :, :, :]).then_inc(s_x.h, 16)
                DMP_AT = s_x.bump(16 * 9)
                def dmp_fin(sync, DMP_AT=DMP_AT):
                    sync.wait_ge(s_x.h, DMP_AT)
                k.emit("sync", dmp_epi)
                k.emit("sync", dmp_fin)
            k.run_block()

        k.new_block()
        fin_sc = s_sc.v
        fin_st = s_st.v
        def q_fin(sync):
            sync.wait_ge(s_sc.h, fin_sc)
            sync.wait_ge(s_st.h, fin_st)
        k.emit("sync", q_fin)
        k.run_block()

        if debug_stage == "dump":
            with nc.sbuf_tensor("dmpbuf", [128, 2, DM], BF16) as dmpbuf:
                k.new_block()
                base = s_x.v

                def dmp(sync, base=base):
                    n = 0
                    for cc2 in range(E * CAP // 128):
                        if n >= 2:
                            sync.wait_ge(s_x.h, base + 16 * (2 * (n - 1)))
                        sync.dma_start(dmpbuf[:, n % 2, :],
                                       znbkt[128 * cc2:128 * (cc2 + 1), :]
                                       ).then_inc(s_x.h, 16)
                        sync.wait_ge(s_x.h, base + 32 * n + 16)
                        sync.dma_start(d_zn[128 * cc2:128 * (cc2 + 1), :],
                                       dmpbuf[:, n % 2, :]).then_inc(s_x.h, 16)
                        n += 1
                    sync.dma_start(d_o[:, 0, :], o_all[:, :]).then_inc(s_x.h, 16)
                    sync.dma_start(d_o[:, 1, :], o_all[:, :]).then_inc(s_x.h, 16)
                    sync.dma_start(d_rk[:, :, :], dbg_rk[:, :, :]).then_inc(s_x.h, 16)
                    sync.wait_ge(s_x.h, base + 32 * n + 48)
                s_x.bump(32 * (E * CAP // 128) + 48)
                k.emit("sync", dmp)
                k.run_block()

    return nc


def kernel(**inputs):
    x = np.ascontiguousarray(inputs["L0_attn_x"], np.float32)
    nc = build_program()
    in_maps = []
    for c in range(8):
        shard = {
            "x": np.ascontiguousarray(x[4 * c:4 * (c + 1)].reshape(ROWS, DM)),
            "wqkv": np.ascontiguousarray(inputs["L0_attn_w_qkv"], np.float32),
            "wproj": np.ascontiguousarray(inputs["L0_attn_w_proj"], np.float32),
            "cent": np.ascontiguousarray(inputs["L0_ffn_centroids"], np.float32),
            "lng": np.ascontiguousarray(inputs["L0_ffn_ln_g"], np.float32),
            "lnb": np.ascontiguousarray(inputs["L0_ffn_ln_b"], np.float32),
            "w1": np.ascontiguousarray(inputs["L0_ffn_w1"], np.float32),
            "b1": np.ascontiguousarray(inputs["L0_ffn_b1"], np.float32),
            "w2": np.ascontiguousarray(inputs["L0_ffn_w2"], np.float32),
            "b2": np.ascontiguousarray(inputs["L0_ffn_b2"], np.float32),
        }
        in_maps.append(shard)
    res = run_bass_kernel_spmd(nc, in_maps, core_ids=list(range(8)))
    out = np.concatenate([r["out"].reshape(4, N, DM) for r in res.results], axis=0)
    return out
